# version 30
# baseline (speedup 1.0000x reference)
import contextlib
import math

import numpy as np
import concourse.bass as bass
import concourse.mybir as mybir
from concourse.bass_utils import run_bass_kernel_spmd

F32 = mybir.dt.float32
BF16 = mybir.dt.bfloat16
I32 = mybir.dt.int32
AF = mybir.ActivationFunctionType
ALU = mybir.AluOpType
AX = mybir.AxisListType

NCORES = 8
T = 1024
NT = 8
E = 4096
EC = 32
ISD = float(1.0 / math.sqrt(128.0))


class Sem:
    def __init__(self, h):
        self.h = h
        self.v = 0

    def inc(self, n=1):
        self.v += n
        return self.v


def build_program():
    nc = bass.Bass("TRN2", target_bir_lowering=False)

    xs = nc.dram_tensor("xs", [T, E], F32, kind="ExternalInput").ap()
    wkm = nc.dram_tensor("wkm", [512, 128, 8], F32, kind="ExternalInput").ap()
    wvm = nc.dram_tensor("wvm", [512, 128, 8], F32, kind="ExternalInput").ap()
    wqm = nc.dram_tensor("wqm", [512, 128, 32], F32, kind="ExternalInput").ap()
    wfm = nc.dram_tensor("wfm", [512, E], F32, kind="ExternalInput").ap()
    bq = nc.dram_tensor("bq", [128, 32], F32, kind="ExternalInput").ap()
    bk = nc.dram_tensor("bk", [128, 8], F32, kind="ExternalInput").ap()
    bv = nc.dram_tensor("bv", [128, 8], F32, kind="ExternalInput").ap()
    fb = nc.dram_tensor("fb", [1, E], F32, kind="ExternalInput").ap()
    la = nc.dram_tensor("la", [3, 16, E], F32, kind="ExternalInput").ap()
    lbw = nc.dram_tensor("lbw", [3, E, 16], F32, kind="ExternalInput").ap()
    lsi = nc.dram_tensor("lsi", [1, 4], I32, kind="ExternalInput").ap()
    bidx = nc.dram_tensor("bidx", [1, 1], I32, kind="ExternalInput").ap()
    out = nc.dram_tensor("out", [T, E], F32, kind="ExternalOutput").ap()

    wkstg = nc.dram_tensor("wkstg", [1024, 512], BF16, kind="Internal").ap()
    wvstg = nc.dram_tensor("wvstg", [1024, 512], BF16, kind="Internal").ap()
    wqstg = nc.dram_tensor("wqstg", [1024, 512], BF16, kind="Internal").ap()
    wfstg = nc.dram_tensor("wfstg", [512, E], BF16, kind="Internal").ap()
    wk_scr = nc.dram_tensor("wk_scr", [8192, 512], BF16, kind="Internal",
                            addr_space="Shared").ap()
    wv_scr = nc.dram_tensor("wv_scr", [8192, 512], BF16, kind="Internal",
                            addr_space="Shared").ap()
    wq_scr = nc.dram_tensor("wq_scr", [8192, 512], BF16, kind="Internal",
                            addr_space="Shared").ap()
    wfT_scr = nc.dram_tensor("wfT_scr", [E, E], BF16, kind="Internal",
                             addr_space="Shared").ap()
    qtok_scr = nc.dram_tensor("qtok_scr", [NT, 128, 9, 128], BF16, kind="Internal").ap()
    ktok_scr = nc.dram_tensor("ktok_scr", [NT, 128, 8, 128], BF16, kind="Internal").ap()
    attn_scr = nc.dram_tensor("attn_scr", [NT, 128, 128, 32], F32, kind="Internal").ap()
    brow_scr = nc.dram_tensor("brow_scr", [3, 8, 128], BF16, kind="Internal").ap()
    beffT_scr = nc.dram_tensor("beffT_scr", [17, 8, 512], BF16, kind="Internal").ap()

    es = contextlib.ExitStack()
    sb = lambda n, s, d: es.enter_context(nc.sbuf_tensor(n, s, d))
    pst = lambda n, s, d: es.enter_context(nc.psum_tensor(n, s, d))
    mks = lambda n: Sem(es.enter_context(nc.semaphore(n)))

    ptpA = pst("ptpA", [128, 4, 128], BF16)
    ptpB = pst("ptpB", [128, 4, 128], BF16)
    pmAB = pst("pmAB", [128, 1024], F32)
    psAB = pst("psAB", [128, 1024], F32)
    poAB = pst("poAB", [128, 1024], F32)
    pmA = pmAB[:, 0:512]
    pmB = pmAB[:, 512:1024]
    psA = psAB[:, 0:512].rearrange("p (a b) -> p a b", b=128)
    psB = psAB[:, 512:1024].rearrange("p (a b) -> p a b", b=128)
    poA = poAB[:, 0:512].rearrange("p (a b) -> p a b", b=128)
    poB = poAB[:, 512:1024].rearrange("p (a b) -> p a b", b=128)
    psAb = psAB[:, 0:512].bitcast(BF16).rearrange("p (a b) -> p a b", b=128)
    psBb = psAB[:, 512:1024].bitcast(BF16).rearrange("p (a b) -> p a b", b=128)
    poAb = poAB[:, 0:512].bitcast(BF16).rearrange("p (a b) -> p a b", b=128)
    poBb = poAB[:, 512:1024].bitcast(BF16).rearrange("p (a b) -> p a b", b=128)
    ptp4 = (ptpA[:], ptpB[:], psAb[:, 0:4, :], psBb[:, 0:4, :])
    rotS8 = (psAB[:].rearrange("p (a b) -> p a b", b=128),
             pmAB[:].rearrange("p (a b) -> p a b", b=128))
    rotO8 = (poAB[:, 0:264].rearrange("p (a b) -> p a b", b=33),
             poAB[:, 512:776].rearrange("p (a b) -> p a b", b=33))
    pmb = (pmA, pmB)
    ptp = (ptpA, ptpB)

    xnT = sb("xnT", [128, EC, T], BF16)
    xnt2 = sb("xnt2", [128, 2, E], BF16)
    ident = sb("ident", [128, 128], BF16)
    lsi_s = sb("lsi_s", [1, 4], I32)
    lsi_f = sb("lsi_f", [1, 4], F32)
    bidx_s = sb("bidx_s", [1, 1], I32)
    bidx_f = sb("bidx_f", [1, 1], F32)
    lsi_rep = sb("lsi_rep", [128, 4], F32)
    bidx_rep = sb("bidx_rep", [128, 1], F32)
    cge = sb("cge", [128, 4], F32)
    seg = sb("seg", [128, 1], F32)
    usev = sb("usev", [128, 1], F32)
    m0 = sb("m0", [128, 1], F32)
    m1 = sb("m1", [128, 1], F32)
    m2 = sb("m2", [128, 1], F32)
    ta_ = sb("ta_", [128, 1], F32)
    tb_ = sb("tb_", [128, 1], F32)
    tc_ = sb("tc_", [128, 1], F32)
    td_ = sb("td_", [128, 1], F32)
    bq_s = sb("bq_s", [128, 32], F32)
    bqsum = sb("bqsum", [128, 8], F32)
    bk_s = sb("bk_s", [128, 8], F32)
    bv_s = sb("bv_s", [128, 8], F32)
    aeffT = sb("aeffT", [128, EC, 16], BF16)
    t1T = sb("t1T", [17, T], BF16)
    qmax = sb("qmax", [128, NT, 8], F32)
    kmax = sb("kmax", [128, NT, 8], F32)
    cb8 = sb("cb8", [128, NT], F32)
    cbprod = sb("cbprod", [128, 8], F32)
    ones512 = sb("ones512", [1, 512], BF16)
    bns = sb("bns", [128, 2, 8, 6], F32)
    mv = sb("mv", [128, 2, 2], F32)
    negmu = sb("negmu", [128, 2, 1], F32)
    lnv = sb("lnv", [128, 2, 1], F32)
    rstd = sb("rstd", [128, 2, 1], F32)
    lnt = sb("lnt", [128, 2, 1], F32)

    s_small = mks("s_small"); s_lab = mks("s_lab"); s_gpd = mks("s_gpd"); s_gp = mks("s_gp")
    s_v0 = mks("s_v0"); s_t0 = mks("s_t0"); s_sc0 = mks("s_sc0")
    s_x = [mks("s_x0"), mks("s_x1")]
    s_ln = mks("s_ln"); s_rs = mks("s_rs"); s_ap = mks("s_ap")
    s_tr = mks("s_tr"); s_xe = mks("s_xe")
    s_lk = [mks("s_lk0"), mks("s_lk1")]
    s_lv = [mks("s_lv0"), mks("s_lv1")]
    s_lq = [mks("s_lq0"), mks("s_lq1")]
    s_lf = [mks("s_lf0"), mks("s_lf1")]
    s_wl = [mks("s_wl0"), mks("s_wl1")]
    s_gpm = mks("s_gpm")
    s_gpe = mks("s_gpe")
    s_lacc = mks("s_lacc")
    s_wt4 = [mks("s_wt40"), mks("s_wt41"), mks("s_wt42"), mks("s_wt43")]
    s_sk = [mks("s_sk0"), mks("s_sk1")]
    s_sv = [mks("s_sv0"), mks("s_sv1")]
    s_sq = [mks("s_sq0"), mks("s_sq1")]
    s_sf = [mks("s_sf0"), mks("s_sf1")]
    s_ck = mks("s_ck"); s_cv = mks("s_cv"); s_js = mks("s_js"); s_brw = mks("s_brw")
    s_bt = mks("s_bt"); s_bias = mks("s_bias"); s_fbc = mks("s_fbc")
    s_cf = mks("s_cf")
    s_cck = mks("s_cck"); s_ccv = mks("s_ccv")
    s_ccq = mks("s_ccq"); s_ccf = mks("s_ccf")
    s_ktr = mks("s_ktr"); s_qtr = mks("s_qtr")
    s_kte = mks("s_kte"); s_qte = mks("s_qte")
    s_ktw = [mks("s_ktw0"), mks("s_ktw1")]
    s_qtw = [mks("s_qtw0"), mks("s_qtw1")]
    s_kmx = mks("s_kmx"); s_qmx = mks("s_qmx")
    s_cb = mks("s_cb"); s_cbw = [mks("s_cbw0"), mks("s_cbw1")]
    s_t1 = mks("s_t1"); s_t1e = mks("s_t1e"); s_bto = mks("s_bto"); s_bti = mks("s_bti")
    s_stk = [mks("s_stk0"), mks("s_stk1")]; s_vi = mks("s_vi")
    s_sm = mks("s_sm"); s_ex = mks("s_ex"); s_om = mks("s_om")
    s_rc = mks("s_rc"); s_nm = mks("s_nm"); s_mg = [mks("s_mg0"), mks("s_mg1")]
    s_wc = mks("s_wc"); s_wt = mks("s_wt"); s_we = mks("s_we"); s_fm = mks("s_fm")
    s_ar = [mks("s_ar0"), mks("s_ar1")]
    s_xr = [mks("s_xr0"), mks("s_xr1")]
    s_a1 = mks("s_a1"); s_a2 = mks("s_a2")
    s_out = [mks("s_out0"), mks("s_out1")]

    wproj = xnt2[:].rearrange("p b (c d) -> p b c d", d=128)
    wprj = wproj
    xsr = xs.rearrange("(n p) e -> n p e", p=128)
    outr = out.rearrange("(n p) e -> n p e", p=128)

    block = es.enter_context(nc.Block())

    es2a = contextlib.ExitStack()
    sb2a = lambda n, s, d: es2a.enter_context(nc.sbuf_tensor(n, s, d))
    wqraw = sb2a("wqraw", [128, 2, 2048], F32)
    wkc = wqraw[:, 0, :].rearrange("p (b c d) -> p b c d", b=2, c=128, d=8)
    wvc = wqraw[:, 1, :].rearrange("p (b c d) -> p b c d", b=2, c=128, d=8)
    wk_st = sb2a("wk_st", [128, 2, 8, 128], BF16)
    wv_st = sb2a("wv_st", [128, 2, 8, 128], BF16)
    bias_rows = sb2a("bias_rows", [1, 3, 8, 128], BF16)

    es_xt = contextlib.ExitStack()
    xt = es_xt.enter_context(nc.sbuf_tensor("xt", [128, 2, E], F32))

    es0 = contextlib.ExitStack()
    sb0 = lambda n, s, d: es0.enter_context(nc.sbuf_tensor(n, s, d))
    la_a = sb0("la_a", [16, E], F32)
    iv = sb0("iv", [128, 128], I32)
    onesf = sb0("onesf", [1, 128], F32)
    bqt1 = sb0("bqt1", [128, 8], F32)
    bqt2 = sb0("bqt2", [128, 8], F32)
    ones_long = sb0("ones_long", [1, 1024], BF16)
    aeffA = sb0("aeffA", [16, E], F32)
    aeffb = sb0("aeffb", [16, E], BF16)
    lb_a = sb0("lb_a", [128, EC, 16], F32)
    beffA = sb0("beffA", [128, EC, 16], F32)
    beffb = sb0("beffb", [128, EC, 16], BF16)
    beffT = sb0("beffT", [17, 8, 512], BF16)
    bqb = sb0("bqb", [128, 3, 8], BF16)
    brow_sb = sb0("brow_sb", [8, 3, 128], BF16)

    @block.sync
    def _(sync):
        for dst, src in ((lsi_s, lsi), (bidx_s, bidx), (bq_s, bq), (bk_s, bk),
                         (bv_s, bv)):
            sync.dma_start(out=dst[:], in_=src[:]).then_inc(s_small.h, 16)
            s_small.inc(16)

    @block.gpsimd
    def _(gp):
        gp.iota(iv[:], pattern=[[1, 128]], base=0, channel_multiplier=-1).then_inc(s_gp.h, 1)
        s_gp.inc()
        gp.memset(onesf[:], 1.0).then_inc(s_gp.h, 1)
        s_gp.inc()
        gp.memset(ones_long[:], 1.0).then_inc(s_gp.h, 1)
        s_gp.inc()
        gp.memset(ones512[:], 1.0).then_inc(s_gp.h, 1)
        s_gp.inc()

    wkmr = wkm.rearrange("(cl p) d g -> cl p d g", p=128)
    wvmr = wvm.rearrange("(cl p) d g -> cl p d g", p=128)
    wqmr = wqm.rearrange("(cl p) d h -> cl p d h", p=128)
    wfmr = wfm.rearrange("(mc p) o -> mc p o", p=128)
    wkstg_v = wkstg.rearrange("(g p) (cl d) -> g p cl d", p=128, d=128)
    wvstg_v = wvstg.rearrange("(g p) (cl d) -> g p cl d", p=128, d=128)
    wqstg_v = wqstg.rearrange("(g p) (cl d) -> g p cl d", p=128, d=128)

    @block.sync
    def _(sync):
        for st in ("k", "v"):
            lvw, stg_v, st_b, s_l, s_s, s_c = {
                "k": (wkmr, wkstg_v, wk_st, s_lk, s_sk, s_ck),
                "v": (wvmr, wvstg_v, wv_st, s_lv, s_sv, s_cv),
            }[st]
            for cl in range(4):
                if cl >= 2:
                    sync.wait_ge(s_c.h, cl - 1)
                sync.dma_start(out=(wkc if st == "k" else wvc)[:, cl % 2, :, :],
                               in_=lvw[cl]).then_inc(s_l[cl % 2].h, 16)
                s_l[cl % 2].inc(16)
                if cl >= 1:
                    sync.wait_ge(s_c.h, cl)
                    sync.dma_start(out=stg_v[:, :, cl - 1, :].transpose([1, 0, 2]),
                                   in_=st_b[:, (cl - 1) % 2, :, :]
                                   ).then_inc(s_s[(cl - 1) % 2].h, 16)
                    s_s[(cl - 1) % 2].inc(16)
            sync.wait_ge(s_c.h, 4)
            sync.dma_start(out=stg_v[:, :, 3, :].transpose([1, 0, 2]),
                           in_=st_b[:, 1, :, :]).then_inc(s_s[1].h, 16)
            s_s[1].inc(16)

    @block.sync
    def _(sync):
        sync.wait_ge(s_brw.h, 16)
        sync.dma_start(out=bias_rows[:], in_=brow_scr.unsqueeze(0)
                       ).then_inc(s_bias.h, 16)
        s_bias.inc(16)

    @block.gpsimd
    def _(gp):
        for cl in range(4):
            gp.wait_ge(s_lk[cl % 2].h, 16 * (cl // 2 + 1))
            if cl >= 2:
                gp.wait_ge(s_sk[cl % 2].h, 16 * (cl // 2))
            gp.tensor_copy(wk_st[:, cl % 2, :, :],
                           wkc[:, cl % 2, :, :].transpose([0, 2, 1])).then_inc(s_ck.h, 1)
            s_ck.inc()
        gp.wait_ge(s_sk[0].h, 32)
        gp.wait_ge(s_sk[1].h, 32)
        gp.collective_compute("AllGather", ALU.bypass,
                              replica_groups=[list(range(8))],
                              ins=[wkstg], outs=[wk_scr]).then_inc(s_cck.h, 1)
        s_cck.inc()
        for cl in range(4):
            gp.wait_ge(s_lv[cl % 2].h, 16 * (cl // 2 + 1))
            if cl >= 2:
                gp.wait_ge(s_sv[cl % 2].h, 16 * (cl // 2))
            gp.tensor_copy(wv_st[:, cl % 2, :, :],
                           wvc[:, cl % 2, :, :].transpose([0, 2, 1])).then_inc(s_cv.h, 1)
            s_cv.inc()
        gp.wait_ge(s_sv[0].h, 32)
        gp.wait_ge(s_sv[1].h, 32)
        gp.collective_compute("AllGather", ALU.bypass,
                              replica_groups=[list(range(8))],
                              ins=[wvstg], outs=[wv_scr]).then_inc(s_ccv.h, 1)
        s_ccv.inc()

    @block.vector
    def _(v):
        v.wait_ge(s_gp.h, 1)
        v.tensor_scalar(ident[:], iv[:], 0.0, None, ALU.is_equal).then_inc(s_v0.h, 1)
        s_v0.inc()
        v.wait_ge(s_small.h, 80)
        v.tensor_copy(lsi_f[:], lsi_s[:]).then_inc(s_v0.h, 1)
        s_v0.inc()
        v.tensor_copy(bidx_f[:], bidx_s[:]).then_inc(s_v0.h, 1)
        s_v0.inc()

    @block.tensor
    def _(t):
        t.wait_ge(s_v0.h, 3)
        t.wait_ge(s_gp.h, 2)
        t.matmul(poA[:, 0, 0:4], onesf[:], lsi_f[:], start=True, stop=True).then_inc(s_t0.h, 1)
        s_t0.inc()
        t.matmul(poA[:, 1, 0:1], onesf[:], bidx_f[:], start=True, stop=True).then_inc(s_t0.h, 1)
        s_t0.inc()

    @block.scalar
    def _(sc):
        sc.wait_ge(s_t0.h, 2)
        sc.copy(lsi_rep[:], poA[:, 0, 0:4]).then_inc(s_sc0.h, 1)
        s_sc0.inc()
        sc.copy(bidx_rep[:], poA[:, 1, 0:1]).then_inc(s_sc0.h, 1)
        s_sc0.inc()

    @block.vector
    def _(v):
        def step(fn):
            v.wait_ge(s_v0.h, s_v0.v)
            fn().then_inc(s_v0.h, 1)
            s_v0.inc()
        v.wait_ge(s_sc0.h, 2)
        step(lambda: v.tensor_scalar(cge[:], lsi_rep[:], bidx_rep[:, 0:1], None, ALU.is_gt))
        step(lambda: v.tensor_scalar(ta_[:], cge[:, 0:1], -1.0, 1.0, ALU.mult, ALU.add))
        step(lambda: v.tensor_scalar(tb_[:], ta_[:], -2.0, None, ALU.add))
        step(lambda: v.tensor_tensor(tc_[:], tb_[:], cge[:, 1:2], ALU.mult))
        step(lambda: v.tensor_scalar(tc_[:], tc_[:], 2.0, None, ALU.add))
        step(lambda: v.tensor_scalar(td_[:], cge[:, 3:4], -1.0, 4.0, ALU.mult, ALU.add))
        step(lambda: v.tensor_tensor(ta_[:], tc_[:], td_[:], ALU.subtract))
        step(lambda: v.tensor_tensor(tb_[:], ta_[:], cge[:, 2:3], ALU.mult))
        step(lambda: v.tensor_tensor(tc_[:], tb_[:], td_[:], ALU.add))
        step(lambda: v.tensor_scalar(seg[:], tc_[:], -1.0, None, ALU.add))
        step(lambda: v.tensor_scalar(usev[:], seg[:], 0.0, None, ALU.is_gt))
        step(lambda: v.tensor_scalar(ta_[:], seg[:], 1.0, None, ALU.is_le))
        step(lambda: v.tensor_tensor(m0[:], ta_[:], usev[:], ALU.mult))
        step(lambda: v.tensor_scalar(ta_[:], seg[:], 2.0, None, ALU.is_equal))
        step(lambda: v.tensor_tensor(m1[:], ta_[:], usev[:], ALU.mult))
        step(lambda: v.tensor_scalar(ta_[:], seg[:], 3.0, None, ALU.is_ge))
        step(lambda: v.tensor_tensor(m2[:], ta_[:], usev[:], ALU.mult))
        bq4 = bq_s[:].rearrange("p (g j) -> p g j", j=4)
        step(lambda: v.tensor_tensor(bqt1[:], bq4[:, :, 0], bq4[:, :, 1], ALU.add))
        step(lambda: v.tensor_tensor(bqt2[:], bq4[:, :, 2], bq4[:, :, 3], ALU.add))
        step(lambda: v.tensor_tensor(bqsum[:], bqt1[:], bqt2[:], ALU.add))
        step(lambda: v.tensor_copy(bqb[:, 0, :], bqsum[:]))
        step(lambda: v.tensor_copy(bqb[:, 1, :], bk_s[:]))
        step(lambda: v.tensor_copy(bqb[:, 2, :], bv_s[:]))

    masks = (m0, m1, m2)

    def emit_la_load(a):
        @block.scalar
        def _(sc, a=a):
            if a > 0:
                sc.wait_ge(s_lacc.h, 2 * a)
            sc.dma_start(out=la_a[:], in_=la[a]).then_inc(s_lab.h, 16)
            s_lab.inc(16)
            sc.dma_start(out=lb_a[:],
                         in_=lbw[a].rearrange("(c p) r -> p c r", p=128)
                         ).then_inc(s_lab.h, 16)
            s_lab.inc(16)

    @block.tensor
    def _(t):
        t.wait_ge(s_v0.h, 26)
        for j in range(3):
            t.transpose(poBb[0:8, j, :], bqb[:, j, :], ident[:]).then_inc(s_bt.h, 1)
            s_bt.inc()

    def emit_brow_evac():
        @block.scalar
        def _(sc):
            sc.wait_ge(s_bt.h, 3)
            sc.copy(brow_sb[:], poBb[0:8, 0:3, :]).then_inc(s_bt.h, 1)
            s_bt.inc()
            sc.wait_ge(s_bt.h, 4)
            sc.dma_start(out=brow_scr.transpose([1, 0, 2]), in_=brow_sb[:]
                         ).then_inc(s_brw.h, 16)
            s_brw.inc(16)

    pob = (poAb, poBb)

    def emit_combine(a):
        @block.vector
        def _(v, a=a):
            v.wait_ge(s_lab.h, 32 * (a + 1))
            if a == 0:
                v.wait_ge(s_v0.h, 23)
                v.tensor_scalar(aeffA[:], la_a[:], masks[a][0:16, 0:1], None,
                                ALU.mult).then_inc(s_lacc.h, 1)
            else:
                v.wait_ge(s_lacc.h, 2 * a)
                v.scalar_tensor_tensor(aeffA[:], la_a[:], masks[a][0:16, 0:1],
                                       aeffA[:], ALU.mult, ALU.add).then_inc(s_lacc.h, 1)
            s_lacc.inc()
            v.wait_ge(s_lacc.h, s_lacc.v)
            if a == 0:
                v.tensor_scalar(beffA[:], lb_a[:], masks[a][:, 0:1], None,
                                ALU.mult).then_inc(s_lacc.h, 1)
            else:
                v.scalar_tensor_tensor(beffA[:], lb_a[:], masks[a][:, 0:1],
                                       beffA[:], ALU.mult, ALU.add).then_inc(s_lacc.h, 1)
            s_lacc.inc()

    def emit_eff_casts():
        @block.scalar
        def _(sc):
            sc.wait_ge(s_lacc.h, 6)
            sc.copy(aeffb[:], aeffA[:]).then_inc(s_lacc.h, 1)
            s_lacc.inc()
            sc.wait_ge(s_lacc.h, s_lacc.v)
            sc.copy(beffb[:], beffA[:]).then_inc(s_lacc.h, 1)
            s_lacc.inc()

    p0c = {}

    def emit_p0_transposes():
        @block.tensor
        def _(t):
            t.wait_ge(s_lacc.h, 8)
            t.wait_ge(s_bt.h, 4)
            for c in range(EC):
                grp, slot = c // 4, c % 4
                if grp == 0:
                    t.wait_ge(s_sc0.h, 2)
                if grp >= 2:
                    t.wait_ge(s_sc0.h, 2 + (grp - 1))
                t.transpose(pob[grp % 2][:, slot, 0:16], aeffb[:, 128 * c:128 * (c + 1)],
                            ident[0:16, 0:16]).then_inc(s_t0.h, 1)
                s_t0.inc()

        @block.scalar
        def _(sc):
            for grp in range(8):
                sc.wait_ge(s_t0.h, 2 + 4 * (grp + 1))
                sc.copy(aeffT[:, 4 * grp:4 * (grp + 1), :], pob[grp % 2][:, 0:4, 0:16]
                        ).then_inc(s_sc0.h, 1)
                s_sc0.inc()
        p0c["SC0_AEFF"] = s_sc0.v
        p0c["T0_AEFF"] = s_t0.v

        @block.tensor
        def _(t):
            for c in range(EC):
                grp, slot = c // 4, c % 4
                if slot == 0:
                    t.wait_ge(s_sc0.h, p0c["SC0_AEFF"] - 1 + grp)
                t.transpose(pob[grp % 2][0:16, slot, :], beffb[:, c, :], ident[:]
                            ).then_inc(s_t0.h, 1)
                s_t0.inc()
        p0c["T0_ALL"] = s_t0.v

        @block.scalar
        def _(sc):
            for grp in range(8):
                sc.wait_ge(s_t0.h, p0c["T0_AEFF"] + 4 * (grp + 1))
                sc.copy(beffT[0:16, grp, :], pob[grp % 2][0:16, 0:4, :]).then_inc(s_sc0.h, 1)
                s_sc0.inc()
        p0c["SC0_ALL"] = s_sc0.v

    def emit_xload(i):
        @block.scalar
        def _(sc, i=i):
            if i >= 2:
                sc.wait_ge(s_ap.h, i - 1)
            hi = E if i < 2 else E // 2
            sc.dma_start(out=xt[:, i % 2, 0:hi], in_=xsr[i][:, 0:hi]
                         ).then_inc(s_x[i % 2].h, 16)
            s_x[i % 2].inc(16)

        if i >= 2:
            @block.sync
            def _(sync, i=i):
                sync.wait_ge(s_ap.h, i - 1)
                sync.dma_start(out=xt[:, i % 2, E // 2:E], in_=xsr[i][:, E // 2:E]
                               ).then_inc(s_x[i % 2].h, 16)
                s_x[i % 2].inc(16)

    emit_xload(0)
    emit_xload(1)
    emit_brow_evac()

    def emit_apply(i):
        @block.vector
        def _(v, i=i):
            v.wait_ge(s_rs.h, 2 * (i + 1))
            if i >= 2:
                v.wait_ge(s_tr.h, 32 * (i - 1))
            v.tensor_scalar(xnt2[:, i % 2, :], xt[:, i % 2, :], negmu[:, i % 2, :],
                            rstd[:, i % 2, :], ALU.add, ALU.mult).then_inc(s_ap.h, 1)
            s_ap.inc()

        @block.tensor
        def _(t, i=i):
            t.wait_ge(s_ap.h, i + 1)
            if i == 0:
                t.wait_ge(s_v0.h, 1)
            for c in range(EC):
                grp = c // 4
                gidx = 8 * i + grp
                if gidx >= 2:
                    t.wait_ge(s_xe.h, gidx - 1)
                t.transpose(ptp[gidx % 2][:, c % 4, :],
                            xnt2[:, i % 2, 128 * c:128 * (c + 1)], ident[:]
                            ).then_inc(s_tr.h, 1)
                s_tr.inc()

        @block.vector
        def _(v, i=i):
            for grp in range(8):
                gidx = 8 * i + grp
                v.wait_ge(s_tr.h, 4 * (gidx + 1))
                v.tensor_copy(xnT[:, 4 * grp:4 * (grp + 1), 128 * i:128 * (i + 1)],
                              ptp[gidx % 2][:, :, :]).then_inc(s_xe.h, 1)
                s_xe.inc()

    for i in range(NT):
        if i >= 1:
            emit_apply(i - 1)

        @block.vector
        def _(v, i=i):
            v.wait_ge(s_x[i % 2].h, 16 if i < 2 else 16 + 32 * (i // 2))
            for c in range(8):
                v.bn_stats(bns[:, i % 2, c, :], xt[:, i % 2, 512 * c:512 * (c + 1)]
                           ).then_inc(s_ln.h, 1)
                s_ln.inc()
            v.wait_ge(s_ln.h, s_ln.v)
            v.wait_ge(s_rs.h, 2 * i)
            v.bn_aggr(mv[:, i % 2, :], bns[:, i % 2, :, :]).then_inc(s_ln.h, 1)
            s_ln.inc()
            v.wait_ge(s_ln.h, s_ln.v)
            v.tensor_scalar(negmu[:, i % 2, :], mv[:, i % 2, 0:1], -1.0, None,
                            ALU.mult).then_inc(s_ln.h, 1)
            s_ln.inc()
            v.wait_ge(s_ln.h, s_ln.v)
            v.tensor_scalar(lnv[:, i % 2, :], mv[:, i % 2, 1:2], 1e-5, None,
                            ALU.add).then_inc(s_ln.h, 1)
            s_ln.inc()

        @block.scalar
        def _(sc, i=i):
            sc.wait_ge(s_ln.h, 11 * i + 11)
            sc.activation(lnt[:, i % 2, :], lnv[:, i % 2, :], AF.Ln).then_inc(s_rs.h, 1)
            s_rs.inc()
            sc.wait_ge(s_rs.h, s_rs.v)
            sc.activation(rstd[:, i % 2, :], lnt[:, i % 2, :], AF.Exp, scale=-0.5
                          ).then_inc(s_rs.h, 1)
            s_rs.inc()

        if i + 2 < NT:
            emit_xload(i + 2)
        if i in (0, 2, 4):
            emit_la_load(i // 2)
        if i in (1, 3, 5):
            emit_combine(i // 2)
        if i == 5:
            emit_eff_casts()
        if i == 6:
            emit_p0_transposes()

    emit_apply(7)

    @block.gpsimd
    def _(gp):
        gp.wait_ge(s_sc0.h, p0c["SC0_ALL"])
        gp.wait_ge(s_t0.h, p0c["T0_ALL"])
        gp.dma_start(out=beffT[16:17, :, :], in_=fb.rearrange("a (b c) -> a b c", c=512)
                     ).then_inc(s_fbc.h, 16)
        s_fbc.inc(16)

    @block.scalar
    def _(sc):
        sc.wait_ge(s_fbc.h, 16)
        sc.dma_start(out=beffT_scr[:], in_=beffT[:]).then_inc(s_bto.h, 16)
        s_bto.inc(16)
        sc.wait_ge(s_gp.h, 4)
        sc.dma_start(out=t1T[16:17, :], in_=ones_long[0:1, 0:1024]).then_inc(s_gpd.h, 16)
        s_gpd.inc(16)

    es0.close()
    es_xt.close()
    es_boot = contextlib.ExitStack()
    wTboot = es_boot.enter_context(nc.sbuf_tensor("wTboot", [128, 16, 512], BF16))
    es3 = contextlib.ExitStack()
    sb3 = lambda n, s, d: es3.enter_context(nc.sbuf_tensor(n, s, d))
    vT = sb3("vT", [128, 8, T], BF16)
    q_stk = sb3("q_stk", [9, 2, 16, 128], BF16)
    k_stk = sb3("k_stk", [9, 2, 16, 128], BF16)
    v_int = sb3("v_int", [128, 2, 128, 33], BF16)
    exp_sb = sb3("exp_sb", [128, 3, 8, 128], BF16)
    attn_sb = sb3("attn_sb", [128, 2, 16, 32], F32)
    rcp8 = sb3("rcp8", [128, 3, 8], F32)

    es2 = contextlib.ExitStack()
    sb2 = lambda n, s, d: es2.enter_context(nc.sbuf_tensor(n, s, d))
    cbc = sb2("cbc", [128, 128], BF16)
    kpl = sb2("kpl", [128, 2, 512], BF16)
    qtk4 = sb2("qtk4", [128, 2, 4, 128], BF16)
    wq_st = sb2("wq_st", [128, 8, 128], BF16)
    tjA = sb2("tjA", [128, 8, 64], BF16)
    wfb = sb2("wfb", [128, 2048], BF16)

    @block.sync
    def _(sync):
        for cl in range(4):
            for hh in range(2):
                h = 2 * cl + hh
                if h == 0:
                    sync.wait_ge(s_ck.h, 4)
                    sync.wait_ge(s_cv.h, 4)
                if h >= 2:
                    sync.wait_ge(s_js.h, 3 * (h - 1))
                sync.dma_start(out=wqraw[:, h % 2, :]
                               .rearrange("p (d h4) -> p d h4", h4=32),
                               in_=wqmr[cl][:, 64 * hh:64 * (hh + 1), :]
                               ).then_inc(s_lq[h % 2].h, 16)
                s_lq[h % 2].inc(16)
            sync.wait_ge(s_js.h, 6 * (cl + 1))
            sync.dma_start(out=wqstg_v[:, :, cl, :].transpose([1, 0, 2]),
                           in_=wq_st[:, :, :]).then_inc(s_sq[cl % 2].h, 16)
            s_sq[cl % 2].inc(16)
        for pc in range(8):
            mc, hh = pc // 2, pc % 2
            if pc < 2:
                sync.wait_ge(s_js.h, 24)
            if pc >= 2:
                sync.wait_ge(s_cf.h, pc - 1)
            sync.dma_start(out=wqraw[:, pc % 2, :],
                           in_=wfmr[mc][:, 2048 * hh:2048 * (hh + 1)]
                           ).then_inc(s_lf[pc % 2].h, 16)
            s_lf[pc % 2].inc(16)
            if pc >= 1:
                w = pc - 1
                sync.wait_ge(s_cf.h, w + 1)
                sync.dma_start(out=wfstg[128 * (w // 2):128 * (w // 2 + 1),
                                         2048 * (w % 2):2048 * (w % 2 + 1)],
                               in_=wfb[:]).then_inc(s_sf[w % 2].h, 16)
                s_sf[w % 2].inc(16)
        w = 7
        sync.wait_ge(s_cf.h, 8)
        sync.dma_start(out=wfstg[128 * (w // 2):128 * (w // 2 + 1),
                                 2048 * (w % 2):2048 * (w % 2 + 1)],
                       in_=wfb[:]).then_inc(s_sf[w % 2].h, 16)
        s_sf[w % 2].inc(16)

    @block.gpsimd
    def _(gp):
        for h in range(8):
            cl, hh = h // 2, h % 2
            gp.wait_ge(s_lq[h % 2].h, 16 * (h // 2 + 1))
            if cl >= 1 and hh == 0:
                gp.wait_ge(s_sq[(cl - 1) % 2].h, 16 * ((cl - 1) // 2 + 1))
            wqh = wqraw[:, h % 2, :].rearrange("p (d g j) -> p d g j", g=8, j=4)
            a0 = wqh[:, :, :, 0].transpose([0, 2, 1])
            a1 = wqh[:, :, :, 1].transpose([0, 2, 1])
            a2 = wqh[:, :, :, 2].transpose([0, 2, 1])
            a3 = wqh[:, :, :, 3].transpose([0, 2, 1])
            outv = wq_st[:, :, 64 * hh:64 * (hh + 1)]
            if h == 0:
                gp.wait_ge(s_bto.h, 16)
                gp.wait_ge(s_gpd.h, 16)
                gp.wait_ge(s_lacc.h, 8)
                gp.wait_ge(s_brw.h, 16)
                gp.wait_ge(s_ap.h, 8)
            if h >= 1:
                gp.wait_ge(s_js.h, s_js.v)
            gp.tensor_tensor(tjA[:], a0, a1, ALU.add).then_inc(s_js.h, 1)
            s_js.inc()
            gp.tensor_tensor(outv, a2, a3, ALU.add).then_inc(s_js.h, 1)
            s_js.inc()
            gp.wait_ge(s_js.h, s_js.v)
            gp.tensor_tensor(outv, outv, tjA[:], ALU.add).then_inc(s_js.h, 1)
            s_js.inc()
        gp.wait_ge(s_sq[0].h, 32)
        gp.wait_ge(s_sq[1].h, 32)
        gp.collective_compute("AllGather", ALU.bypass,
                              replica_groups=[list(range(8))],
                              ins=[wqstg], outs=[wq_scr]).then_inc(s_ccq.h, 1)
        s_ccq.inc()
        for pc in range(8):
            gp.wait_ge(s_lf[pc % 2].h, 16 * (pc // 2 + 1))
            if pc == 0:
                gp.wait_ge(s_bto.h, 16)
                gp.wait_ge(s_gpd.h, 16)
                gp.wait_ge(s_lacc.h, 8)
                gp.wait_ge(s_brw.h, 16)
                gp.wait_ge(s_ap.h, 8)
            if pc >= 1:
                gp.wait_ge(s_sf[(pc - 1) % 2].h, 16 * ((pc - 1) // 2 + 1))
            gp.tensor_copy(wfb[:], wqraw[:, pc % 2, :]).then_inc(s_cf.h, 1)
            s_cf.inc()
        gp.wait_ge(s_sf[0].h, 64)
        gp.wait_ge(s_sf[1].h, 64)
        gp.collective_compute("AllGather", ALU.bypass,
                              replica_groups=[list(range(8))],
                              ins=[wfstg], outs=[wfT_scr]).then_inc(s_ccf.h, 1)
        s_ccf.inc()


    @block.scalar
    def _(sc):
        sc.wait_ge(s_gpd.h, 16)
        sc.wait_ge(s_bto.h, 16)
        sc.wait_ge(s_lacc.h, 8)
        sc.wait_ge(s_brw.h, 16)
        sc.wait_ge(s_ap.h, 8)
        for half in range(2):
            for hh in range(2):
                sc.dma_start(out=k_stk[8:9, half, 8 * hh:8 * (hh + 1), :]
                             .rearrange("a c d -> a (c d)"),
                             in_=t1T[16:17, 0:1024]).then_inc(s_gpd.h, 16)
                s_gpd.inc(16)

    @block.vector
    def _(v):
        v.wait_ge(s_bto.h, 16)
        v.wait_ge(s_gpd.h, 16)
        v.wait_ge(s_lacc.h, 8)
        v.wait_ge(s_brw.h, 16)
        v.memset(v_int[:, :, :, 32:33], 1.0).then_inc(s_gp.h, 1)
        s_gp.inc()

    wk_scr5 = wk_scr.rearrange("(r g p) (cl d) -> r g p cl d", g=8, p=128, d=128)
    wv_scr5 = wv_scr.rearrange("(r g p) (cl d) -> r g p cl d", g=8, p=128, d=128)
    wq_scr5 = wq_scr.rearrange("(r g p) (cl d) -> r g p cl d", g=8, p=128, d=128)

    def plane_load(j):
        pl = j // 8
        g = j % 8
        s_cc = (s_cck, s_ccv, s_ccq)[pl]
        scr5 = (wk_scr5, wv_scr5, wq_scr5)[pl]

        @block.scalar
        def _(sc, j=j, g=g, s_cc=s_cc, scr5=scr5):
            sc.wait_ge(s_cc.h, 1)
            if j < 2:
                sc.wait_ge(s_tr.h, 224 if j == 0 else 256)
            else:
                sc.wait_ge(s_gpm.h, 2 * (j - 1))
            sc.dma_start(out=wprj[:, j % 2].rearrange("p (r cl) d -> p r cl d", cl=4),
                         in_=scr5[:, g].transpose([1, 0, 2, 3])
                         ).then_inc(s_wl[j % 2].h, 16)
            s_wl[j % 2].inc(16)

    def plane_phase(kind):
        plane = {"k": 0, "v": 1, "q": 2}[kind]
        bidx_row = {"k": 1, "v": 2, "q": 0}[kind]
        s_tr_, s_te_, s_tw_, s_mx_ = {
            "k": (s_ktr, s_kte, s_ktw, s_kmx),
            "q": (s_qtr, s_qte, s_qtw, s_qmx),
            "v": (None, None, None, None),
        }[kind]
        tok_scr = {"k": ktok_scr, "q": qtok_scr, "v": None}[kind]
        mx = {"k": kmax, "q": qmax, "v": None}[kind]

        def emit_mm(m):
            g, th = m // 2, m % 2
            mg = 16 * plane + m
            j = 8 * plane + g
            bank = pmb[m % 2]

            @block.tensor
            def _(t, g=g, th=th, m=m, mg=mg, j=j, bank=bank):
                t.wait_ge(s_wl[j % 2].h, 16 * (j // 2 + 1))

                if kind == "k":
                    t.wait_ge(s_xe.h, 32 if th == 0 else 64)
                if mg >= 2:
                    t.wait_ge(s_gpe.h, mg - 1)
                if mg == 0:
                    t.wait_ge(s_bias.h, 16)
                    t.wait_ge(s_gp.h, 4)
                for c in range(EC):
                    t.matmul(bank[:], wprj[:, j % 2, c, :],
                             xnT[:, c, 512 * th:512 * (th + 1)],
                             start=(c == 0), stop=False)
                t.matmul(bank[:], bias_rows[0:1, bidx_row, g, :],
                         ones512[:], start=False, stop=True).then_inc(s_gpm.h, 1)
                s_gpm.inc()

        def emit_evac(m):
            g, th = m // 2, m % 2
            mg = 16 * plane + m
            bank = pmb[m % 2]

            @block.vector
            def _(v, g=g, th=th, m=m, mg=mg, bank=bank):
                v.wait_ge(s_gpm.h, mg + 1)
                if mg == 0:
                    v.wait_ge(s_bto.h, 16)
                    v.wait_ge(s_gpd.h, 16)
                    v.wait_ge(s_lacc.h, 8)
                    v.wait_ge(s_brw.h, 16)
                if kind == "v":
                    v.tensor_copy(vT[:, g, 512 * th:512 * (th + 1)], bank[:]
                                  ).then_inc(s_gpe.h, 1)
                    s_gpe.inc()
                else:
                    if m >= 2:
                        v.wait_ge(s_tr_.h, 4 * (m - 1))
                    v.tensor_copy(kpl[:, m % 2, :], bank[:]).then_inc(s_gpe.h, 1)
                    s_gpe.inc()

        def emit_tr(m):
            g, th = m // 2, m % 2
            mg = 16 * plane + m

            @block.tensor
            def _(t, m=m, mg=mg):
                t.wait_ge(s_gpe.h, mg + 1)
                for j in range(4):
                    if m >= 4 and j == 0:
                        t.wait_ge(s_te_.h, m - 3)
                    t.transpose(ptp4[m % 4][:, j, :],
                                kpl[:, m % 2, 128 * j:128 * (j + 1)], ident[:]
                                ).then_inc(s_tr_.h, 1)
                    s_tr_.inc()

            @block.vector
            def _(v, g=g, th=th, m=m):
                v.wait_ge(s_tr_.h, 4 * (m + 1))
                if m == 0 and kind == "q":
                    v.wait_ge(s_ktw[0].h, 128)
                    v.wait_ge(s_ktw[1].h, 128)
                    v.wait_ge(s_kmx.h, 64)
                if m >= 2:
                    v.wait_ge(s_tw_[m % 2].h, 16 * (m // 2))
                    v.wait_ge(s_mx_.h, 4 * (m - 1))
                v.tensor_copy(qtk4[:, m % 2, :, :], ptp4[m % 4][:, :, :]
                              ).then_inc(s_te_.h, 1)
                s_te_.inc()

            @block.scalar
            def _(sc, g=g, th=th, m=m):
                sc.wait_ge(s_te_.h, m + 1)
                sc.dma_start(
                    out=tok_scr[4 * th:4 * th + 4, :, g, :].transpose([1, 0, 2]),
                    in_=qtk4[:, m % 2, :, :]).then_inc(s_tw_[m % 2].h, 16)
                s_tw_[m % 2].inc(16)

            @block.vector
            def _(v, g=g, th=th, m=m):
                v.wait_ge(s_te_.h, m + 1)
                for j in range(4):
                    v.reduce_max(mx[:, 4 * th + j, g:g + 1],
                                 qtk4[:, m % 2, j, :], axis=AX.X,
                                 apply_absolute_value=True).then_inc(s_mx_.h, 1)
                    s_mx_.inc()

        for m in range(16):
            emit_mm(m)
            emit_evac(m)
            if kind != "v" and m >= 1:
                emit_tr(m - 1)
            if m % 2 == 1:
                nj = 8 * plane + m // 2 + 2
                if nj < 24:
                    plane_load(nj)
        if kind != "v":
            emit_tr(15)

    plane_load(0)
    plane_load(1)
    plane_phase("k")
    plane_phase("v")

    plane_phase("q")

    for i in range(NT):
        @block.vector
        def _(v, i=i):
            if i == 0:
                v.wait_ge(s_qmx.h, 64)
                v.wait_ge(s_kmx.h, 64)
            v.wait_ge(s_cb.h, s_cb.v)
            v.tensor_tensor(cbprod[:], qmax[:, i, :], kmax[:, i, :], ALU.mult
                            ).then_inc(s_cb.h, 1)
            s_cb.inc()
            v.wait_ge(s_cb.h, s_cb.v)
            v.reduce_sum(cb8[:, i:i + 1], cbprod[:], axis=AX.X, negate=True
                         ).then_inc(s_cb.h, 1)
            s_cb.inc()
            v.wait_ge(s_cb.h, s_cb.v)
            if i >= 1:
                v.wait_ge(s_cbw[(i - 1) % 2].h, 16 * ((i - 1) // 2 + 1))
            v.tensor_copy(cbc[:], cb8[:, i:i + 1].broadcast_to([128, 128])
                          ).then_inc(s_cb.h, 1)
            s_cb.inc()

        @block.scalar
        def _(sc, i=i):
            sc.wait_ge(s_cb.h, 3 * (i + 1))
            sc.dma_start(out=qtok_scr[i, :, 8, :], in_=cbc[:]).then_inc(s_cbw[i % 2].h, 16)
            s_cbw[i % 2].inc(16)

    for th in range(2):
        @block.tensor
        def _(t, th=th):
            bank = pmb[th % 2]
            t.wait_ge(s_gpe.h, 48)
            t.wait_ge(s_sc0.h, p0c["SC0_ALL"])
            for c in range(EC):
                mm = t.matmul(bank[0:16, :], aeffT[:, c, :],
                              xnT[:, c, 512 * th:512 * (th + 1)],
                              start=(c == 0), stop=(c == EC - 1))
                if c == EC - 1:
                    mm.then_inc(s_t1.h, 1)
            s_t1.inc()

        @block.scalar
        def _(sc, th=th):
            sc.wait_ge(s_t1.h, th + 1)
            sc.copy(t1T[0:16, 512 * th:512 * (th + 1)], pmb[th % 2][0:16, :]
                    ).then_inc(s_t1e.h, 1)
            s_t1e.inc()

    def emit_stack(sg):
        tt = sg // 8
        r0 = 16 * (sg % 8)

        @block.sync
        def _(sync, sg=sg, tt=tt, r0=r0):
            if sg == 0:
                sync.wait_ge(s_qtw[0].h, 128)
                sync.wait_ge(s_qtw[1].h, 128)
                sync.wait_ge(s_ktw[0].h, 128)
                sync.wait_ge(s_ktw[1].h, 128)
                sync.wait_ge(s_cbw[0].h, s_cbw[0].v)
                sync.wait_ge(s_cbw[1].h, s_cbw[1].v)
                sync.wait_ge(s_gpd.h, 80)
            if sg >= 2:
                sync.wait_ge(s_sm.h, 16 * (sg - 1))
            sync.dma_start(out=q_stk[:, sg % 2, :, :],
                           in_=qtok_scr[tt, r0:r0 + 16, :, :].transpose([1, 0, 2])
                           ).then_inc(s_stk[sg % 2].h, 16)
            s_stk[sg % 2].inc(16)
            sync.dma_start(out=k_stk[0:8, sg % 2, :, :],
                           in_=ktok_scr[tt, r0:r0 + 16, :, :].transpose([1, 0, 2])
                           ).then_inc(s_stk[sg % 2].h, 16)
            s_stk[sg % 2].inc(16)

    def emit_merge(sg):
        tt = sg // 8
        r0 = 16 * (sg % 8)

        @block.gpsimd
        def _(gp, sg=sg, tt=tt, r0=r0):
            gp.wait_ge(s_nm.h, 2 * (sg + 1))
            for hh in range(2):
                gp.dma_start(
                    out=attn_scr[tt, r0 + 8 * hh:r0 + 8 * (hh + 1), :, :]
                    .transpose([1, 0, 2]),
                    in_=attn_sb[:, sg % 2, 8 * hh:8 * (hh + 1), :]
                    ).then_inc(s_mg[sg % 2].h, 16)
                s_mg[sg % 2].inc(16)

    @block.scalar
    def _(sc):
        sc.wait_ge(s_ccf.h, 1)
        sc.wait_ge(s_cf.h, 8)
        for s in (s_sk[0], s_sk[1], s_sv[0], s_sv[1]):
            sc.wait_ge(s.h, 32)
        sc.wait_ge(s_ap.h, 8)
        sc.wait_ge(s_bto.h, 16)
        sc.wait_ge(s_lacc.h, 8)
        sc.wait_ge(s_gpd.h, 16)
        sc.wait_ge(s_brw.h, 16)
        sc.dma_start(out=wTboot[:],
                     in_=wfT_scr.rearrange("(c p) o -> p c o", p=128)[:, 0:16, 0:512]
                     ).then_inc(s_brw.h, 16)
        s_brw.inc(16)

    for tt in range(NT):
        @block.scalar
        def _(sc, tt=tt):
            for lg in range(16):
                B = 16 * tt + lg
                sc.wait_ge(s_sm.h, 8 * (B + 1))
                if B >= 3:
                    sc.wait_ge(s_om.h, 8 * (B - 2))
                sc.activation(exp_sb[:, B % 3, :, :], rotS8[B % 2], AF.Exp, scale=ISD
                              ).then_inc(s_ex.h, 1)
                s_ex.inc()

    emit_stack(0)
    emit_stack(1)
    def emit_vint(tt):
        @block.vector
        def _(v, tt=tt):
            if tt == 0:
                v.wait_ge(s_gpe.h, 32)
                v.wait_ge(s_gp.h, 5)
            if tt >= 2:
                v.wait_ge(s_om.h, 128 * (tt - 1))
            for g in range(8):
                v.tensor_copy(
                    v_int[:, tt % 2, :, 4 * g:4 * (g + 1)],
                    vT[:, g, 128 * tt:128 * (tt + 1)].unsqueeze(2)
                    .broadcast_to([128, 128, 4])).then_inc(s_vi.h, 1)
                s_vi.inc()

    emit_vint(0)
    for sg in range(64):
        tt = sg // 8
        if sg % 8 == 4 and tt + 1 < NT:
            emit_vint(tt + 1)

        for bh in range(2):
            B = 2 * sg + bh

            @block.tensor
            def _(t, sg=sg, bh=bh, B=B):
                if bh == 0:
                    t.wait_ge(s_stk[sg % 2].h, 32 * (sg // 2 + 1))
                if B == 0:
                    t.wait_ge(s_qte.h, 16)
                if B == 1:
                    t.wait_ge(s_t1e.h, 2)
                if B >= 2:
                    t.wait_ge(s_ex.h, B - 1)
                for i in range(8):
                    mm = t.matmul(rotS8[B % 2][:, i, :], k_stk[:, sg % 2, 8 * bh + i, :],
                                  q_stk[:, sg % 2, 8 * bh + i, :], start=True, stop=True)
                    if i == 7:
                        mm.then_inc(s_sm.h, 8)
                        s_sm.inc(8)

            Bm = B - 1
            if Bm >= 0:
                @block.tensor
                def _(t, Bm=Bm):
                    ttm = Bm // 16
                    t.wait_ge(s_ex.h, Bm + 1)
                    if Bm % 16 == 0:
                        t.wait_ge(s_vi.h, 8 * (ttm + 1))
                    if Bm == 0:
                        t.wait_ge(s_sc0.h, p0c["SC0_ALL"])
                    if Bm >= 2:
                        t.wait_ge(s_nm.h, Bm - 1)
                    tloc = 16 * ((Bm // 2) % 8) + 8 * (Bm % 2)
                    for i in range(8):
                        mm = t.matmul(rotO8[Bm % 2][:, i, 0:33], exp_sb[:, Bm % 3, i, :],
                                      v_int[:, ttm % 2, tloc + i, :],
                                      start=True, stop=True)
                        if i == 7:
                            mm.then_inc(s_om.h, 8)
                            s_om.inc(8)

        @block.vector
        def _(v, sg=sg):
            for bh in range(2):
                B = 2 * sg + bh
                v.wait_ge(s_om.h, 8 * (B + 1))
                v.reciprocal(rcp8[:, B % 3, :], rotO8[B % 2][:, :, 32]).then_inc(s_rc.h, 1)
                s_rc.inc()
                v.wait_ge(s_rc.h, s_rc.v)
                if sg >= 2 and bh == 0:
                    v.wait_ge(s_mg[sg % 2].h, 32 * (sg // 2))
                v.tensor_tensor(attn_sb[:, sg % 2, 8 * bh:8 * (bh + 1), :],
                                rotO8[B % 2][:, :, 0:32],
                                rcp8[:, B % 3, :].unsqueeze(2).broadcast_to([128, 8, 32]),
                                ALU.mult).then_inc(s_nm.h, 1)
                s_nm.inc()

        if sg + 2 < 64:
            emit_stack(sg + 2)
        emit_merge(sg)

    @block.tensor
    def _(t):
        Bm = 127
        t.wait_ge(s_ex.h, 128)
        t.wait_ge(s_vi.h, 64)
        t.wait_ge(s_nm.h, Bm - 1)
        tloc = 16 * ((Bm // 2) % 8) + 8 * (Bm % 2)
        for i in range(8):
            mm = t.matmul(rotO8[Bm % 2][:, i, 0:33], exp_sb[:, Bm % 3, i, :],
                          v_int[:, (Bm // 16) % 2, tloc + i, :],
                          start=True, stop=True)
            if i == 7:
                mm.then_inc(s_om.h, 8)
                s_om.inc(8)

    es2.close()
    es3.close()
    es4 = contextlib.ExitStack()
    sb4 = lambda n, s, d: es4.enter_context(nc.sbuf_tensor(n, s, d))
    wT = sb4("wT", [128, 4, 16, 512], BF16)
    atile = sb4("atile", [128, 2, 512], F32)
    xtile = sb4("xtile", [128, 2, 512], F32)
    beffT2 = sb4("beffT2", [17, 2, 512], BF16)

    wfT_scr_v = wfT_scr.rearrange("(c p) o -> p c o", p=128)
    for h in range(1, 16):
        @block.scalar
        def _(sc, h=h):
            if h == 1:
                sc.wait_ge(s_om.h, 1024)
                sc.wait_ge(s_nm.h, 128)
                sc.wait_ge(s_vi.h, 64)
                sc.wait_ge(s_mg[0].h, s_mg[0].v)
                sc.wait_ge(s_mg[1].h, s_mg[1].v)
                sc.wait_ge(s_ccf.h, 1)
            if h >= 5:
                sc.wait_ge(s_fm.h, 8 * ((h - 4) // 2 + 1))
            oc, hh = h // 2, h % 2
            sc.dma_start(out=wT[:, (h - 1) % 4, :, :],
                         in_=wfT_scr_v[:, 16 * hh:16 * (hh + 1),
                                       512 * oc:512 * (oc + 1)]
                         ).then_inc(s_wt4[(h - 1) % 4].h, 16)
            s_wt4[(h - 1) % 4].inc(16)

    for oc in range(8):
        @block.sync
        def _(sync, oc=oc):
            if oc == 0:
                sync.wait_ge(s_ex.h, 128)
                sync.wait_ge(s_om.h, 1024)
                sync.wait_ge(s_nm.h, 128)
                sync.wait_ge(s_vi.h, 64)
                sync.wait_ge(s_mg[0].h, s_mg[0].v)
                sync.wait_ge(s_mg[1].h, s_mg[1].v)
                sync.wait_ge(s_ccf.h, 1)
            if oc >= 2:
                sync.wait_ge(s_fm.h, 8 * (oc - 1))
            sync.dma_start(out=beffT2[:, oc % 2, :], in_=beffT_scr[:, oc, :]
                           ).then_inc(s_bti.h, 16)
            s_bti.inc(16)

        for tt in range(NT):
            i = oc * NT + tt
            bnk = pmb[i % 2]

            @block.sync
            def _(sync, oc=oc, tt=tt, i=i):
                if i == 0:
                    sync.wait_ge(s_mg[0].h, s_mg[0].v)
                    sync.wait_ge(s_mg[1].h, s_mg[1].v)
                if i >= 2:
                    sync.wait_ge(s_out[i % 2].h, 16 * (i // 2))
                sync.dma_start(out=atile[:, i % 2, :],
                               in_=attn_scr[tt, :, 16 * oc:16 * (oc + 1), :]
                               .rearrange("p a b -> p (a b)")).then_inc(s_ar[i % 2].h, 16)
                s_ar[i % 2].inc(16)
                if i >= 2:
                    sync.wait_ge(s_a2.h, i - 1)
                sync.dma_start(out=xtile[:, i % 2, :],
                               in_=xsr[tt][:, 512 * oc:512 * (oc + 1)]).then_inc(s_xr[i % 2].h, 16)
                s_xr[i % 2].inc(16)

            @block.tensor
            def _(t, oc=oc, tt=tt, i=i, bnk=bnk):
                if tt == 0:
                    for h in (2 * oc, 2 * oc + 1):
                        if h == 0:
                            t.wait_ge(s_brw.h, 32)
                        else:
                            t.wait_ge(s_wt4[(h - 1) % 4].h, 16 * ((h - 1) // 4 + 1))
                if i == 0:
                    t.wait_ge(s_ex.h, 128)
                    t.wait_ge(s_nm.h, 128)
                if i >= 2:
                    t.wait_ge(s_a1.h, i - 1)
                for c in range(EC):
                    h_used = 2 * oc + c // 16
                    rhs = (wTboot[:, c, :] if h_used == 0
                           else wT[:, (h_used - 1) % 4, c % 16, :])
                    t.matmul(bnk[:], xnT[:, c, 128 * tt:128 * (tt + 1)],
                             rhs, start=(c == 0), stop=False)
                if tt == 0:
                    t.wait_ge(s_bti.h, 16 * (oc + 1))
                t.matmul(bnk[:], t1T[:, 128 * tt:128 * (tt + 1)], beffT2[:, oc % 2, :],
                         start=False, stop=True).then_inc(s_fm.h, 1)
                s_fm.inc()

            @block.vector
            def _(v, i=i, bnk=bnk):
                v.wait_ge(s_fm.h, i + 1)
                v.wait_ge(s_ar[i % 2].h, 16 * (i // 2 + 1))
                if i >= 2:
                    v.wait_ge(s_a2.h, i - 1)
                v.tensor_tensor(atile[:, i % 2, :], bnk[:], atile[:, i % 2, :],
                                ALU.add).then_inc(s_a1.h, 1)
                s_a1.inc()

            @block.gpsimd
            def _(gp, oc=oc, tt=tt, i=i):
                gp.wait_ge(s_a1.h, i + 1)
                gp.wait_ge(s_xr[i % 2].h, 16 * (i // 2 + 1))
                gp.tensor_tensor(atile[:, i % 2, :], atile[:, i % 2, :],
                                 xtile[:, i % 2, :], ALU.add).then_inc(s_a2.h, 1)
                s_a2.inc()
                gp.wait_ge(s_a2.h, i + 1)
                gp.dma_start(out=outr[tt][:, 512 * oc:512 * (oc + 1)],
                             in_=atile[:, i % 2, :]).then_inc(s_out[i % 2].h, 16)
                s_out[i % 2].inc(16)

    @block.scalar
    def _(sc):
        for s in (s_x + s_ktw + s_qtw + s_cbw + s_wl + s_wt4):
            sc.wait_ge(s.h, s.v)

    @block.sync
    def _(sync):
        for s in ([s_small, s_lab, s_bti, s_cck, s_ccv, s_ccq, s_ccf]
                  + s_lk + s_lv + s_lq + s_lf + s_sk + s_sv + s_sq + s_sf
                  + s_stk + s_mg + s_ar + s_xr + s_out):
            sync.wait_ge(s.h, s.v)

    @block.gpsimd
    def _(gp):
        gp.wait_ge(s_gpd.h, s_gpd.v)
        gp.wait_ge(s_fbc.h, s_fbc.v)
        gp.wait_ge(s_bias.h, s_bias.v)

    es4.close()
    es_boot.close()
    es2a.close()
    es.close()
    return nc


def kernel(**inputs):
    x = np.asarray(inputs["L0_attn_x"], dtype=np.float32).reshape(2 * 4096, E)
    shards = [np.ascontiguousarray(x[i * T:(i + 1) * T]) for i in range(NCORES)]
    wqf = np.asarray(inputs["L0_attn_Wq"], dtype=np.float32).reshape(E, 128, 32)
    wkf = np.asarray(inputs["L0_attn_Wk"], dtype=np.float32).reshape(E, 128, 8)
    wvf = np.asarray(inputs["L0_attn_Wv"], dtype=np.float32).reshape(E, 128, 8)
    wfT = np.asarray(inputs["L0_ffn_weight"], dtype=np.float32).reshape(E, E).T
    base = {
        "bq": np.ascontiguousarray(np.asarray(inputs["L0_attn_bq"], dtype=np.float32)),
        "bk": np.ascontiguousarray(np.asarray(inputs["L0_attn_bk"], dtype=np.float32)),
        "bv": np.ascontiguousarray(np.asarray(inputs["L0_attn_bv"], dtype=np.float32)),
        "fb": np.ascontiguousarray(np.asarray(inputs["L0_ffn_bias"], dtype=np.float32)).reshape(1, E),
        "la": np.ascontiguousarray(np.asarray(inputs["L0_ffn_lora_A"], dtype=np.float32)),
        "lbw": np.ascontiguousarray(np.asarray(inputs["L0_ffn_lora_B"], dtype=np.float32)),
        "lsi": np.ascontiguousarray(np.asarray(inputs["L0_ffn_lora_start_indices"], dtype=np.int32)).reshape(1, 4),
    }
    in_maps = []
    for core in range(NCORES):
        m = dict(base)
        m["xs"] = shards[core]
        m["wkm"] = np.ascontiguousarray(wkf[512 * core:512 * (core + 1)])
        m["wvm"] = np.ascontiguousarray(wvf[512 * core:512 * (core + 1)])
        m["wqm"] = np.ascontiguousarray(wqf[512 * core:512 * (core + 1)])
        m["wfm"] = np.ascontiguousarray(wfT[512 * core:512 * (core + 1)])
        m["bidx"] = np.array([[core // 4]], dtype=np.int32)
        in_maps.append(m)

    nc = build_program()
    res = run_bass_kernel_spmd(nc, in_maps, core_ids=list(range(NCORES)))
    outs = [np.asarray(r["out"]) for r in res.results]
    return np.concatenate(outs, axis=0).reshape(2, 4096, E).astype(np.float32)


# version 31
# speedup vs baseline: 1.0016x; 1.0016x over previous
import contextlib
import math

import numpy as np
import concourse.bass as bass
import concourse.mybir as mybir
from concourse.bass_utils import run_bass_kernel_spmd

F32 = mybir.dt.float32
BF16 = mybir.dt.bfloat16
I32 = mybir.dt.int32
AF = mybir.ActivationFunctionType
ALU = mybir.AluOpType
AX = mybir.AxisListType

NCORES = 8
T = 1024
NT = 8
E = 4096
EC = 32
ISD = float(1.0 / math.sqrt(128.0))


class Sem:
    def __init__(self, h):
        self.h = h
        self.v = 0

    def inc(self, n=1):
        self.v += n
        return self.v


def build_program():
    nc = bass.Bass("TRN2", target_bir_lowering=False)

    xs = nc.dram_tensor("xs", [T, E], F32, kind="ExternalInput").ap()
    wkm = nc.dram_tensor("wkm", [512, 128, 8], F32, kind="ExternalInput").ap()
    wvm = nc.dram_tensor("wvm", [512, 128, 8], F32, kind="ExternalInput").ap()
    wqm = nc.dram_tensor("wqm", [512, 128, 32], F32, kind="ExternalInput").ap()
    wfm = nc.dram_tensor("wfm", [512, E], F32, kind="ExternalInput").ap()
    bq = nc.dram_tensor("bq", [128, 32], F32, kind="ExternalInput").ap()
    bk = nc.dram_tensor("bk", [128, 8], F32, kind="ExternalInput").ap()
    bv = nc.dram_tensor("bv", [128, 8], F32, kind="ExternalInput").ap()
    fb = nc.dram_tensor("fb", [1, E], F32, kind="ExternalInput").ap()
    la = nc.dram_tensor("la", [3, 16, E], F32, kind="ExternalInput").ap()
    lbw = nc.dram_tensor("lbw", [3, E, 16], F32, kind="ExternalInput").ap()
    lsi = nc.dram_tensor("lsi", [1, 4], I32, kind="ExternalInput").ap()
    bidx = nc.dram_tensor("bidx", [1, 1], I32, kind="ExternalInput").ap()
    out = nc.dram_tensor("out", [T, E], F32, kind="ExternalOutput").ap()

    wkstg = nc.dram_tensor("wkstg", [1024, 512], BF16, kind="Internal").ap()
    wvstg = nc.dram_tensor("wvstg", [1024, 512], BF16, kind="Internal").ap()
    wqstg = nc.dram_tensor("wqstg", [1024, 512], BF16, kind="Internal").ap()
    wfstg = nc.dram_tensor("wfstg", [512, E], BF16, kind="Internal").ap()
    wk_scr = nc.dram_tensor("wk_scr", [8192, 512], BF16, kind="Internal",
                            addr_space="Shared").ap()
    wv_scr = nc.dram_tensor("wv_scr", [8192, 512], BF16, kind="Internal",
                            addr_space="Shared").ap()
    wq_scr = nc.dram_tensor("wq_scr", [8192, 512], BF16, kind="Internal",
                            addr_space="Shared").ap()
    wfT_scr = nc.dram_tensor("wfT_scr", [E, E], BF16, kind="Internal",
                             addr_space="Shared").ap()
    qtok_scr = nc.dram_tensor("qtok_scr", [NT, 128, 9, 128], BF16, kind="Internal").ap()
    ktok_scr = nc.dram_tensor("ktok_scr", [NT, 128, 8, 128], BF16, kind="Internal").ap()
    attn_scr = nc.dram_tensor("attn_scr", [NT, 128, 128, 32], F32, kind="Internal").ap()
    brow_scr = nc.dram_tensor("brow_scr", [3, 8, 128], BF16, kind="Internal").ap()
    beffT_scr = nc.dram_tensor("beffT_scr", [17, 8, 512], BF16, kind="Internal").ap()

    es = contextlib.ExitStack()
    sb = lambda n, s, d: es.enter_context(nc.sbuf_tensor(n, s, d))
    pst = lambda n, s, d: es.enter_context(nc.psum_tensor(n, s, d))
    mks = lambda n: Sem(es.enter_context(nc.semaphore(n)))

    ptpA = pst("ptpA", [128, 4, 128], BF16)
    ptpB = pst("ptpB", [128, 4, 128], BF16)
    pmAB = pst("pmAB", [128, 1024], F32)
    psAB = pst("psAB", [128, 1024], F32)
    poAB = pst("poAB", [128, 1024], F32)
    pmA = pmAB[:, 0:512]
    pmB = pmAB[:, 512:1024]
    psA = psAB[:, 0:512].rearrange("p (a b) -> p a b", b=128)
    psB = psAB[:, 512:1024].rearrange("p (a b) -> p a b", b=128)
    poA = poAB[:, 0:512].rearrange("p (a b) -> p a b", b=128)
    poB = poAB[:, 512:1024].rearrange("p (a b) -> p a b", b=128)
    psAb = psAB[:, 0:512].bitcast(BF16).rearrange("p (a b) -> p a b", b=128)
    psBb = psAB[:, 512:1024].bitcast(BF16).rearrange("p (a b) -> p a b", b=128)
    poAb = poAB[:, 0:512].bitcast(BF16).rearrange("p (a b) -> p a b", b=128)
    poBb = poAB[:, 512:1024].bitcast(BF16).rearrange("p (a b) -> p a b", b=128)
    ptp4 = (ptpA[:], ptpB[:], psAb[:, 0:4, :], psBb[:, 0:4, :])
    rotS8 = (psAB[:].rearrange("p (a b) -> p a b", b=128),
             pmAB[:].rearrange("p (a b) -> p a b", b=128))
    rotO8 = (poAB[:, 0:264].rearrange("p (a b) -> p a b", b=33),
             poAB[:, 512:776].rearrange("p (a b) -> p a b", b=33))
    pmb = (pmA, pmB)
    ptp = (ptpA, ptpB)

    xnT = sb("xnT", [128, EC, T], BF16)
    xnt2 = sb("xnt2", [128, 2, E], BF16)
    ident = sb("ident", [128, 128], BF16)
    lsi_s = sb("lsi_s", [1, 4], I32)
    lsi_f = sb("lsi_f", [1, 4], F32)
    bidx_s = sb("bidx_s", [1, 1], I32)
    bidx_f = sb("bidx_f", [1, 1], F32)
    lsi_rep = sb("lsi_rep", [128, 4], F32)
    bidx_rep = sb("bidx_rep", [128, 1], F32)
    cge = sb("cge", [128, 4], F32)
    seg = sb("seg", [128, 1], F32)
    usev = sb("usev", [128, 1], F32)
    m0 = sb("m0", [128, 1], F32)
    m1 = sb("m1", [128, 1], F32)
    m2 = sb("m2", [128, 1], F32)
    ta_ = sb("ta_", [128, 1], F32)
    tb_ = sb("tb_", [128, 1], F32)
    tc_ = sb("tc_", [128, 1], F32)
    td_ = sb("td_", [128, 1], F32)
    bq_s = sb("bq_s", [128, 32], F32)
    bqsum = sb("bqsum", [128, 8], F32)
    bk_s = sb("bk_s", [128, 8], F32)
    bv_s = sb("bv_s", [128, 8], F32)
    aeffT = sb("aeffT", [128, EC, 16], BF16)
    t1T = sb("t1T", [17, T], BF16)
    qmax = sb("qmax", [128, NT, 8], F32)
    kmax = sb("kmax", [128, NT, 8], F32)
    cb8 = sb("cb8", [128, NT], F32)
    cbprod = sb("cbprod", [128, 8], F32)
    ones512 = sb("ones512", [1, 512], BF16)
    bns = sb("bns", [128, 2, 8, 6], F32)
    mv = sb("mv", [128, 2, 2], F32)
    negmu = sb("negmu", [128, 2, 1], F32)
    lnv = sb("lnv", [128, 2, 1], F32)
    rstd = sb("rstd", [128, 2, 1], F32)
    lnt = sb("lnt", [128, 2, 1], F32)

    s_small = mks("s_small"); s_lab = mks("s_lab"); s_gpd = mks("s_gpd"); s_gp = mks("s_gp")
    s_v0 = mks("s_v0"); s_t0 = mks("s_t0"); s_sc0 = mks("s_sc0")
    s_x = [mks("s_x0"), mks("s_x1"), mks("s_x2")]
    s_ln = mks("s_ln"); s_rs = mks("s_rs"); s_ap = mks("s_ap")
    s_tr = mks("s_tr"); s_xe = mks("s_xe")
    s_lk = [mks("s_lk0"), mks("s_lk1")]
    s_lv = [mks("s_lv0"), mks("s_lv1")]
    s_lq = [mks("s_lq0"), mks("s_lq1")]
    s_lf = [mks("s_lf0"), mks("s_lf1")]
    s_wl = [mks("s_wl0"), mks("s_wl1")]
    s_gpm = mks("s_gpm")
    s_gpe = mks("s_gpe")
    s_lacc = mks("s_lacc")
    s_wt4 = [mks("s_wt40"), mks("s_wt41"), mks("s_wt42"), mks("s_wt43")]
    s_sk = [mks("s_sk0"), mks("s_sk1")]
    s_sv = [mks("s_sv0"), mks("s_sv1")]
    s_sq = [mks("s_sq0"), mks("s_sq1")]
    s_sf = [mks("s_sf0"), mks("s_sf1")]
    s_ck = mks("s_ck"); s_cv = mks("s_cv"); s_js = mks("s_js"); s_brw = mks("s_brw")
    s_bt = mks("s_bt"); s_bias = mks("s_bias"); s_fbc = mks("s_fbc")
    s_cf = mks("s_cf")
    s_cck = mks("s_cck"); s_ccv = mks("s_ccv")
    s_ccq = mks("s_ccq"); s_ccf = mks("s_ccf")
    s_ktr = mks("s_ktr"); s_qtr = mks("s_qtr")
    s_kte = mks("s_kte"); s_qte = mks("s_qte")
    s_ktw = [mks("s_ktw0"), mks("s_ktw1")]
    s_qtw = [mks("s_qtw0"), mks("s_qtw1")]
    s_kmx = mks("s_kmx"); s_qmx = mks("s_qmx")
    s_cb = mks("s_cb"); s_cbw = [mks("s_cbw0"), mks("s_cbw1")]
    s_t1 = mks("s_t1"); s_t1e = mks("s_t1e"); s_bto = mks("s_bto"); s_bti = mks("s_bti")
    s_stk = [mks("s_stk0"), mks("s_stk1")]; s_vi = mks("s_vi")
    s_sm = mks("s_sm"); s_ex = mks("s_ex"); s_om = mks("s_om")
    s_rc = mks("s_rc"); s_nm = mks("s_nm"); s_mg = [mks("s_mg0"), mks("s_mg1")]
    s_wc = mks("s_wc"); s_wt = mks("s_wt"); s_we = mks("s_we"); s_fm = mks("s_fm")
    s_ar = [mks("s_ar0"), mks("s_ar1")]
    s_xr = [mks("s_xr0"), mks("s_xr1")]
    s_a1 = mks("s_a1"); s_a2 = mks("s_a2")
    s_out = [mks("s_out0"), mks("s_out1")]

    wproj = xnt2[:].rearrange("p b (c d) -> p b c d", d=128)
    wprj = wproj
    xsr = xs.rearrange("(n p) e -> n p e", p=128)
    outr = out.rearrange("(n p) e -> n p e", p=128)

    block = es.enter_context(nc.Block())

    es2a = contextlib.ExitStack()
    sb2a = lambda n, s, d: es2a.enter_context(nc.sbuf_tensor(n, s, d))
    wqraw = sb2a("wqraw", [128, 2, 2048], F32)
    wkc = wqraw[:, 0, :].rearrange("p (b c d) -> p b c d", b=2, c=128, d=8)
    wvc = wqraw[:, 1, :].rearrange("p (b c d) -> p b c d", b=2, c=128, d=8)
    wk_st = sb2a("wk_st", [128, 2, 8, 128], BF16)
    wv_st = sb2a("wv_st", [128, 2, 8, 128], BF16)
    bias_rows = sb2a("bias_rows", [1, 3, 8, 128], BF16)

    es_xt = contextlib.ExitStack()
    xt = es_xt.enter_context(nc.sbuf_tensor("xt", [128, 3, E], F32))

    es0 = contextlib.ExitStack()
    sb0 = lambda n, s, d: es0.enter_context(nc.sbuf_tensor(n, s, d))
    la_a = sb0("la_a", [16, E], BF16)
    iv = sb0("iv", [128, 128], I32)
    onesf = sb0("onesf", [1, 128], F32)
    bqt1 = sb0("bqt1", [128, 8], F32)
    bqt2 = sb0("bqt2", [128, 8], F32)
    ones_long = sb0("ones_long", [1, 1024], BF16)
    aeffA = sb0("aeffA", [16, E], BF16)
    lb_a = sb0("lb_a", [128, EC, 16], BF16)
    beffA = sb0("beffA", [128, EC, 16], BF16)
    beffT = sb0("beffT", [17, 8, 512], BF16)
    bqb = sb0("bqb", [128, 3, 8], BF16)
    brow_sb = sb0("brow_sb", [8, 3, 128], BF16)

    @block.sync
    def _(sync):
        for dst, src in ((lsi_s, lsi), (bidx_s, bidx), (bq_s, bq), (bk_s, bk),
                         (bv_s, bv)):
            sync.dma_start(out=dst[:], in_=src[:]).then_inc(s_small.h, 16)
            s_small.inc(16)

    @block.gpsimd
    def _(gp):
        gp.iota(iv[:], pattern=[[1, 128]], base=0, channel_multiplier=-1).then_inc(s_gp.h, 1)
        s_gp.inc()
        gp.memset(onesf[:], 1.0).then_inc(s_gp.h, 1)
        s_gp.inc()
        gp.memset(ones_long[:], 1.0).then_inc(s_gp.h, 1)
        s_gp.inc()
        gp.memset(ones512[:], 1.0).then_inc(s_gp.h, 1)
        s_gp.inc()

    wkmr = wkm.rearrange("(cl p) d g -> cl p d g", p=128)
    wvmr = wvm.rearrange("(cl p) d g -> cl p d g", p=128)
    wqmr = wqm.rearrange("(cl p) d h -> cl p d h", p=128)
    wfmr = wfm.rearrange("(mc p) o -> mc p o", p=128)
    wkstg_v = wkstg.rearrange("(g p) (cl d) -> g p cl d", p=128, d=128)
    wvstg_v = wvstg.rearrange("(g p) (cl d) -> g p cl d", p=128, d=128)
    wqstg_v = wqstg.rearrange("(g p) (cl d) -> g p cl d", p=128, d=128)

    @block.sync
    def _(sync):
        for st in ("k", "v"):
            lvw, stg_v, st_b, s_l, s_s, s_c = {
                "k": (wkmr, wkstg_v, wk_st, s_lk, s_sk, s_ck),
                "v": (wvmr, wvstg_v, wv_st, s_lv, s_sv, s_cv),
            }[st]
            for cl in range(4):
                if cl >= 2:
                    sync.wait_ge(s_c.h, cl - 1)
                sync.dma_start(out=(wkc if st == "k" else wvc)[:, cl % 2, :, :],
                               in_=lvw[cl]).then_inc(s_l[cl % 2].h, 16)
                s_l[cl % 2].inc(16)
                if cl >= 1:
                    sync.wait_ge(s_c.h, cl)
                    sync.dma_start(out=stg_v[:, :, cl - 1, :].transpose([1, 0, 2]),
                                   in_=st_b[:, (cl - 1) % 2, :, :]
                                   ).then_inc(s_s[(cl - 1) % 2].h, 16)
                    s_s[(cl - 1) % 2].inc(16)
            sync.wait_ge(s_c.h, 4)
            sync.dma_start(out=stg_v[:, :, 3, :].transpose([1, 0, 2]),
                           in_=st_b[:, 1, :, :]).then_inc(s_s[1].h, 16)
            s_s[1].inc(16)

    @block.sync
    def _(sync):
        sync.wait_ge(s_brw.h, 16)
        sync.dma_start(out=bias_rows[:], in_=brow_scr.unsqueeze(0)
                       ).then_inc(s_bias.h, 16)
        s_bias.inc(16)

    def emit_la_load_gp(gp, a):
        if a > 0:
            gp.wait_ge(s_lacc.h, 2 * a)
        gp.dma_start(out=la_a[:], in_=la[a]).then_inc(s_lab.h, 16)
        s_lab.inc(16)
        gp.dma_start(out=lb_a[:],
                     in_=lbw[a].rearrange("(c p) r -> p c r", p=128)
                     ).then_inc(s_lab.h, 16)
        s_lab.inc(16)

    @block.gpsimd
    def _(gp):
        emit_la_load_gp(gp, 0)
        for cl in range(4):
            gp.wait_ge(s_lk[cl % 2].h, 16 * (cl // 2 + 1))
            if cl >= 2:
                gp.wait_ge(s_sk[cl % 2].h, 16 * (cl // 2))
            gp.tensor_copy(wk_st[:, cl % 2, :, :],
                           wkc[:, cl % 2, :, :].transpose([0, 2, 1])).then_inc(s_ck.h, 1)
            s_ck.inc()
        gp.wait_ge(s_sk[0].h, 32)
        gp.wait_ge(s_sk[1].h, 32)
        gp.collective_compute("AllGather", ALU.bypass,
                              replica_groups=[list(range(8))],
                              ins=[wkstg], outs=[wk_scr]).then_inc(s_cck.h, 1)
        s_cck.inc()
        emit_la_load_gp(gp, 1)
        for cl in range(4):
            gp.wait_ge(s_lv[cl % 2].h, 16 * (cl // 2 + 1))
            if cl >= 2:
                gp.wait_ge(s_sv[cl % 2].h, 16 * (cl // 2))
            gp.tensor_copy(wv_st[:, cl % 2, :, :],
                           wvc[:, cl % 2, :, :].transpose([0, 2, 1])).then_inc(s_cv.h, 1)
            s_cv.inc()
        gp.wait_ge(s_sv[0].h, 32)
        gp.wait_ge(s_sv[1].h, 32)
        gp.collective_compute("AllGather", ALU.bypass,
                              replica_groups=[list(range(8))],
                              ins=[wvstg], outs=[wv_scr]).then_inc(s_ccv.h, 1)
        s_ccv.inc()
        emit_la_load_gp(gp, 2)

    @block.vector
    def _(v):
        v.wait_ge(s_gp.h, 1)
        v.tensor_scalar(ident[:], iv[:], 0.0, None, ALU.is_equal).then_inc(s_v0.h, 1)
        s_v0.inc()
        v.wait_ge(s_small.h, 80)
        v.tensor_copy(lsi_f[:], lsi_s[:]).then_inc(s_v0.h, 1)
        s_v0.inc()
        v.tensor_copy(bidx_f[:], bidx_s[:]).then_inc(s_v0.h, 1)
        s_v0.inc()

    @block.tensor
    def _(t):
        t.wait_ge(s_v0.h, 3)
        t.wait_ge(s_gp.h, 2)
        t.matmul(poA[:, 0, 0:4], onesf[:], lsi_f[:], start=True, stop=True).then_inc(s_t0.h, 1)
        s_t0.inc()
        t.matmul(poA[:, 1, 0:1], onesf[:], bidx_f[:], start=True, stop=True).then_inc(s_t0.h, 1)
        s_t0.inc()

    @block.scalar
    def _(sc):
        sc.wait_ge(s_t0.h, 2)
        sc.copy(lsi_rep[:], poA[:, 0, 0:4]).then_inc(s_sc0.h, 1)
        s_sc0.inc()
        sc.copy(bidx_rep[:], poA[:, 1, 0:1]).then_inc(s_sc0.h, 1)
        s_sc0.inc()

    @block.vector
    def _(v):
        def step(fn):
            v.wait_ge(s_v0.h, s_v0.v)
            fn().then_inc(s_v0.h, 1)
            s_v0.inc()
        v.wait_ge(s_sc0.h, 2)
        step(lambda: v.tensor_scalar(cge[:], lsi_rep[:], bidx_rep[:, 0:1], None, ALU.is_gt))
        step(lambda: v.tensor_scalar(ta_[:], cge[:, 0:1], -1.0, 1.0, ALU.mult, ALU.add))
        step(lambda: v.tensor_scalar(tb_[:], ta_[:], -2.0, None, ALU.add))
        step(lambda: v.tensor_tensor(tc_[:], tb_[:], cge[:, 1:2], ALU.mult))
        step(lambda: v.tensor_scalar(tc_[:], tc_[:], 2.0, None, ALU.add))
        step(lambda: v.tensor_scalar(td_[:], cge[:, 3:4], -1.0, 4.0, ALU.mult, ALU.add))
        step(lambda: v.tensor_tensor(ta_[:], tc_[:], td_[:], ALU.subtract))
        step(lambda: v.tensor_tensor(tb_[:], ta_[:], cge[:, 2:3], ALU.mult))
        step(lambda: v.tensor_tensor(tc_[:], tb_[:], td_[:], ALU.add))
        step(lambda: v.tensor_scalar(seg[:], tc_[:], -1.0, None, ALU.add))
        step(lambda: v.tensor_scalar(usev[:], seg[:], 0.0, None, ALU.is_gt))
        step(lambda: v.tensor_scalar(ta_[:], seg[:], 1.0, None, ALU.is_le))
        step(lambda: v.tensor_tensor(m0[:], ta_[:], usev[:], ALU.mult))
        step(lambda: v.tensor_scalar(ta_[:], seg[:], 2.0, None, ALU.is_equal))
        step(lambda: v.tensor_tensor(m1[:], ta_[:], usev[:], ALU.mult))
        step(lambda: v.tensor_scalar(ta_[:], seg[:], 3.0, None, ALU.is_ge))
        step(lambda: v.tensor_tensor(m2[:], ta_[:], usev[:], ALU.mult))
        bq4 = bq_s[:].rearrange("p (g j) -> p g j", j=4)
        step(lambda: v.tensor_tensor(bqt1[:], bq4[:, :, 0], bq4[:, :, 1], ALU.add))
        step(lambda: v.tensor_tensor(bqt2[:], bq4[:, :, 2], bq4[:, :, 3], ALU.add))
        step(lambda: v.tensor_tensor(bqsum[:], bqt1[:], bqt2[:], ALU.add))
        step(lambda: v.tensor_copy(bqb[:, 0, :], bqsum[:]))
        step(lambda: v.tensor_copy(bqb[:, 1, :], bk_s[:]))
        step(lambda: v.tensor_copy(bqb[:, 2, :], bv_s[:]))

    masks = (m0, m1, m2)

    @block.tensor
    def _(t):
        t.wait_ge(s_v0.h, 26)
        for j in range(3):
            t.transpose(poBb[0:8, j, :], bqb[:, j, :], ident[:]).then_inc(s_bt.h, 1)
            s_bt.inc()

    def emit_brow_evac():
        @block.scalar
        def _(sc):
            sc.wait_ge(s_bt.h, 3)
            sc.copy(brow_sb[:], poBb[0:8, 0:3, :]).then_inc(s_bt.h, 1)
            s_bt.inc()
            sc.wait_ge(s_bt.h, 4)
            sc.dma_start(out=brow_scr.transpose([1, 0, 2]), in_=brow_sb[:]
                         ).then_inc(s_brw.h, 16)
            s_brw.inc(16)

    pob = (poAb, poBb)

    def emit_combine(a):
        @block.vector
        def _(v, a=a):
            v.wait_ge(s_lab.h, 32 * (a + 1))
            if a == 0:
                v.wait_ge(s_v0.h, 23)
                v.tensor_scalar(aeffA[:], la_a[:], masks[a][0:16, 0:1], None,
                                ALU.mult).then_inc(s_lacc.h, 1)
            else:
                v.wait_ge(s_lacc.h, 2 * a)
                v.scalar_tensor_tensor(aeffA[:], la_a[:], masks[a][0:16, 0:1],
                                       aeffA[:], ALU.mult, ALU.add).then_inc(s_lacc.h, 1)
            s_lacc.inc()
            v.wait_ge(s_lacc.h, s_lacc.v)
            if a == 0:
                v.tensor_scalar(beffA[:], lb_a[:], masks[a][:, 0:1], None,
                                ALU.mult).then_inc(s_lacc.h, 1)
            else:
                v.scalar_tensor_tensor(beffA[:], lb_a[:], masks[a][:, 0:1],
                                       beffA[:], ALU.mult, ALU.add).then_inc(s_lacc.h, 1)
            s_lacc.inc()

    p0c = {}

    def emit_p0_transposes():
        @block.tensor
        def _(t):
            t.wait_ge(s_lacc.h, 6)
            t.wait_ge(s_bt.h, 4)
            for c in range(EC):
                grp, slot = c // 4, c % 4
                if grp == 0:
                    t.wait_ge(s_sc0.h, 2)
                if grp >= 2:
                    t.wait_ge(s_sc0.h, 2 + (grp - 1))
                t.transpose(pob[grp % 2][:, slot, 0:16], aeffA[:, 128 * c:128 * (c + 1)],
                            ident[0:16, 0:16]).then_inc(s_t0.h, 1)
                s_t0.inc()

        @block.scalar
        def _(sc):
            for grp in range(8):
                sc.wait_ge(s_t0.h, 2 + 4 * (grp + 1))
                sc.copy(aeffT[:, 4 * grp:4 * (grp + 1), :], pob[grp % 2][:, 0:4, 0:16]
                        ).then_inc(s_sc0.h, 1)
                s_sc0.inc()
        p0c["SC0_AEFF"] = s_sc0.v
        p0c["T0_AEFF"] = s_t0.v

        @block.tensor
        def _(t):
            for c in range(EC):
                grp, slot = c // 4, c % 4
                if slot == 0:
                    t.wait_ge(s_sc0.h, p0c["SC0_AEFF"] - 1 + grp)
                t.transpose(pob[grp % 2][0:16, slot, :], beffA[:, c, :], ident[:]
                            ).then_inc(s_t0.h, 1)
                s_t0.inc()
        p0c["T0_ALL"] = s_t0.v

        @block.scalar
        def _(sc):
            for grp in range(8):
                sc.wait_ge(s_t0.h, p0c["T0_AEFF"] + 4 * (grp + 1))
                sc.copy(beffT[0:16, grp, :], pob[grp % 2][0:16, 0:4, :]).then_inc(s_sc0.h, 1)
                s_sc0.inc()
        p0c["SC0_ALL"] = s_sc0.v

    def emit_xload(i):
        @block.scalar
        def _(sc, i=i):
            if i >= 3:
                sc.wait_ge(s_ap.h, i - 2)
            hi = E if i < 2 else E // 2
            sc.dma_start(out=xt[:, i % 3, 0:hi], in_=xsr[i][:, 0:hi]
                         ).then_inc(s_x[i % 3].h, 16)
            s_x[i % 3].inc(16)

        if i >= 2:
            @block.sync
            def _(sync, i=i):
                if i >= 3:
                    sync.wait_ge(s_ap.h, i - 2)
                sync.dma_start(out=xt[:, i % 3, E // 2:E], in_=xsr[i][:, E // 2:E]
                               ).then_inc(s_x[i % 3].h, 16)
                s_x[i % 3].inc(16)

    emit_xload(0)
    emit_xload(1)
    emit_brow_evac()

    def emit_apply(i):
        @block.vector
        def _(v, i=i):
            v.wait_ge(s_rs.h, 2 * (i + 1))
            if i >= 2:
                v.wait_ge(s_tr.h, 32 * (i - 1))
            v.tensor_scalar(xnt2[:, i % 2, :], xt[:, i % 3, :], negmu[:, i % 2, :],
                            rstd[:, i % 2, :], ALU.add, ALU.mult).then_inc(s_ap.h, 1)
            s_ap.inc()

        @block.tensor
        def _(t, i=i):
            t.wait_ge(s_ap.h, i + 1)
            if i == 0:
                t.wait_ge(s_v0.h, 1)
            for c in range(EC):
                grp = c // 4
                gidx = 8 * i + grp
                if gidx >= 2:
                    t.wait_ge(s_xe.h, gidx - 1)
                t.transpose(ptp[gidx % 2][:, c % 4, :],
                            xnt2[:, i % 2, 128 * c:128 * (c + 1)], ident[:]
                            ).then_inc(s_tr.h, 1)
                s_tr.inc()

        @block.scalar
        def _(sc, i=i):
            for grp in range(8):
                gidx = 8 * i + grp
                sc.wait_ge(s_tr.h, 4 * (gidx + 1))
                sc.copy(xnT[:, 4 * grp:4 * (grp + 1), 128 * i:128 * (i + 1)],
                        ptp[gidx % 2][:, :, :]).then_inc(s_xe.h, 1)
                s_xe.inc()

    for i in range(NT):
        if i >= 1:
            emit_apply(i - 1)

        @block.vector
        def _(v, i=i):
            xw = [16, 16, 32, 48, 48, 64, 80, 80][i]
            v.wait_ge(s_x[i % 3].h, xw)
            for c in range(8):
                v.bn_stats(bns[:, i % 2, c, :], xt[:, i % 3, 512 * c:512 * (c + 1)]
                           ).then_inc(s_ln.h, 1)
                s_ln.inc()
            v.wait_ge(s_ln.h, s_ln.v)
            v.wait_ge(s_rs.h, 2 * i)
            v.bn_aggr(mv[:, i % 2, :], bns[:, i % 2, :, :]).then_inc(s_ln.h, 1)
            s_ln.inc()
            v.wait_ge(s_ln.h, s_ln.v)
            v.tensor_scalar(negmu[:, i % 2, :], mv[:, i % 2, 0:1], -1.0, None,
                            ALU.mult).then_inc(s_ln.h, 1)
            s_ln.inc()
            v.wait_ge(s_ln.h, s_ln.v)
            v.tensor_scalar(lnv[:, i % 2, :], mv[:, i % 2, 1:2], 1e-5, None,
                            ALU.add).then_inc(s_ln.h, 1)
            s_ln.inc()

        @block.scalar
        def _(sc, i=i):
            sc.wait_ge(s_ln.h, 11 * i + 11)
            sc.activation(lnt[:, i % 2, :], lnv[:, i % 2, :], AF.Ln).then_inc(s_rs.h, 1)
            s_rs.inc()
            sc.wait_ge(s_rs.h, s_rs.v)
            sc.activation(rstd[:, i % 2, :], lnt[:, i % 2, :], AF.Exp, scale=-0.5
                          ).then_inc(s_rs.h, 1)
            s_rs.inc()

        if i + 2 < NT:
            emit_xload(i + 2)
        if i in (1, 3, 5):
            emit_combine(i // 2)
        if i == 6:
            emit_p0_transposes()

    emit_apply(7)

    @block.gpsimd
    def _(gp):
        gp.wait_ge(s_sc0.h, p0c["SC0_ALL"])
        gp.wait_ge(s_t0.h, p0c["T0_ALL"])
        gp.dma_start(out=beffT[16:17, :, :], in_=fb.rearrange("a (b c) -> a b c", c=512)
                     ).then_inc(s_fbc.h, 16)
        s_fbc.inc(16)

    @block.scalar
    def _(sc):
        sc.wait_ge(s_fbc.h, 16)
        sc.dma_start(out=beffT_scr[:], in_=beffT[:]).then_inc(s_bto.h, 16)
        s_bto.inc(16)
        sc.wait_ge(s_gp.h, 4)
        sc.dma_start(out=t1T[16:17, :], in_=ones_long[0:1, 0:1024]).then_inc(s_gpd.h, 16)
        s_gpd.inc(16)

    es0.close()
    es_xt.close()
    es_boot = contextlib.ExitStack()
    wTboot = es_boot.enter_context(nc.sbuf_tensor("wTboot", [128, 16, 512], BF16))
    es3 = contextlib.ExitStack()
    sb3 = lambda n, s, d: es3.enter_context(nc.sbuf_tensor(n, s, d))
    vT = sb3("vT", [128, 8, T], BF16)
    q_stk = sb3("q_stk", [9, 2, 16, 128], BF16)
    k_stk = sb3("k_stk", [9, 2, 16, 128], BF16)
    v_int = sb3("v_int", [128, 2, 128, 33], BF16)
    exp_sb = sb3("exp_sb", [128, 3, 8, 128], BF16)
    attn_sb = sb3("attn_sb", [128, 2, 16, 32], F32)
    rcp8 = sb3("rcp8", [128, 3, 8], F32)

    es2 = contextlib.ExitStack()
    sb2 = lambda n, s, d: es2.enter_context(nc.sbuf_tensor(n, s, d))
    cbc = sb2("cbc", [128, 128], BF16)
    kpl = sb2("kpl", [128, 2, 512], BF16)
    qtk4 = sb2("qtk4", [128, 2, 4, 128], BF16)
    wq_st = sb2("wq_st", [128, 8, 128], BF16)
    tjA = sb2("tjA", [128, 8, 64], BF16)
    wfb = sb2("wfb", [128, 2048], BF16)

    @block.sync
    def _(sync):
        for cl in range(4):
            for hh in range(2):
                h = 2 * cl + hh
                if h == 0:
                    sync.wait_ge(s_ck.h, 4)
                    sync.wait_ge(s_cv.h, 4)
                if h >= 2:
                    sync.wait_ge(s_js.h, 3 * (h - 1))
                sync.dma_start(out=wqraw[:, h % 2, :]
                               .rearrange("p (d h4) -> p d h4", h4=32),
                               in_=wqmr[cl][:, 64 * hh:64 * (hh + 1), :]
                               ).then_inc(s_lq[h % 2].h, 16)
                s_lq[h % 2].inc(16)
            sync.wait_ge(s_js.h, 6 * (cl + 1))
            sync.dma_start(out=wqstg_v[:, :, cl, :].transpose([1, 0, 2]),
                           in_=wq_st[:, :, :]).then_inc(s_sq[cl % 2].h, 16)
            s_sq[cl % 2].inc(16)
        for pc in range(8):
            mc, hh = pc // 2, pc % 2
            if pc < 2:
                sync.wait_ge(s_js.h, 24)
            if pc >= 2:
                sync.wait_ge(s_cf.h, pc - 1)
            sync.dma_start(out=wqraw[:, pc % 2, :],
                           in_=wfmr[mc][:, 2048 * hh:2048 * (hh + 1)]
                           ).then_inc(s_lf[pc % 2].h, 16)
            s_lf[pc % 2].inc(16)
            if pc >= 1:
                w = pc - 1
                sync.wait_ge(s_cf.h, w + 1)
                sync.dma_start(out=wfstg[128 * (w // 2):128 * (w // 2 + 1),
                                         2048 * (w % 2):2048 * (w % 2 + 1)],
                               in_=wfb[:]).then_inc(s_sf[w % 2].h, 16)
                s_sf[w % 2].inc(16)
        w = 7
        sync.wait_ge(s_cf.h, 8)
        sync.dma_start(out=wfstg[128 * (w // 2):128 * (w // 2 + 1),
                                 2048 * (w % 2):2048 * (w % 2 + 1)],
                       in_=wfb[:]).then_inc(s_sf[w % 2].h, 16)
        s_sf[w % 2].inc(16)

    @block.gpsimd
    def _(gp):
        for h in range(8):
            cl, hh = h // 2, h % 2
            gp.wait_ge(s_lq[h % 2].h, 16 * (h // 2 + 1))
            if cl >= 1 and hh == 0:
                gp.wait_ge(s_sq[(cl - 1) % 2].h, 16 * ((cl - 1) // 2 + 1))
            wqh = wqraw[:, h % 2, :].rearrange("p (d g j) -> p d g j", g=8, j=4)
            a0 = wqh[:, :, :, 0].transpose([0, 2, 1])
            a1 = wqh[:, :, :, 1].transpose([0, 2, 1])
            a2 = wqh[:, :, :, 2].transpose([0, 2, 1])
            a3 = wqh[:, :, :, 3].transpose([0, 2, 1])
            outv = wq_st[:, :, 64 * hh:64 * (hh + 1)]
            if h == 0:
                gp.wait_ge(s_bto.h, 16)
                gp.wait_ge(s_gpd.h, 16)
                gp.wait_ge(s_lacc.h, 6)
                gp.wait_ge(s_brw.h, 16)
                gp.wait_ge(s_ap.h, 8)
            if h >= 1:
                gp.wait_ge(s_js.h, s_js.v)
            gp.tensor_tensor(tjA[:], a0, a1, ALU.add).then_inc(s_js.h, 1)
            s_js.inc()
            gp.tensor_tensor(outv, a2, a3, ALU.add).then_inc(s_js.h, 1)
            s_js.inc()
            gp.wait_ge(s_js.h, s_js.v)
            gp.tensor_tensor(outv, outv, tjA[:], ALU.add).then_inc(s_js.h, 1)
            s_js.inc()
        gp.wait_ge(s_sq[0].h, 32)
        gp.wait_ge(s_sq[1].h, 32)
        gp.collective_compute("AllGather", ALU.bypass,
                              replica_groups=[list(range(8))],
                              ins=[wqstg], outs=[wq_scr]).then_inc(s_ccq.h, 1)
        s_ccq.inc()
        for pc in range(8):
            gp.wait_ge(s_lf[pc % 2].h, 16 * (pc // 2 + 1))
            if pc == 0:
                gp.wait_ge(s_bto.h, 16)
                gp.wait_ge(s_gpd.h, 16)
                gp.wait_ge(s_lacc.h, 6)
                gp.wait_ge(s_brw.h, 16)
                gp.wait_ge(s_ap.h, 8)
            if pc >= 1:
                gp.wait_ge(s_sf[(pc - 1) % 2].h, 16 * ((pc - 1) // 2 + 1))
            gp.tensor_copy(wfb[:], wqraw[:, pc % 2, :]).then_inc(s_cf.h, 1)
            s_cf.inc()
        gp.wait_ge(s_sf[0].h, 64)
        gp.wait_ge(s_sf[1].h, 64)
        gp.collective_compute("AllGather", ALU.bypass,
                              replica_groups=[list(range(8))],
                              ins=[wfstg], outs=[wfT_scr]).then_inc(s_ccf.h, 1)
        s_ccf.inc()


    @block.scalar
    def _(sc):
        sc.wait_ge(s_gpd.h, 16)
        sc.wait_ge(s_bto.h, 16)
        sc.wait_ge(s_lacc.h, 6)
        sc.wait_ge(s_brw.h, 16)
        sc.wait_ge(s_ap.h, 8)
        for half in range(2):
            for hh in range(2):
                sc.dma_start(out=k_stk[8:9, half, 8 * hh:8 * (hh + 1), :]
                             .rearrange("a c d -> a (c d)"),
                             in_=t1T[16:17, 0:1024]).then_inc(s_gpd.h, 16)
                s_gpd.inc(16)

    @block.vector
    def _(v):
        v.wait_ge(s_bto.h, 16)
        v.wait_ge(s_gpd.h, 16)
        v.wait_ge(s_lacc.h, 6)
        v.wait_ge(s_brw.h, 16)
        v.memset(v_int[:, :, :, 32:33], 1.0).then_inc(s_gp.h, 1)
        s_gp.inc()

    wk_scr5 = wk_scr.rearrange("(r g p) (cl d) -> r g p cl d", g=8, p=128, d=128)
    wv_scr5 = wv_scr.rearrange("(r g p) (cl d) -> r g p cl d", g=8, p=128, d=128)
    wq_scr5 = wq_scr.rearrange("(r g p) (cl d) -> r g p cl d", g=8, p=128, d=128)

    def plane_load(j):
        pl = j // 8
        g = j % 8
        s_cc = (s_cck, s_ccv, s_ccq)[pl]
        scr5 = (wk_scr5, wv_scr5, wq_scr5)[pl]

        @block.scalar
        def _(sc, j=j, g=g, s_cc=s_cc, scr5=scr5):
            sc.wait_ge(s_cc.h, 1)
            if j < 2:
                sc.wait_ge(s_tr.h, 224 if j == 0 else 256)
            else:
                sc.wait_ge(s_gpm.h, 2 * (j - 1))
            sc.dma_start(out=wprj[:, j % 2].rearrange("p (r cl) d -> p r cl d", cl=4),
                         in_=scr5[:, g].transpose([1, 0, 2, 3])
                         ).then_inc(s_wl[j % 2].h, 16)
            s_wl[j % 2].inc(16)

    def plane_phase(kind):
        plane = {"k": 0, "v": 1, "q": 2}[kind]
        bidx_row = {"k": 1, "v": 2, "q": 0}[kind]
        s_tr_, s_te_, s_tw_, s_mx_ = {
            "k": (s_ktr, s_kte, s_ktw, s_kmx),
            "q": (s_qtr, s_qte, s_qtw, s_qmx),
            "v": (None, None, None, None),
        }[kind]
        tok_scr = {"k": ktok_scr, "q": qtok_scr, "v": None}[kind]
        mx = {"k": kmax, "q": qmax, "v": None}[kind]

        def emit_mm(m):
            g, th = m // 2, m % 2
            mg = 16 * plane + m
            j = 8 * plane + g
            bank = pmb[m % 2]

            @block.tensor
            def _(t, g=g, th=th, m=m, mg=mg, j=j, bank=bank):
                t.wait_ge(s_wl[j % 2].h, 16 * (j // 2 + 1))

                if kind == "k":
                    t.wait_ge(s_xe.h, 32 if th == 0 else 64)
                if mg >= 2:
                    t.wait_ge(s_gpe.h, mg - 1)
                if mg == 0:
                    t.wait_ge(s_bias.h, 16)
                    t.wait_ge(s_gp.h, 4)
                for c in range(EC):
                    t.matmul(bank[:], wprj[:, j % 2, c, :],
                             xnT[:, c, 512 * th:512 * (th + 1)],
                             start=(c == 0), stop=False)
                t.matmul(bank[:], bias_rows[0:1, bidx_row, g, :],
                         ones512[:], start=False, stop=True).then_inc(s_gpm.h, 1)
                s_gpm.inc()

        def emit_evac(m):
            g, th = m // 2, m % 2
            mg = 16 * plane + m
            bank = pmb[m % 2]

            @block.vector
            def _(v, g=g, th=th, m=m, mg=mg, bank=bank):
                v.wait_ge(s_gpm.h, mg + 1)
                if mg == 0:
                    v.wait_ge(s_bto.h, 16)
                    v.wait_ge(s_gpd.h, 16)
                    v.wait_ge(s_lacc.h, 6)
                    v.wait_ge(s_brw.h, 16)
                if kind == "v":
                    v.tensor_copy(vT[:, g, 512 * th:512 * (th + 1)], bank[:]
                                  ).then_inc(s_gpe.h, 1)
                    s_gpe.inc()
                else:
                    if m >= 2:
                        v.wait_ge(s_tr_.h, 4 * (m - 1))
                    v.tensor_copy(kpl[:, m % 2, :], bank[:]).then_inc(s_gpe.h, 1)
                    s_gpe.inc()

        def emit_tr(m):
            g, th = m // 2, m % 2
            mg = 16 * plane + m

            @block.tensor
            def _(t, m=m, mg=mg):
                t.wait_ge(s_gpe.h, mg + 1)
                for j in range(4):
                    if m >= 4 and j == 0:
                        t.wait_ge(s_te_.h, m - 3)
                    t.transpose(ptp4[m % 4][:, j, :],
                                kpl[:, m % 2, 128 * j:128 * (j + 1)], ident[:]
                                ).then_inc(s_tr_.h, 1)
                    s_tr_.inc()

            @block.vector
            def _(v, g=g, th=th, m=m):
                v.wait_ge(s_tr_.h, 4 * (m + 1))
                if m == 0 and kind == "q":
                    v.wait_ge(s_ktw[0].h, 128)
                    v.wait_ge(s_ktw[1].h, 128)
                    v.wait_ge(s_kmx.h, 64)
                if m >= 2:
                    v.wait_ge(s_tw_[m % 2].h, 16 * (m // 2))
                    v.wait_ge(s_mx_.h, 4 * (m - 1))
                v.tensor_copy(qtk4[:, m % 2, :, :], ptp4[m % 4][:, :, :]
                              ).then_inc(s_te_.h, 1)
                s_te_.inc()

            @block.scalar
            def _(sc, g=g, th=th, m=m):
                sc.wait_ge(s_te_.h, m + 1)
                sc.dma_start(
                    out=tok_scr[4 * th:4 * th + 4, :, g, :].transpose([1, 0, 2]),
                    in_=qtk4[:, m % 2, :, :]).then_inc(s_tw_[m % 2].h, 16)
                s_tw_[m % 2].inc(16)

            @block.vector
            def _(v, g=g, th=th, m=m):
                v.wait_ge(s_te_.h, m + 1)
                for j in range(4):
                    v.reduce_max(mx[:, 4 * th + j, g:g + 1],
                                 qtk4[:, m % 2, j, :], axis=AX.X,
                                 apply_absolute_value=True).then_inc(s_mx_.h, 1)
                    s_mx_.inc()

        for m in range(16):
            emit_mm(m)
            emit_evac(m)
            if kind != "v" and m >= 1:
                emit_tr(m - 1)
            if m % 2 == 1:
                nj = 8 * plane + m // 2 + 2
                if nj < 24:
                    plane_load(nj)
        if kind != "v":
            emit_tr(15)

    plane_load(0)
    plane_load(1)
    plane_phase("k")
    plane_phase("v")

    plane_phase("q")

    for i in range(NT):
        @block.vector
        def _(v, i=i):
            if i == 0:
                v.wait_ge(s_qmx.h, 64)
                v.wait_ge(s_kmx.h, 64)
            v.wait_ge(s_cb.h, s_cb.v)
            v.tensor_tensor(cbprod[:], qmax[:, i, :], kmax[:, i, :], ALU.mult
                            ).then_inc(s_cb.h, 1)
            s_cb.inc()
            v.wait_ge(s_cb.h, s_cb.v)
            v.reduce_sum(cb8[:, i:i + 1], cbprod[:], axis=AX.X, negate=True
                         ).then_inc(s_cb.h, 1)
            s_cb.inc()
            v.wait_ge(s_cb.h, s_cb.v)
            if i >= 1:
                v.wait_ge(s_cbw[(i - 1) % 2].h, 16 * ((i - 1) // 2 + 1))
            v.tensor_copy(cbc[:], cb8[:, i:i + 1].broadcast_to([128, 128])
                          ).then_inc(s_cb.h, 1)
            s_cb.inc()

        @block.scalar
        def _(sc, i=i):
            sc.wait_ge(s_cb.h, 3 * (i + 1))
            sc.dma_start(out=qtok_scr[i, :, 8, :], in_=cbc[:]).then_inc(s_cbw[i % 2].h, 16)
            s_cbw[i % 2].inc(16)

    for th in range(2):
        @block.tensor
        def _(t, th=th):
            bank = pmb[th % 2]
            t.wait_ge(s_gpe.h, 48)
            t.wait_ge(s_sc0.h, p0c["SC0_ALL"])
            for c in range(EC):
                mm = t.matmul(bank[0:16, :], aeffT[:, c, :],
                              xnT[:, c, 512 * th:512 * (th + 1)],
                              start=(c == 0), stop=(c == EC - 1))
                if c == EC - 1:
                    mm.then_inc(s_t1.h, 1)
            s_t1.inc()

        @block.scalar
        def _(sc, th=th):
            sc.wait_ge(s_t1.h, th + 1)
            sc.copy(t1T[0:16, 512 * th:512 * (th + 1)], pmb[th % 2][0:16, :]
                    ).then_inc(s_t1e.h, 1)
            s_t1e.inc()

    def emit_stack(sg):
        tt = sg // 8
        r0 = 16 * (sg % 8)

        @block.sync
        def _(sync, sg=sg, tt=tt, r0=r0):
            if sg == 0:
                sync.wait_ge(s_qtw[0].h, 128)
                sync.wait_ge(s_qtw[1].h, 128)
                sync.wait_ge(s_ktw[0].h, 128)
                sync.wait_ge(s_ktw[1].h, 128)
                sync.wait_ge(s_cbw[0].h, s_cbw[0].v)
                sync.wait_ge(s_cbw[1].h, s_cbw[1].v)
                sync.wait_ge(s_gpd.h, 80)
            if sg >= 2:
                sync.wait_ge(s_sm.h, 16 * (sg - 1))
            sync.dma_start(out=q_stk[:, sg % 2, :, :],
                           in_=qtok_scr[tt, r0:r0 + 16, :, :].transpose([1, 0, 2])
                           ).then_inc(s_stk[sg % 2].h, 16)
            s_stk[sg % 2].inc(16)
            sync.dma_start(out=k_stk[0:8, sg % 2, :, :],
                           in_=ktok_scr[tt, r0:r0 + 16, :, :].transpose([1, 0, 2])
                           ).then_inc(s_stk[sg % 2].h, 16)
            s_stk[sg % 2].inc(16)

    def emit_merge(sg):
        tt = sg // 8
        r0 = 16 * (sg % 8)

        @block.gpsimd
        def _(gp, sg=sg, tt=tt, r0=r0):
            gp.wait_ge(s_nm.h, 2 * (sg + 1))
            for hh in range(2):
                gp.dma_start(
                    out=attn_scr[tt, r0 + 8 * hh:r0 + 8 * (hh + 1), :, :]
                    .transpose([1, 0, 2]),
                    in_=attn_sb[:, sg % 2, 8 * hh:8 * (hh + 1), :]
                    ).then_inc(s_mg[sg % 2].h, 16)
                s_mg[sg % 2].inc(16)

    @block.scalar
    def _(sc):
        sc.wait_ge(s_ccf.h, 1)
        sc.wait_ge(s_cf.h, 8)
        for s in (s_sk[0], s_sk[1], s_sv[0], s_sv[1]):
            sc.wait_ge(s.h, 32)
        sc.wait_ge(s_ap.h, 8)
        sc.wait_ge(s_bto.h, 16)
        sc.wait_ge(s_lacc.h, 6)
        sc.wait_ge(s_gpd.h, 16)
        sc.wait_ge(s_brw.h, 16)
        sc.dma_start(out=wTboot[:],
                     in_=wfT_scr.rearrange("(c p) o -> p c o", p=128)[:, 0:16, 0:512]
                     ).then_inc(s_brw.h, 16)
        s_brw.inc(16)

    for tt in range(NT):
        @block.scalar
        def _(sc, tt=tt):
            for lg in range(16):
                B = 16 * tt + lg
                sc.wait_ge(s_sm.h, 8 * (B + 1))
                if B >= 3:
                    sc.wait_ge(s_om.h, 8 * (B - 2))
                sc.activation(exp_sb[:, B % 3, :, :], rotS8[B % 2], AF.Exp, scale=ISD
                              ).then_inc(s_ex.h, 1)
                s_ex.inc()

    emit_stack(0)
    emit_stack(1)
    def emit_vint(tt):
        @block.vector
        def _(v, tt=tt):
            if tt == 0:
                v.wait_ge(s_gpe.h, 32)
                v.wait_ge(s_gp.h, 5)
            if tt >= 2:
                v.wait_ge(s_om.h, 128 * (tt - 1))
            for g in range(8):
                v.tensor_copy(
                    v_int[:, tt % 2, :, 4 * g:4 * (g + 1)],
                    vT[:, g, 128 * tt:128 * (tt + 1)].unsqueeze(2)
                    .broadcast_to([128, 128, 4])).then_inc(s_vi.h, 1)
                s_vi.inc()

    emit_vint(0)
    for sg in range(64):
        tt = sg // 8
        if sg % 8 == 4 and tt + 1 < NT:
            emit_vint(tt + 1)

        for bh in range(2):
            B = 2 * sg + bh

            @block.tensor
            def _(t, sg=sg, bh=bh, B=B):
                if bh == 0:
                    t.wait_ge(s_stk[sg % 2].h, 32 * (sg // 2 + 1))
                if B == 0:
                    t.wait_ge(s_qte.h, 16)
                if B == 1:
                    t.wait_ge(s_t1e.h, 2)
                if B >= 2:
                    t.wait_ge(s_ex.h, B - 1)
                for i in range(8):
                    mm = t.matmul(rotS8[B % 2][:, i, :], k_stk[:, sg % 2, 8 * bh + i, :],
                                  q_stk[:, sg % 2, 8 * bh + i, :], start=True, stop=True)
                    if i == 7:
                        mm.then_inc(s_sm.h, 8)
                        s_sm.inc(8)

            Bm = B - 1
            if Bm >= 0:
                @block.tensor
                def _(t, Bm=Bm):
                    ttm = Bm // 16
                    t.wait_ge(s_ex.h, Bm + 1)
                    if Bm % 16 == 0:
                        t.wait_ge(s_vi.h, 8 * (ttm + 1))
                    if Bm == 0:
                        t.wait_ge(s_sc0.h, p0c["SC0_ALL"])
                    if Bm >= 2:
                        t.wait_ge(s_nm.h, Bm - 1)
                    tloc = 16 * ((Bm // 2) % 8) + 8 * (Bm % 2)
                    for i in range(8):
                        mm = t.matmul(rotO8[Bm % 2][:, i, 0:33], exp_sb[:, Bm % 3, i, :],
                                      v_int[:, ttm % 2, tloc + i, :],
                                      start=True, stop=True)
                        if i == 7:
                            mm.then_inc(s_om.h, 8)
                            s_om.inc(8)

        @block.vector
        def _(v, sg=sg):
            for bh in range(2):
                B = 2 * sg + bh
                v.wait_ge(s_om.h, 8 * (B + 1))
                v.reciprocal(rcp8[:, B % 3, :], rotO8[B % 2][:, :, 32]).then_inc(s_rc.h, 1)
                s_rc.inc()
                v.wait_ge(s_rc.h, s_rc.v)
                if sg >= 2 and bh == 0:
                    v.wait_ge(s_mg[sg % 2].h, 32 * (sg // 2))
                v.tensor_tensor(attn_sb[:, sg % 2, 8 * bh:8 * (bh + 1), :],
                                rotO8[B % 2][:, :, 0:32],
                                rcp8[:, B % 3, :].unsqueeze(2).broadcast_to([128, 8, 32]),
                                ALU.mult).then_inc(s_nm.h, 1)
                s_nm.inc()

        if sg + 2 < 64:
            emit_stack(sg + 2)
        emit_merge(sg)

    @block.tensor
    def _(t):
        Bm = 127
        t.wait_ge(s_ex.h, 128)
        t.wait_ge(s_vi.h, 64)
        t.wait_ge(s_nm.h, Bm - 1)
        tloc = 16 * ((Bm // 2) % 8) + 8 * (Bm % 2)
        for i in range(8):
            mm = t.matmul(rotO8[Bm % 2][:, i, 0:33], exp_sb[:, Bm % 3, i, :],
                          v_int[:, (Bm // 16) % 2, tloc + i, :],
                          start=True, stop=True)
            if i == 7:
                mm.then_inc(s_om.h, 8)
                s_om.inc(8)

    es2.close()
    es3.close()
    es4 = contextlib.ExitStack()
    sb4 = lambda n, s, d: es4.enter_context(nc.sbuf_tensor(n, s, d))
    wT = sb4("wT", [128, 4, 16, 512], BF16)
    atile = sb4("atile", [128, 2, 512], F32)
    xtile = sb4("xtile", [128, 2, 512], F32)
    beffT2 = sb4("beffT2", [17, 2, 512], BF16)

    wfT_scr_v = wfT_scr.rearrange("(c p) o -> p c o", p=128)
    for h in range(1, 16):
        @block.scalar
        def _(sc, h=h):
            if h == 1:
                sc.wait_ge(s_om.h, 1024)
                sc.wait_ge(s_nm.h, 128)
                sc.wait_ge(s_vi.h, 64)
                sc.wait_ge(s_mg[0].h, s_mg[0].v)
                sc.wait_ge(s_mg[1].h, s_mg[1].v)
                sc.wait_ge(s_ccf.h, 1)
            if h >= 5:
                sc.wait_ge(s_fm.h, 8 * ((h - 4) // 2 + 1))
            oc, hh = h // 2, h % 2
            sc.dma_start(out=wT[:, (h - 1) % 4, :, :],
                         in_=wfT_scr_v[:, 16 * hh:16 * (hh + 1),
                                       512 * oc:512 * (oc + 1)]
                         ).then_inc(s_wt4[(h - 1) % 4].h, 16)
            s_wt4[(h - 1) % 4].inc(16)

    for oc in range(8):
        @block.sync
        def _(sync, oc=oc):
            if oc == 0:
                sync.wait_ge(s_ex.h, 128)
                sync.wait_ge(s_om.h, 1024)
                sync.wait_ge(s_nm.h, 128)
                sync.wait_ge(s_vi.h, 64)
                sync.wait_ge(s_mg[0].h, s_mg[0].v)
                sync.wait_ge(s_mg[1].h, s_mg[1].v)
                sync.wait_ge(s_ccf.h, 1)
            if oc >= 2:
                sync.wait_ge(s_fm.h, 8 * (oc - 1))
            sync.dma_start(out=beffT2[:, oc % 2, :], in_=beffT_scr[:, oc, :]
                           ).then_inc(s_bti.h, 16)
            s_bti.inc(16)

        for tt in range(NT):
            i = oc * NT + tt
            bnk = pmb[i % 2]

            @block.sync
            def _(sync, oc=oc, tt=tt, i=i):
                if i == 0:
                    sync.wait_ge(s_mg[0].h, s_mg[0].v)
                    sync.wait_ge(s_mg[1].h, s_mg[1].v)
                if i >= 2:
                    sync.wait_ge(s_out[i % 2].h, 16 * (i // 2))
                sync.dma_start(out=atile[:, i % 2, :],
                               in_=attn_scr[tt, :, 16 * oc:16 * (oc + 1), :]
                               .rearrange("p a b -> p (a b)")).then_inc(s_ar[i % 2].h, 16)
                s_ar[i % 2].inc(16)
                if i >= 2:
                    sync.wait_ge(s_a2.h, i - 1)
                sync.dma_start(out=xtile[:, i % 2, :],
                               in_=xsr[tt][:, 512 * oc:512 * (oc + 1)]).then_inc(s_xr[i % 2].h, 16)
                s_xr[i % 2].inc(16)

            @block.tensor
            def _(t, oc=oc, tt=tt, i=i, bnk=bnk):
                if tt == 0:
                    for h in (2 * oc, 2 * oc + 1):
                        if h == 0:
                            t.wait_ge(s_brw.h, 32)
                        else:
                            t.wait_ge(s_wt4[(h - 1) % 4].h, 16 * ((h - 1) // 4 + 1))
                if i == 0:
                    t.wait_ge(s_ex.h, 128)
                    t.wait_ge(s_nm.h, 128)
                if i >= 2:
                    t.wait_ge(s_a1.h, i - 1)
                for c in range(EC):
                    h_used = 2 * oc + c // 16
                    rhs = (wTboot[:, c, :] if h_used == 0
                           else wT[:, (h_used - 1) % 4, c % 16, :])
                    t.matmul(bnk[:], xnT[:, c, 128 * tt:128 * (tt + 1)],
                             rhs, start=(c == 0), stop=False)
                if tt == 0:
                    t.wait_ge(s_bti.h, 16 * (oc + 1))
                t.matmul(bnk[:], t1T[:, 128 * tt:128 * (tt + 1)], beffT2[:, oc % 2, :],
                         start=False, stop=True).then_inc(s_fm.h, 1)
                s_fm.inc()

            @block.vector
            def _(v, i=i, bnk=bnk):
                v.wait_ge(s_fm.h, i + 1)
                v.wait_ge(s_ar[i % 2].h, 16 * (i // 2 + 1))
                if i >= 2:
                    v.wait_ge(s_a2.h, i - 1)
                v.tensor_tensor(atile[:, i % 2, :], bnk[:], atile[:, i % 2, :],
                                ALU.add).then_inc(s_a1.h, 1)
                s_a1.inc()

            @block.gpsimd
            def _(gp, oc=oc, tt=tt, i=i):
                gp.wait_ge(s_a1.h, i + 1)
                gp.wait_ge(s_xr[i % 2].h, 16 * (i // 2 + 1))
                gp.tensor_tensor(atile[:, i % 2, :], atile[:, i % 2, :],
                                 xtile[:, i % 2, :], ALU.add).then_inc(s_a2.h, 1)
                s_a2.inc()
                gp.wait_ge(s_a2.h, i + 1)
                gp.dma_start(out=outr[tt][:, 512 * oc:512 * (oc + 1)],
                             in_=atile[:, i % 2, :]).then_inc(s_out[i % 2].h, 16)
                s_out[i % 2].inc(16)

    @block.scalar
    def _(sc):
        for s in (s_x + s_ktw + s_qtw + s_cbw + s_wl + s_wt4):
            sc.wait_ge(s.h, s.v)

    @block.sync
    def _(sync):
        for s in ([s_small, s_lab, s_bti, s_cck, s_ccv, s_ccq, s_ccf]
                  + s_lk + s_lv + s_lq + s_lf + s_sk + s_sv + s_sq + s_sf
                  + s_stk + s_mg + s_ar + s_xr + s_out):
            sync.wait_ge(s.h, s.v)

    @block.gpsimd
    def _(gp):
        gp.wait_ge(s_gpd.h, s_gpd.v)
        gp.wait_ge(s_fbc.h, s_fbc.v)
        gp.wait_ge(s_bias.h, s_bias.v)

    es4.close()
    es_boot.close()
    es2a.close()
    es.close()
    return nc


def kernel(**inputs):
    x = np.asarray(inputs["L0_attn_x"], dtype=np.float32).reshape(2 * 4096, E)
    shards = [np.ascontiguousarray(x[i * T:(i + 1) * T]) for i in range(NCORES)]
    wqf = np.asarray(inputs["L0_attn_Wq"], dtype=np.float32).reshape(E, 128, 32)
    wkf = np.asarray(inputs["L0_attn_Wk"], dtype=np.float32).reshape(E, 128, 8)
    wvf = np.asarray(inputs["L0_attn_Wv"], dtype=np.float32).reshape(E, 128, 8)
    wfT = np.asarray(inputs["L0_ffn_weight"], dtype=np.float32).reshape(E, E).T
    base = {
        "bq": np.ascontiguousarray(np.asarray(inputs["L0_attn_bq"], dtype=np.float32)),
        "bk": np.ascontiguousarray(np.asarray(inputs["L0_attn_bk"], dtype=np.float32)),
        "bv": np.ascontiguousarray(np.asarray(inputs["L0_attn_bv"], dtype=np.float32)),
        "fb": np.ascontiguousarray(np.asarray(inputs["L0_ffn_bias"], dtype=np.float32)).reshape(1, E),
        "la": np.ascontiguousarray(np.asarray(inputs["L0_ffn_lora_A"], dtype=np.float32)),
        "lbw": np.ascontiguousarray(np.asarray(inputs["L0_ffn_lora_B"], dtype=np.float32)),
        "lsi": np.ascontiguousarray(np.asarray(inputs["L0_ffn_lora_start_indices"], dtype=np.int32)).reshape(1, 4),
    }
    in_maps = []
    for core in range(NCORES):
        m = dict(base)
        m["xs"] = shards[core]
        m["wkm"] = np.ascontiguousarray(wkf[512 * core:512 * (core + 1)])
        m["wvm"] = np.ascontiguousarray(wvf[512 * core:512 * (core + 1)])
        m["wqm"] = np.ascontiguousarray(wqf[512 * core:512 * (core + 1)])
        m["wfm"] = np.ascontiguousarray(wfT[512 * core:512 * (core + 1)])
        m["bidx"] = np.array([[core // 4]], dtype=np.int32)
        in_maps.append(m)

    nc = build_program()
    res = run_bass_kernel_spmd(nc, in_maps, core_ids=list(range(NCORES)))
    outs = [np.asarray(r["out"]) for r in res.results]
    return np.concatenate(outs, axis=0).reshape(2, 4096, E).astype(np.float32)


# version 32
# speedup vs baseline: 1.0094x; 1.0077x over previous
import contextlib
import math

import numpy as np
import concourse.bass as bass
import concourse.mybir as mybir
from concourse.bass_utils import run_bass_kernel_spmd

F32 = mybir.dt.float32
BF16 = mybir.dt.bfloat16
I32 = mybir.dt.int32
AF = mybir.ActivationFunctionType
ALU = mybir.AluOpType
AX = mybir.AxisListType

NCORES = 8
T = 1024
NT = 8
E = 4096
EC = 32
ISD = float(1.0 / math.sqrt(128.0))


class Sem:
    def __init__(self, h):
        self.h = h
        self.v = 0

    def inc(self, n=1):
        self.v += n
        return self.v


def build_program():
    nc = bass.Bass("TRN2", target_bir_lowering=False)

    xs = nc.dram_tensor("xs", [T, E], F32, kind="ExternalInput").ap()
    wkm = nc.dram_tensor("wkm", [512, 128, 8], F32, kind="ExternalInput").ap()
    wvm = nc.dram_tensor("wvm", [512, 128, 8], F32, kind="ExternalInput").ap()
    wqm = nc.dram_tensor("wqm", [512, 128, 32], F32, kind="ExternalInput").ap()
    wfm = nc.dram_tensor("wfm", [512, E], F32, kind="ExternalInput").ap()
    bq = nc.dram_tensor("bq", [128, 32], F32, kind="ExternalInput").ap()
    bk = nc.dram_tensor("bk", [128, 8], F32, kind="ExternalInput").ap()
    bv = nc.dram_tensor("bv", [128, 8], F32, kind="ExternalInput").ap()
    fb = nc.dram_tensor("fb", [1, E], F32, kind="ExternalInput").ap()
    la = nc.dram_tensor("la", [3, 16, E], F32, kind="ExternalInput").ap()
    lbw = nc.dram_tensor("lbw", [3, E, 16], F32, kind="ExternalInput").ap()
    lsi = nc.dram_tensor("lsi", [1, 4], I32, kind="ExternalInput").ap()
    bidx = nc.dram_tensor("bidx", [1, 1], I32, kind="ExternalInput").ap()
    out = nc.dram_tensor("out", [T, E], F32, kind="ExternalOutput").ap()

    wkstg = nc.dram_tensor("wkstg", [1024, 512], BF16, kind="Internal").ap()
    wvstg = nc.dram_tensor("wvstg", [1024, 512], BF16, kind="Internal").ap()
    wqstg = nc.dram_tensor("wqstg", [1024, 512], BF16, kind="Internal").ap()
    wfstg = nc.dram_tensor("wfstg", [512, E], BF16, kind="Internal").ap()
    wk_scr = nc.dram_tensor("wk_scr", [8192, 512], BF16, kind="Internal",
                            addr_space="Shared").ap()
    wv_scr = nc.dram_tensor("wv_scr", [8192, 512], BF16, kind="Internal",
                            addr_space="Shared").ap()
    wq_scr = nc.dram_tensor("wq_scr", [8192, 512], BF16, kind="Internal",
                            addr_space="Shared").ap()
    wfT_scr = nc.dram_tensor("wfT_scr", [E, E], BF16, kind="Internal",
                             addr_space="Shared").ap()
    qtok_scr = nc.dram_tensor("qtok_scr", [NT, 128, 9, 128], BF16, kind="Internal").ap()
    ktok_scr = nc.dram_tensor("ktok_scr", [NT, 128, 8, 128], BF16, kind="Internal").ap()
    attn_scr = nc.dram_tensor("attn_scr", [NT, 128, 128, 32], F32, kind="Internal").ap()
    brow_scr = nc.dram_tensor("brow_scr", [3, 8, 128], BF16, kind="Internal").ap()
    beffT_scr = nc.dram_tensor("beffT_scr", [17, 8, 512], BF16, kind="Internal").ap()

    es = contextlib.ExitStack()
    sb = lambda n, s, d: es.enter_context(nc.sbuf_tensor(n, s, d))
    pst = lambda n, s, d: es.enter_context(nc.psum_tensor(n, s, d))
    mks = lambda n: Sem(es.enter_context(nc.semaphore(n)))

    ptpA = pst("ptpA", [128, 4, 128], BF16)
    ptpB = pst("ptpB", [128, 4, 128], BF16)
    pmAB = pst("pmAB", [128, 1024], F32)
    psAB = pst("psAB", [128, 1024], F32)
    poAB = pst("poAB", [128, 1024], F32)
    pmA = pmAB[:, 0:512]
    pmB = pmAB[:, 512:1024]
    psA = psAB[:, 0:512].rearrange("p (a b) -> p a b", b=128)
    psB = psAB[:, 512:1024].rearrange("p (a b) -> p a b", b=128)
    poA = poAB[:, 0:512].rearrange("p (a b) -> p a b", b=128)
    poB = poAB[:, 512:1024].rearrange("p (a b) -> p a b", b=128)
    psAb = psAB[:, 0:512].bitcast(BF16).rearrange("p (a b) -> p a b", b=128)
    psBb = psAB[:, 512:1024].bitcast(BF16).rearrange("p (a b) -> p a b", b=128)
    poAb = poAB[:, 0:512].bitcast(BF16).rearrange("p (a b) -> p a b", b=128)
    poBb = poAB[:, 512:1024].bitcast(BF16).rearrange("p (a b) -> p a b", b=128)
    ptp4 = (ptpA[:], ptpB[:], psAb[:, 0:4, :], psBb[:, 0:4, :])
    rotS8 = (psAB[:].rearrange("p (a b) -> p a b", b=128),
             pmAB[:].rearrange("p (a b) -> p a b", b=128))
    rotO8 = (poAB[:, 0:264].rearrange("p (a b) -> p a b", b=33),
             poAB[:, 512:776].rearrange("p (a b) -> p a b", b=33))
    pmb = (pmA, pmB)
    ptp = (ptpA, ptpB)

    xnT = sb("xnT", [128, EC, T], BF16)
    xnt2 = sb("xnt2", [128, 2, E], BF16)
    ident = sb("ident", [128, 128], BF16)
    lsi_s = sb("lsi_s", [1, 4], I32)
    lsi_f = sb("lsi_f", [1, 4], F32)
    bidx_s = sb("bidx_s", [1, 1], I32)
    bidx_f = sb("bidx_f", [1, 1], F32)
    lsi_rep = sb("lsi_rep", [128, 4], F32)
    bidx_rep = sb("bidx_rep", [128, 1], F32)
    cge = sb("cge", [128, 4], F32)
    seg = sb("seg", [128, 1], F32)
    usev = sb("usev", [128, 1], F32)
    m0 = sb("m0", [128, 1], F32)
    m1 = sb("m1", [128, 1], F32)
    m2 = sb("m2", [128, 1], F32)
    ta_ = sb("ta_", [128, 1], F32)
    tb_ = sb("tb_", [128, 1], F32)
    tc_ = sb("tc_", [128, 1], F32)
    td_ = sb("td_", [128, 1], F32)
    bq_s = sb("bq_s", [128, 32], F32)
    bqsum = sb("bqsum", [128, 8], F32)
    bk_s = sb("bk_s", [128, 8], F32)
    bv_s = sb("bv_s", [128, 8], F32)
    aeffT = sb("aeffT", [128, EC, 16], BF16)
    t1T = sb("t1T", [17, T], BF16)
    qmax = sb("qmax", [128, NT, 8], F32)
    kmax = sb("kmax", [128, NT, 8], F32)
    cb8 = sb("cb8", [128, NT], F32)
    cbprod = sb("cbprod", [128, 8], F32)
    ones512 = sb("ones512", [1, 512], BF16)
    bns = sb("bns", [128, 2, 8, 6], F32)
    mv = sb("mv", [128, 2, 2], F32)
    negmu = sb("negmu", [128, 2, 1], F32)
    lnv = sb("lnv", [128, 2, 1], F32)
    rstd = sb("rstd", [128, 2, 1], F32)
    lnt = sb("lnt", [128, 2, 1], F32)

    s_small = mks("s_small"); s_lab = mks("s_lab"); s_gpd = mks("s_gpd"); s_gp = mks("s_gp")
    s_v0 = mks("s_v0"); s_t0 = mks("s_t0"); s_sc0 = mks("s_sc0")
    s_x = [mks("s_x0"), mks("s_x1"), mks("s_x2")]
    s_ln = mks("s_ln"); s_rs = mks("s_rs"); s_ap = mks("s_ap")
    s_tr = mks("s_tr"); s_xe = mks("s_xe")
    s_lk = [mks("s_lk0"), mks("s_lk1")]
    s_lv = [mks("s_lv0"), mks("s_lv1")]
    s_lq = [mks("s_lq0"), mks("s_lq1")]
    s_lf = [mks("s_lf0"), mks("s_lf1")]
    s_wl = [mks("s_wl0"), mks("s_wl1")]
    s_gpm = mks("s_gpm")
    s_gpe = mks("s_gpe")
    s_lacc = mks("s_lacc")
    s_wt4 = [mks("s_wt40"), mks("s_wt41"), mks("s_wt42"), mks("s_wt43")]
    s_sk = [mks("s_sk0"), mks("s_sk1")]
    s_sv = [mks("s_sv0"), mks("s_sv1")]
    s_sq = [mks("s_sq0"), mks("s_sq1")]
    s_sf = [mks("s_sf0"), mks("s_sf1")]
    s_ck = mks("s_ck"); s_cv = mks("s_cv"); s_js = mks("s_js"); s_brw = mks("s_brw")
    s_bt = mks("s_bt"); s_bias = mks("s_bias"); s_fbc = mks("s_fbc")
    s_cf = mks("s_cf")
    s_cck = mks("s_cck"); s_ccv = mks("s_ccv")
    s_ccq = mks("s_ccq"); s_ccf = mks("s_ccf")
    s_ktr = mks("s_ktr"); s_qtr = mks("s_qtr")
    s_kte = mks("s_kte"); s_qte = mks("s_qte")
    s_ktw = [mks("s_ktw0"), mks("s_ktw1")]
    s_qtw = [mks("s_qtw0"), mks("s_qtw1")]
    s_kmx = mks("s_kmx"); s_qmx = mks("s_qmx")
    s_cb = mks("s_cb"); s_cbw = [mks("s_cbw0"), mks("s_cbw1")]
    s_t1 = mks("s_t1"); s_t1e = mks("s_t1e"); s_bto = mks("s_bto"); s_bti = mks("s_bti")
    s_stk = [mks("s_stk0"), mks("s_stk1")]; s_vi = mks("s_vi")
    s_sm = mks("s_sm"); s_ex = mks("s_ex"); s_om = mks("s_om")
    s_rc = mks("s_rc"); s_nm = mks("s_nm"); s_mg = [mks("s_mg0"), mks("s_mg1")]
    s_wc = mks("s_wc"); s_wt = mks("s_wt"); s_we = mks("s_we"); s_fm = mks("s_fm")
    s_ar = [mks("s_ar0"), mks("s_ar1")]
    s_xr = [mks("s_xr0"), mks("s_xr1")]
    s_a1 = mks("s_a1"); s_a2 = mks("s_a2")
    s_out = [mks("s_out0"), mks("s_out1")]

    wproj = xnt2[:].rearrange("p b (c d) -> p b c d", d=128)
    wprj = wproj
    xsr = xs.rearrange("(n p) e -> n p e", p=128)
    outr = out.rearrange("(n p) e -> n p e", p=128)

    block = es.enter_context(nc.Block())

    es2a = contextlib.ExitStack()
    sb2a = lambda n, s, d: es2a.enter_context(nc.sbuf_tensor(n, s, d))
    wqraw = sb2a("wqraw", [128, 2, 2048], F32)
    wkc = wqraw[:, 0, :].rearrange("p (b c d) -> p b c d", b=2, c=128, d=8)
    wvc = wqraw[:, 1, :].rearrange("p (b c d) -> p b c d", b=2, c=128, d=8)
    wk_st = sb2a("wk_st", [128, 2, 8, 128], BF16)
    wv_st = sb2a("wv_st", [128, 2, 8, 128], BF16)
    bias_rows = sb2a("bias_rows", [1, 3, 8, 128], BF16)

    es_xt = contextlib.ExitStack()
    xt = es_xt.enter_context(nc.sbuf_tensor("xt", [128, 3, E], F32))

    es0 = contextlib.ExitStack()
    sb0 = lambda n, s, d: es0.enter_context(nc.sbuf_tensor(n, s, d))
    la_a = sb0("la_a", [16, E], BF16)
    iv = sb0("iv", [128, 128], I32)
    onesf = sb0("onesf", [1, 128], F32)
    bqt1 = sb0("bqt1", [128, 8], F32)
    bqt2 = sb0("bqt2", [128, 8], F32)
    ones_long = sb0("ones_long", [1, 1024], BF16)
    aeffA = sb0("aeffA", [16, E], BF16)
    lb_a = sb0("lb_a", [128, EC, 16], BF16)
    beffA = sb0("beffA", [128, EC, 16], BF16)
    beffT = sb0("beffT", [17, 8, 512], BF16)
    bqb = sb0("bqb", [128, 3, 8], BF16)
    brow_sb = sb0("brow_sb", [8, 3, 128], BF16)

    @block.sync
    def _(sync):
        for dst, src in ((lsi_s, lsi), (bidx_s, bidx), (bq_s, bq), (bk_s, bk),
                         (bv_s, bv)):
            sync.dma_start(out=dst[:], in_=src[:]).then_inc(s_small.h, 16)
            s_small.inc(16)

    @block.gpsimd
    def _(gp):
        gp.iota(iv[:], pattern=[[1, 128]], base=0, channel_multiplier=-1).then_inc(s_gp.h, 1)
        s_gp.inc()
        gp.memset(onesf[:], 1.0).then_inc(s_gp.h, 1)
        s_gp.inc()
        gp.memset(ones_long[:], 1.0).then_inc(s_gp.h, 1)
        s_gp.inc()
        gp.memset(ones512[:], 1.0).then_inc(s_gp.h, 1)
        s_gp.inc()

    wkmr = wkm.rearrange("(cl p) d g -> cl p d g", p=128)
    wvmr = wvm.rearrange("(cl p) d g -> cl p d g", p=128)
    wqmr = wqm.rearrange("(cl p) d h -> cl p d h", p=128)
    wfmr = wfm.rearrange("(mc p) o -> mc p o", p=128)
    wkstg_v = wkstg.rearrange("(g p) (cl d) -> g p cl d", p=128, d=128)
    wvstg_v = wvstg.rearrange("(g p) (cl d) -> g p cl d", p=128, d=128)
    wqstg_v = wqstg.rearrange("(g p) (cl d) -> g p cl d", p=128, d=128)

    @block.sync
    def _(sync):
        for st in ("k", "v"):
            lvw, stg_v, st_b, s_l, s_s, s_c = {
                "k": (wkmr, wkstg_v, wk_st, s_lk, s_sk, s_ck),
                "v": (wvmr, wvstg_v, wv_st, s_lv, s_sv, s_cv),
            }[st]
            for cl in range(4):
                if cl >= 2:
                    sync.wait_ge(s_c.h, cl - 1)
                sync.dma_start(out=(wkc if st == "k" else wvc)[:, cl % 2, :, :],
                               in_=lvw[cl]).then_inc(s_l[cl % 2].h, 16)
                s_l[cl % 2].inc(16)
                if cl >= 1:
                    sync.wait_ge(s_c.h, cl)
                    sync.dma_start(out=stg_v[:, :, cl - 1, :].transpose([1, 0, 2]),
                                   in_=st_b[:, (cl - 1) % 2, :, :]
                                   ).then_inc(s_s[(cl - 1) % 2].h, 16)
                    s_s[(cl - 1) % 2].inc(16)
            sync.wait_ge(s_c.h, 4)
            sync.dma_start(out=stg_v[:, :, 3, :].transpose([1, 0, 2]),
                           in_=st_b[:, 1, :, :]).then_inc(s_s[1].h, 16)
            s_s[1].inc(16)

    @block.sync
    def _(sync):
        sync.wait_ge(s_brw.h, 16)
        sync.dma_start(out=bias_rows[:], in_=brow_scr.unsqueeze(0)
                       ).then_inc(s_bias.h, 16)
        s_bias.inc(16)

    def emit_la_load_gp(gp, a):
        if a > 0:
            gp.wait_ge(s_lacc.h, 2 * a)
        gp.dma_start(out=la_a[:], in_=la[a]).then_inc(s_lab.h, 16)
        s_lab.inc(16)
        gp.dma_start(out=lb_a[:],
                     in_=lbw[a].rearrange("(c p) r -> p c r", p=128)
                     ).then_inc(s_lab.h, 16)
        s_lab.inc(16)

    @block.gpsimd
    def _(gp):
        emit_la_load_gp(gp, 0)
        for cl in range(4):
            gp.wait_ge(s_lk[cl % 2].h, 16 * (cl // 2 + 1))
            if cl >= 2:
                gp.wait_ge(s_sk[cl % 2].h, 16 * (cl // 2))
            gp.tensor_copy(wk_st[:, cl % 2, :, :],
                           wkc[:, cl % 2, :, :].transpose([0, 2, 1])).then_inc(s_ck.h, 1)
            s_ck.inc()
        gp.wait_ge(s_sk[0].h, 32)
        gp.wait_ge(s_sk[1].h, 32)
        gp.collective_compute("AllGather", ALU.bypass,
                              replica_groups=[list(range(8))],
                              ins=[wkstg], outs=[wk_scr]).then_inc(s_cck.h, 1)
        s_cck.inc()
        emit_la_load_gp(gp, 1)
        for cl in range(4):
            gp.wait_ge(s_lv[cl % 2].h, 16 * (cl // 2 + 1))
            if cl >= 2:
                gp.wait_ge(s_sv[cl % 2].h, 16 * (cl // 2))
            gp.tensor_copy(wv_st[:, cl % 2, :, :],
                           wvc[:, cl % 2, :, :].transpose([0, 2, 1])).then_inc(s_cv.h, 1)
            s_cv.inc()
        gp.wait_ge(s_sv[0].h, 32)
        gp.wait_ge(s_sv[1].h, 32)
        gp.collective_compute("AllGather", ALU.bypass,
                              replica_groups=[list(range(8))],
                              ins=[wvstg], outs=[wv_scr]).then_inc(s_ccv.h, 1)
        s_ccv.inc()
        emit_la_load_gp(gp, 2)

    @block.vector
    def _(v):
        v.wait_ge(s_gp.h, 1)
        v.tensor_scalar(ident[:], iv[:], 0.0, None, ALU.is_equal).then_inc(s_v0.h, 1)
        s_v0.inc()
        v.wait_ge(s_small.h, 80)
        v.tensor_copy(lsi_f[:], lsi_s[:]).then_inc(s_v0.h, 1)
        s_v0.inc()
        v.tensor_copy(bidx_f[:], bidx_s[:]).then_inc(s_v0.h, 1)
        s_v0.inc()

    @block.tensor
    def _(t):
        t.wait_ge(s_v0.h, 3)
        t.wait_ge(s_gp.h, 2)
        t.matmul(poA[:, 0, 0:4], onesf[:], lsi_f[:], start=True, stop=True).then_inc(s_t0.h, 1)
        s_t0.inc()
        t.matmul(poA[:, 1, 0:1], onesf[:], bidx_f[:], start=True, stop=True).then_inc(s_t0.h, 1)
        s_t0.inc()

    @block.scalar
    def _(sc):
        sc.wait_ge(s_t0.h, 2)
        sc.copy(lsi_rep[:], poA[:, 0, 0:4]).then_inc(s_sc0.h, 1)
        s_sc0.inc()
        sc.copy(bidx_rep[:], poA[:, 1, 0:1]).then_inc(s_sc0.h, 1)
        s_sc0.inc()

    @block.vector
    def _(v):
        def step(fn):
            v.wait_ge(s_v0.h, s_v0.v)
            fn().then_inc(s_v0.h, 1)
            s_v0.inc()
        v.wait_ge(s_sc0.h, 2)
        step(lambda: v.tensor_scalar(cge[:], lsi_rep[:], bidx_rep[:, 0:1], None, ALU.is_gt))
        step(lambda: v.tensor_scalar(ta_[:], cge[:, 0:1], -1.0, 1.0, ALU.mult, ALU.add))
        step(lambda: v.tensor_scalar(tb_[:], ta_[:], -2.0, None, ALU.add))
        step(lambda: v.tensor_tensor(tc_[:], tb_[:], cge[:, 1:2], ALU.mult))
        step(lambda: v.tensor_scalar(tc_[:], tc_[:], 2.0, None, ALU.add))
        step(lambda: v.tensor_scalar(td_[:], cge[:, 3:4], -1.0, 4.0, ALU.mult, ALU.add))
        step(lambda: v.tensor_tensor(ta_[:], tc_[:], td_[:], ALU.subtract))
        step(lambda: v.tensor_tensor(tb_[:], ta_[:], cge[:, 2:3], ALU.mult))
        step(lambda: v.tensor_tensor(tc_[:], tb_[:], td_[:], ALU.add))
        step(lambda: v.tensor_scalar(seg[:], tc_[:], -1.0, None, ALU.add))
        step(lambda: v.tensor_scalar(usev[:], seg[:], 0.0, None, ALU.is_gt))
        step(lambda: v.tensor_scalar(ta_[:], seg[:], 1.0, None, ALU.is_le))
        step(lambda: v.tensor_tensor(m0[:], ta_[:], usev[:], ALU.mult))
        step(lambda: v.tensor_scalar(ta_[:], seg[:], 2.0, None, ALU.is_equal))
        step(lambda: v.tensor_tensor(m1[:], ta_[:], usev[:], ALU.mult))
        step(lambda: v.tensor_scalar(ta_[:], seg[:], 3.0, None, ALU.is_ge))
        step(lambda: v.tensor_tensor(m2[:], ta_[:], usev[:], ALU.mult))
        bq4 = bq_s[:].rearrange("p (g j) -> p g j", j=4)
        step(lambda: v.tensor_tensor(bqt1[:], bq4[:, :, 0], bq4[:, :, 1], ALU.add))
        step(lambda: v.tensor_tensor(bqt2[:], bq4[:, :, 2], bq4[:, :, 3], ALU.add))
        step(lambda: v.tensor_tensor(bqsum[:], bqt1[:], bqt2[:], ALU.add))
        step(lambda: v.tensor_copy(bqb[:, 0, :], bqsum[:]))
        step(lambda: v.tensor_copy(bqb[:, 1, :], bk_s[:]))
        step(lambda: v.tensor_copy(bqb[:, 2, :], bv_s[:]))

    masks = (m0, m1, m2)

    @block.tensor
    def _(t):
        t.wait_ge(s_v0.h, 26)
        for j in range(3):
            t.transpose(poBb[0:8, j, :], bqb[:, j, :], ident[:]).then_inc(s_bt.h, 1)
            s_bt.inc()

    def emit_brow_evac():
        @block.scalar
        def _(sc):
            sc.wait_ge(s_bt.h, 3)
            sc.copy(brow_sb[:], poBb[0:8, 0:3, :]).then_inc(s_bt.h, 1)
            s_bt.inc()
            sc.wait_ge(s_bt.h, 4)
            sc.dma_start(out=brow_scr.transpose([1, 0, 2]), in_=brow_sb[:]
                         ).then_inc(s_brw.h, 16)
            s_brw.inc(16)

    pob = (poAb, poBb)

    def emit_combine(a):
        @block.vector
        def _(v, a=a):
            v.wait_ge(s_lab.h, 32 * (a + 1))
            if a == 0:
                v.wait_ge(s_v0.h, 23)
                v.tensor_scalar(aeffA[:], la_a[:], masks[a][0:16, 0:1], None,
                                ALU.mult).then_inc(s_lacc.h, 1)
            else:
                v.wait_ge(s_lacc.h, 2 * a)
                v.scalar_tensor_tensor(aeffA[:], la_a[:], masks[a][0:16, 0:1],
                                       aeffA[:], ALU.mult, ALU.add).then_inc(s_lacc.h, 1)
            s_lacc.inc()
            v.wait_ge(s_lacc.h, s_lacc.v)
            if a == 0:
                v.tensor_scalar(beffA[:], lb_a[:], masks[a][:, 0:1], None,
                                ALU.mult).then_inc(s_lacc.h, 1)
            else:
                v.scalar_tensor_tensor(beffA[:], lb_a[:], masks[a][:, 0:1],
                                       beffA[:], ALU.mult, ALU.add).then_inc(s_lacc.h, 1)
            s_lacc.inc()

    p0c = {}

    def emit_p0_transposes():
        @block.tensor
        def _(t):
            t.wait_ge(s_lacc.h, 6)
            t.wait_ge(s_bt.h, 4)
            for c in range(EC):
                grp, slot = c // 4, c % 4
                if grp == 0:
                    t.wait_ge(s_sc0.h, 2)
                if grp >= 2:
                    t.wait_ge(s_sc0.h, 2 + (grp - 1))
                t.transpose(pob[grp % 2][:, slot, 0:16], aeffA[:, 128 * c:128 * (c + 1)],
                            ident[0:16, 0:16]).then_inc(s_t0.h, 1)
                s_t0.inc()

        @block.scalar
        def _(sc):
            for grp in range(8):
                sc.wait_ge(s_t0.h, 2 + 4 * (grp + 1))
                sc.copy(aeffT[:, 4 * grp:4 * (grp + 1), :], pob[grp % 2][:, 0:4, 0:16]
                        ).then_inc(s_sc0.h, 1)
                s_sc0.inc()
        p0c["SC0_AEFF"] = s_sc0.v
        p0c["T0_AEFF"] = s_t0.v

        @block.tensor
        def _(t):
            for c in range(EC):
                grp, slot = c // 4, c % 4
                if slot == 0:
                    t.wait_ge(s_sc0.h, p0c["SC0_AEFF"] - 1 + grp)
                t.transpose(pob[grp % 2][0:16, slot, :], beffA[:, c, :], ident[:]
                            ).then_inc(s_t0.h, 1)
                s_t0.inc()
        p0c["T0_ALL"] = s_t0.v

        @block.scalar
        def _(sc):
            for grp in range(8):
                sc.wait_ge(s_t0.h, p0c["T0_AEFF"] + 4 * (grp + 1))
                sc.copy(beffT[0:16, grp, :], pob[grp % 2][0:16, 0:4, :]).then_inc(s_sc0.h, 1)
                s_sc0.inc()
        p0c["SC0_ALL"] = s_sc0.v

    def emit_xload(i):
        @block.scalar
        def _(sc, i=i):
            if i >= 3:
                sc.wait_ge(s_ap.h, i - 2)
            hi = E if i < 2 else E // 2
            sc.dma_start(out=xt[:, i % 3, 0:hi], in_=xsr[i][:, 0:hi]
                         ).then_inc(s_x[i % 3].h, 16)
            s_x[i % 3].inc(16)

        if i >= 2:
            @block.sync
            def _(sync, i=i):
                if i >= 3:
                    sync.wait_ge(s_ap.h, i - 2)
                sync.dma_start(out=xt[:, i % 3, E // 2:E], in_=xsr[i][:, E // 2:E]
                               ).then_inc(s_x[i % 3].h, 16)
                s_x[i % 3].inc(16)

    emit_xload(0)
    emit_xload(1)
    emit_brow_evac()

    def emit_apply(i):
        @block.vector
        def _(v, i=i):
            v.wait_ge(s_rs.h, 2 * (i + 1))
            if i >= 2:
                v.wait_ge(s_tr.h, 32 * (i - 1))
            v.tensor_scalar(xnt2[:, i % 2, :], xt[:, i % 3, :], negmu[:, i % 2, :],
                            rstd[:, i % 2, :], ALU.add, ALU.mult).then_inc(s_ap.h, 1)
            s_ap.inc()

        @block.tensor
        def _(t, i=i):
            t.wait_ge(s_ap.h, i + 1)
            if i == 0:
                t.wait_ge(s_v0.h, 1)
            for c in range(EC):
                grp = c // 4
                gidx = 8 * i + grp
                if gidx >= 2:
                    t.wait_ge(s_xe.h, gidx - 1)
                t.transpose(ptp[gidx % 2][:, c % 4, :],
                            xnt2[:, i % 2, 128 * c:128 * (c + 1)], ident[:]
                            ).then_inc(s_tr.h, 1)
                s_tr.inc()

        @block.scalar
        def _(sc, i=i):
            for grp in range(8):
                gidx = 8 * i + grp
                sc.wait_ge(s_tr.h, 4 * (gidx + 1))
                sc.copy(xnT[:, 4 * grp:4 * (grp + 1), 128 * i:128 * (i + 1)],
                        ptp[gidx % 2][:, :, :]).then_inc(s_xe.h, 1)
                s_xe.inc()

    for i in range(NT):
        if i >= 1:
            emit_apply(i - 1)

        @block.vector
        def _(v, i=i):
            xw = [16, 16, 32, 48, 48, 64, 80, 80][i]
            v.wait_ge(s_x[i % 3].h, xw)
            for c in range(8):
                v.bn_stats(bns[:, i % 2, c, :], xt[:, i % 3, 512 * c:512 * (c + 1)]
                           ).then_inc(s_ln.h, 1)
                s_ln.inc()
            v.wait_ge(s_ln.h, s_ln.v)
            v.wait_ge(s_rs.h, 2 * i)
            v.bn_aggr(mv[:, i % 2, :], bns[:, i % 2, :, :]).then_inc(s_ln.h, 1)
            s_ln.inc()
            v.wait_ge(s_ln.h, s_ln.v)
            v.tensor_scalar(negmu[:, i % 2, :], mv[:, i % 2, 0:1], -1.0, None,
                            ALU.mult).then_inc(s_ln.h, 1)
            s_ln.inc()
            v.wait_ge(s_ln.h, s_ln.v)
            v.tensor_scalar(lnv[:, i % 2, :], mv[:, i % 2, 1:2], 1e-5, None,
                            ALU.add).then_inc(s_ln.h, 1)
            s_ln.inc()

        @block.scalar
        def _(sc, i=i):
            sc.wait_ge(s_ln.h, 11 * i + 11)
            sc.activation(lnt[:, i % 2, :], lnv[:, i % 2, :], AF.Ln).then_inc(s_rs.h, 1)
            s_rs.inc()
            sc.wait_ge(s_rs.h, s_rs.v)
            sc.activation(rstd[:, i % 2, :], lnt[:, i % 2, :], AF.Exp, scale=-0.5
                          ).then_inc(s_rs.h, 1)
            s_rs.inc()

        if i + 2 < NT:
            emit_xload(i + 2)
        if i in (1, 2, 3):
            emit_combine(i - 1)
        if i == 4:
            emit_p0_transposes()

    emit_apply(7)

    @block.gpsimd
    def _(gp):
        gp.wait_ge(s_sc0.h, p0c["SC0_ALL"])
        gp.wait_ge(s_t0.h, p0c["T0_ALL"])
        gp.dma_start(out=beffT[16:17, :, :], in_=fb.rearrange("a (b c) -> a b c", c=512)
                     ).then_inc(s_fbc.h, 16)
        s_fbc.inc(16)

    @block.scalar
    def _(sc):
        sc.wait_ge(s_fbc.h, 16)
        sc.dma_start(out=beffT_scr[:], in_=beffT[:]).then_inc(s_bto.h, 16)
        s_bto.inc(16)
        sc.wait_ge(s_gp.h, 4)
        sc.dma_start(out=t1T[16:17, :], in_=ones_long[0:1, 0:1024]).then_inc(s_gpd.h, 16)
        s_gpd.inc(16)

    es0.close()
    es_xt.close()
    es_boot = contextlib.ExitStack()
    wTboot = es_boot.enter_context(nc.sbuf_tensor("wTboot", [128, 16, 512], BF16))
    es3 = contextlib.ExitStack()
    sb3 = lambda n, s, d: es3.enter_context(nc.sbuf_tensor(n, s, d))
    vT = sb3("vT", [128, 8, T], BF16)
    q_stk = sb3("q_stk", [9, 2, 16, 128], BF16)
    k_stk = sb3("k_stk", [9, 2, 16, 128], BF16)
    v_int = sb3("v_int", [128, 2, 128, 33], BF16)
    exp_sb = sb3("exp_sb", [128, 3, 8, 128], BF16)
    attn_sb = sb3("attn_sb", [128, 2, 16, 32], F32)
    rcp8 = sb3("rcp8", [128, 3, 8], F32)

    es2 = contextlib.ExitStack()
    sb2 = lambda n, s, d: es2.enter_context(nc.sbuf_tensor(n, s, d))
    cbc = sb2("cbc", [128, 128], BF16)
    kpl = sb2("kpl", [128, 2, 512], BF16)
    qtk4 = sb2("qtk4", [128, 2, 4, 128], BF16)
    wq_st = sb2("wq_st", [128, 8, 128], BF16)
    tjA = sb2("tjA", [128, 8, 64], BF16)
    wfb = sb2("wfb", [128, 2048], BF16)

    @block.sync
    def _(sync):
        for cl in range(4):
            for hh in range(2):
                h = 2 * cl + hh
                if h == 0:
                    sync.wait_ge(s_ck.h, 4)
                    sync.wait_ge(s_cv.h, 4)
                if h >= 2:
                    sync.wait_ge(s_js.h, 3 * (h - 1))
                sync.dma_start(out=wqraw[:, h % 2, :]
                               .rearrange("p (d h4) -> p d h4", h4=32),
                               in_=wqmr[cl][:, 64 * hh:64 * (hh + 1), :]
                               ).then_inc(s_lq[h % 2].h, 16)
                s_lq[h % 2].inc(16)
            sync.wait_ge(s_js.h, 6 * (cl + 1))
            sync.dma_start(out=wqstg_v[:, :, cl, :].transpose([1, 0, 2]),
                           in_=wq_st[:, :, :]).then_inc(s_sq[cl % 2].h, 16)
            s_sq[cl % 2].inc(16)
        for pc in range(8):
            mc, hh = pc // 2, pc % 2
            if pc < 2:
                sync.wait_ge(s_js.h, 24)
            if pc >= 2:
                sync.wait_ge(s_cf.h, pc - 1)
            sync.dma_start(out=wqraw[:, pc % 2, :],
                           in_=wfmr[mc][:, 2048 * hh:2048 * (hh + 1)]
                           ).then_inc(s_lf[pc % 2].h, 16)
            s_lf[pc % 2].inc(16)
            if pc >= 1:
                w = pc - 1
                sync.wait_ge(s_cf.h, w + 1)
                sync.dma_start(out=wfstg[128 * (w // 2):128 * (w // 2 + 1),
                                         2048 * (w % 2):2048 * (w % 2 + 1)],
                               in_=wfb[:]).then_inc(s_sf[w % 2].h, 16)
                s_sf[w % 2].inc(16)
        w = 7
        sync.wait_ge(s_cf.h, 8)
        sync.dma_start(out=wfstg[128 * (w // 2):128 * (w // 2 + 1),
                                 2048 * (w % 2):2048 * (w % 2 + 1)],
                       in_=wfb[:]).then_inc(s_sf[w % 2].h, 16)
        s_sf[w % 2].inc(16)

    @block.gpsimd
    def _(gp):
        for h in range(8):
            cl, hh = h // 2, h % 2
            gp.wait_ge(s_lq[h % 2].h, 16 * (h // 2 + 1))
            if cl >= 1 and hh == 0:
                gp.wait_ge(s_sq[(cl - 1) % 2].h, 16 * ((cl - 1) // 2 + 1))
            wqh = wqraw[:, h % 2, :].rearrange("p (d g j) -> p d g j", g=8, j=4)
            a0 = wqh[:, :, :, 0].transpose([0, 2, 1])
            a1 = wqh[:, :, :, 1].transpose([0, 2, 1])
            a2 = wqh[:, :, :, 2].transpose([0, 2, 1])
            a3 = wqh[:, :, :, 3].transpose([0, 2, 1])
            outv = wq_st[:, :, 64 * hh:64 * (hh + 1)]
            if h == 0:
                gp.wait_ge(s_bto.h, 16)
                gp.wait_ge(s_gpd.h, 16)
                gp.wait_ge(s_lacc.h, 6)
                gp.wait_ge(s_brw.h, 16)
                gp.wait_ge(s_ap.h, 8)
            if h >= 1:
                gp.wait_ge(s_js.h, s_js.v)
            gp.tensor_tensor(tjA[:], a0, a1, ALU.add).then_inc(s_js.h, 1)
            s_js.inc()
            gp.tensor_tensor(outv, a2, a3, ALU.add).then_inc(s_js.h, 1)
            s_js.inc()
            gp.wait_ge(s_js.h, s_js.v)
            gp.tensor_tensor(outv, outv, tjA[:], ALU.add).then_inc(s_js.h, 1)
            s_js.inc()
        gp.wait_ge(s_sq[0].h, 32)
        gp.wait_ge(s_sq[1].h, 32)
        gp.collective_compute("AllGather", ALU.bypass,
                              replica_groups=[list(range(8))],
                              ins=[wqstg], outs=[wq_scr]).then_inc(s_ccq.h, 1)
        s_ccq.inc()
        for pc in range(8):
            gp.wait_ge(s_lf[pc % 2].h, 16 * (pc // 2 + 1))
            if pc == 0:
                gp.wait_ge(s_bto.h, 16)
                gp.wait_ge(s_gpd.h, 16)
                gp.wait_ge(s_lacc.h, 6)
                gp.wait_ge(s_brw.h, 16)
                gp.wait_ge(s_ap.h, 8)
            if pc >= 1:
                gp.wait_ge(s_sf[(pc - 1) % 2].h, 16 * ((pc - 1) // 2 + 1))
            gp.tensor_copy(wfb[:], wqraw[:, pc % 2, :]).then_inc(s_cf.h, 1)
            s_cf.inc()
        gp.wait_ge(s_sf[0].h, 64)
        gp.wait_ge(s_sf[1].h, 64)
        gp.collective_compute("AllGather", ALU.bypass,
                              replica_groups=[list(range(8))],
                              ins=[wfstg], outs=[wfT_scr]).then_inc(s_ccf.h, 1)
        s_ccf.inc()


    @block.scalar
    def _(sc):
        sc.wait_ge(s_gpd.h, 16)
        sc.wait_ge(s_bto.h, 16)
        sc.wait_ge(s_lacc.h, 6)
        sc.wait_ge(s_brw.h, 16)
        sc.wait_ge(s_ap.h, 8)
        for half in range(2):
            for hh in range(2):
                sc.dma_start(out=k_stk[8:9, half, 8 * hh:8 * (hh + 1), :]
                             .rearrange("a c d -> a (c d)"),
                             in_=t1T[16:17, 0:1024]).then_inc(s_gpd.h, 16)
                s_gpd.inc(16)

    @block.vector
    def _(v):
        v.wait_ge(s_bto.h, 16)
        v.wait_ge(s_gpd.h, 16)
        v.wait_ge(s_lacc.h, 6)
        v.wait_ge(s_brw.h, 16)
        v.memset(v_int[:, :, :, 32:33], 1.0).then_inc(s_gp.h, 1)
        s_gp.inc()

    wk_scr5 = wk_scr.rearrange("(r g p) (cl d) -> r g p cl d", g=8, p=128, d=128)
    wv_scr5 = wv_scr.rearrange("(r g p) (cl d) -> r g p cl d", g=8, p=128, d=128)
    wq_scr5 = wq_scr.rearrange("(r g p) (cl d) -> r g p cl d", g=8, p=128, d=128)

    def plane_load(j):
        pl = j // 8
        g = j % 8
        s_cc = (s_cck, s_ccv, s_ccq)[pl]
        scr5 = (wk_scr5, wv_scr5, wq_scr5)[pl]

        @block.scalar
        def _(sc, j=j, g=g, s_cc=s_cc, scr5=scr5):
            sc.wait_ge(s_cc.h, 1)
            if j < 2:
                sc.wait_ge(s_tr.h, 224 if j == 0 else 256)
            else:
                sc.wait_ge(s_gpm.h, 2 * (j - 1))
            sc.dma_start(out=wprj[:, j % 2].rearrange("p (r cl) d -> p r cl d", cl=4),
                         in_=scr5[:, g].transpose([1, 0, 2, 3])
                         ).then_inc(s_wl[j % 2].h, 16)
            s_wl[j % 2].inc(16)

    def plane_phase(kind):
        plane = {"k": 0, "v": 1, "q": 2}[kind]
        bidx_row = {"k": 1, "v": 2, "q": 0}[kind]
        s_tr_, s_te_, s_tw_, s_mx_ = {
            "k": (s_ktr, s_kte, s_ktw, s_kmx),
            "q": (s_qtr, s_qte, s_qtw, s_qmx),
            "v": (None, None, None, None),
        }[kind]
        tok_scr = {"k": ktok_scr, "q": qtok_scr, "v": None}[kind]
        mx = {"k": kmax, "q": qmax, "v": None}[kind]

        def emit_mm(m):
            g, th = m // 2, m % 2
            mg = 16 * plane + m
            j = 8 * plane + g
            bank = pmb[m % 2]

            @block.tensor
            def _(t, g=g, th=th, m=m, mg=mg, j=j, bank=bank):
                t.wait_ge(s_wl[j % 2].h, 16 * (j // 2 + 1))

                if kind == "k":
                    t.wait_ge(s_xe.h, 32 if th == 0 else 64)
                if mg >= 2:
                    t.wait_ge(s_gpe.h, mg - 1)
                if mg == 0:
                    t.wait_ge(s_bias.h, 16)
                    t.wait_ge(s_gp.h, 4)
                for c in range(EC):
                    t.matmul(bank[:], wprj[:, j % 2, c, :],
                             xnT[:, c, 512 * th:512 * (th + 1)],
                             start=(c == 0), stop=False)
                t.matmul(bank[:], bias_rows[0:1, bidx_row, g, :],
                         ones512[:], start=False, stop=True).then_inc(s_gpm.h, 1)
                s_gpm.inc()

        def emit_evac(m):
            g, th = m // 2, m % 2
            mg = 16 * plane + m
            bank = pmb[m % 2]

            @block.vector
            def _(v, g=g, th=th, m=m, mg=mg, bank=bank):
                v.wait_ge(s_gpm.h, mg + 1)
                if mg == 0:
                    v.wait_ge(s_bto.h, 16)
                    v.wait_ge(s_gpd.h, 16)
                    v.wait_ge(s_lacc.h, 6)
                    v.wait_ge(s_brw.h, 16)
                if kind == "v":
                    v.tensor_copy(vT[:, g, 512 * th:512 * (th + 1)], bank[:]
                                  ).then_inc(s_gpe.h, 1)
                    s_gpe.inc()
                else:
                    if m >= 2:
                        v.wait_ge(s_tr_.h, 4 * (m - 1))
                    v.tensor_copy(kpl[:, m % 2, :], bank[:]).then_inc(s_gpe.h, 1)
                    s_gpe.inc()

        def emit_tr(m):
            g, th = m // 2, m % 2
            mg = 16 * plane + m

            @block.tensor
            def _(t, m=m, mg=mg):
                t.wait_ge(s_gpe.h, mg + 1)
                for j in range(4):
                    if m >= 4 and j == 0:
                        t.wait_ge(s_te_.h, m - 3)
                    t.transpose(ptp4[m % 4][:, j, :],
                                kpl[:, m % 2, 128 * j:128 * (j + 1)], ident[:]
                                ).then_inc(s_tr_.h, 1)
                    s_tr_.inc()

            @block.vector
            def _(v, g=g, th=th, m=m):
                v.wait_ge(s_tr_.h, 4 * (m + 1))
                if m == 0 and kind == "q":
                    v.wait_ge(s_ktw[0].h, 128)
                    v.wait_ge(s_ktw[1].h, 128)
                    v.wait_ge(s_kmx.h, 64)
                if m >= 2:
                    v.wait_ge(s_tw_[m % 2].h, 16 * (m // 2))
                    v.wait_ge(s_mx_.h, 4 * (m - 1))
                v.tensor_copy(qtk4[:, m % 2, :, :], ptp4[m % 4][:, :, :]
                              ).then_inc(s_te_.h, 1)
                s_te_.inc()

            @block.scalar
            def _(sc, g=g, th=th, m=m):
                sc.wait_ge(s_te_.h, m + 1)
                sc.dma_start(
                    out=tok_scr[4 * th:4 * th + 4, :, g, :].transpose([1, 0, 2]),
                    in_=qtk4[:, m % 2, :, :]).then_inc(s_tw_[m % 2].h, 16)
                s_tw_[m % 2].inc(16)

            @block.vector
            def _(v, g=g, th=th, m=m):
                v.wait_ge(s_te_.h, m + 1)
                for j in range(4):
                    v.reduce_max(mx[:, 4 * th + j, g:g + 1],
                                 qtk4[:, m % 2, j, :], axis=AX.X,
                                 apply_absolute_value=True).then_inc(s_mx_.h, 1)
                    s_mx_.inc()

        for m in range(16):
            emit_mm(m)
            emit_evac(m)
            if kind != "v" and m >= 1:
                emit_tr(m - 1)
            if m % 2 == 1:
                nj = 8 * plane + m // 2 + 2
                if nj < 24:
                    plane_load(nj)
        if kind != "v":
            emit_tr(15)

    plane_load(0)
    plane_load(1)
    plane_phase("k")
    plane_phase("v")

    plane_phase("q")

    for i in range(NT):
        @block.vector
        def _(v, i=i):
            if i == 0:
                v.wait_ge(s_qmx.h, 64)
                v.wait_ge(s_kmx.h, 64)
            v.wait_ge(s_cb.h, s_cb.v)
            v.tensor_tensor(cbprod[:], qmax[:, i, :], kmax[:, i, :], ALU.mult
                            ).then_inc(s_cb.h, 1)
            s_cb.inc()
            v.wait_ge(s_cb.h, s_cb.v)
            v.reduce_sum(cb8[:, i:i + 1], cbprod[:], axis=AX.X, negate=True
                         ).then_inc(s_cb.h, 1)
            s_cb.inc()
            v.wait_ge(s_cb.h, s_cb.v)
            if i >= 1:
                v.wait_ge(s_cbw[(i - 1) % 2].h, 16 * ((i - 1) // 2 + 1))
            v.tensor_copy(cbc[:], cb8[:, i:i + 1].broadcast_to([128, 128])
                          ).then_inc(s_cb.h, 1)
            s_cb.inc()

        @block.scalar
        def _(sc, i=i):
            sc.wait_ge(s_cb.h, 3 * (i + 1))
            sc.dma_start(out=qtok_scr[i, :, 8, :], in_=cbc[:]).then_inc(s_cbw[i % 2].h, 16)
            s_cbw[i % 2].inc(16)

    for th in range(2):
        @block.tensor
        def _(t, th=th):
            bank = pmb[th % 2]
            t.wait_ge(s_gpe.h, 48)
            t.wait_ge(s_sc0.h, p0c["SC0_ALL"])
            for c in range(EC):
                mm = t.matmul(bank[0:16, :], aeffT[:, c, :],
                              xnT[:, c, 512 * th:512 * (th + 1)],
                              start=(c == 0), stop=(c == EC - 1))
                if c == EC - 1:
                    mm.then_inc(s_t1.h, 1)
            s_t1.inc()

        @block.scalar
        def _(sc, th=th):
            sc.wait_ge(s_t1.h, th + 1)
            sc.copy(t1T[0:16, 512 * th:512 * (th + 1)], pmb[th % 2][0:16, :]
                    ).then_inc(s_t1e.h, 1)
            s_t1e.inc()

    def emit_stack(sg):
        tt = sg // 8
        r0 = 16 * (sg % 8)

        @block.sync
        def _(sync, sg=sg, tt=tt, r0=r0):
            if sg == 0:
                sync.wait_ge(s_qtw[0].h, 128)
                sync.wait_ge(s_qtw[1].h, 128)
                sync.wait_ge(s_ktw[0].h, 128)
                sync.wait_ge(s_ktw[1].h, 128)
                sync.wait_ge(s_cbw[0].h, s_cbw[0].v)
                sync.wait_ge(s_cbw[1].h, s_cbw[1].v)
                sync.wait_ge(s_gpd.h, 80)
            if sg >= 2:
                sync.wait_ge(s_sm.h, 16 * (sg - 1))
            sync.dma_start(out=q_stk[:, sg % 2, :, :],
                           in_=qtok_scr[tt, r0:r0 + 16, :, :].transpose([1, 0, 2])
                           ).then_inc(s_stk[sg % 2].h, 16)
            s_stk[sg % 2].inc(16)
            sync.dma_start(out=k_stk[0:8, sg % 2, :, :],
                           in_=ktok_scr[tt, r0:r0 + 16, :, :].transpose([1, 0, 2])
                           ).then_inc(s_stk[sg % 2].h, 16)
            s_stk[sg % 2].inc(16)

    def emit_merge(sg):
        tt = sg // 8
        r0 = 16 * (sg % 8)

        @block.gpsimd
        def _(gp, sg=sg, tt=tt, r0=r0):
            gp.wait_ge(s_nm.h, 2 * (sg + 1))
            for hh in range(2):
                gp.dma_start(
                    out=attn_scr[tt, r0 + 8 * hh:r0 + 8 * (hh + 1), :, :]
                    .transpose([1, 0, 2]),
                    in_=attn_sb[:, sg % 2, 8 * hh:8 * (hh + 1), :]
                    ).then_inc(s_mg[sg % 2].h, 16)
                s_mg[sg % 2].inc(16)

    @block.scalar
    def _(sc):
        sc.wait_ge(s_ccf.h, 1)
        sc.wait_ge(s_cf.h, 8)
        for s in (s_sk[0], s_sk[1], s_sv[0], s_sv[1]):
            sc.wait_ge(s.h, 32)
        sc.wait_ge(s_ap.h, 8)
        sc.wait_ge(s_bto.h, 16)
        sc.wait_ge(s_lacc.h, 6)
        sc.wait_ge(s_gpd.h, 16)
        sc.wait_ge(s_brw.h, 16)
        sc.dma_start(out=wTboot[:],
                     in_=wfT_scr.rearrange("(c p) o -> p c o", p=128)[:, 0:16, 0:512]
                     ).then_inc(s_brw.h, 16)
        s_brw.inc(16)

    for tt in range(NT):
        @block.scalar
        def _(sc, tt=tt):
            for lg in range(16):
                B = 16 * tt + lg
                sc.wait_ge(s_sm.h, 8 * (B + 1))
                if B >= 3:
                    sc.wait_ge(s_om.h, 8 * (B - 2))
                sc.activation(exp_sb[:, B % 3, :, :], rotS8[B % 2], AF.Exp, scale=ISD
                              ).then_inc(s_ex.h, 1)
                s_ex.inc()

    emit_stack(0)
    emit_stack(1)
    def emit_vint(tt):
        @block.vector
        def _(v, tt=tt):
            if tt == 0:
                v.wait_ge(s_gpe.h, 32)
                v.wait_ge(s_gp.h, 5)
            if tt >= 2:
                v.wait_ge(s_om.h, 128 * (tt - 1))
            for g in range(8):
                v.tensor_copy(
                    v_int[:, tt % 2, :, 4 * g:4 * (g + 1)],
                    vT[:, g, 128 * tt:128 * (tt + 1)].unsqueeze(2)
                    .broadcast_to([128, 128, 4])).then_inc(s_vi.h, 1)
                s_vi.inc()

    emit_vint(0)
    for sg in range(64):
        tt = sg // 8
        if sg % 8 == 4 and tt + 1 < NT:
            emit_vint(tt + 1)

        for bh in range(2):
            B = 2 * sg + bh

            @block.tensor
            def _(t, sg=sg, bh=bh, B=B):
                if bh == 0:
                    t.wait_ge(s_stk[sg % 2].h, 32 * (sg // 2 + 1))
                if B == 0:
                    t.wait_ge(s_qte.h, 16)
                if B == 1:
                    t.wait_ge(s_t1e.h, 2)
                if B >= 2:
                    t.wait_ge(s_ex.h, B - 1)
                for i in range(8):
                    mm = t.matmul(rotS8[B % 2][:, i, :], k_stk[:, sg % 2, 8 * bh + i, :],
                                  q_stk[:, sg % 2, 8 * bh + i, :], start=True, stop=True)
                    if i == 7:
                        mm.then_inc(s_sm.h, 8)
                        s_sm.inc(8)

            Bm = B - 1
            if Bm >= 0:
                @block.tensor
                def _(t, Bm=Bm):
                    ttm = Bm // 16
                    t.wait_ge(s_ex.h, Bm + 1)
                    if Bm % 16 == 0:
                        t.wait_ge(s_vi.h, 8 * (ttm + 1))
                    if Bm == 0:
                        t.wait_ge(s_sc0.h, p0c["SC0_ALL"])
                    if Bm >= 2:
                        t.wait_ge(s_nm.h, Bm - 1)
                    tloc = 16 * ((Bm // 2) % 8) + 8 * (Bm % 2)
                    for i in range(8):
                        mm = t.matmul(rotO8[Bm % 2][:, i, 0:33], exp_sb[:, Bm % 3, i, :],
                                      v_int[:, ttm % 2, tloc + i, :],
                                      start=True, stop=True)
                        if i == 7:
                            mm.then_inc(s_om.h, 8)
                            s_om.inc(8)

        @block.vector
        def _(v, sg=sg):
            for bh in range(2):
                B = 2 * sg + bh
                v.wait_ge(s_om.h, 8 * (B + 1))
                v.reciprocal(rcp8[:, B % 3, :], rotO8[B % 2][:, :, 32]).then_inc(s_rc.h, 1)
                s_rc.inc()
                v.wait_ge(s_rc.h, s_rc.v)
                if sg >= 2 and bh == 0:
                    v.wait_ge(s_mg[sg % 2].h, 32 * (sg // 2))
                v.tensor_tensor(attn_sb[:, sg % 2, 8 * bh:8 * (bh + 1), :],
                                rotO8[B % 2][:, :, 0:32],
                                rcp8[:, B % 3, :].unsqueeze(2).broadcast_to([128, 8, 32]),
                                ALU.mult).then_inc(s_nm.h, 1)
                s_nm.inc()

        if sg + 2 < 64:
            emit_stack(sg + 2)
        emit_merge(sg)

    @block.tensor
    def _(t):
        Bm = 127
        t.wait_ge(s_ex.h, 128)
        t.wait_ge(s_vi.h, 64)
        t.wait_ge(s_nm.h, Bm - 1)
        tloc = 16 * ((Bm // 2) % 8) + 8 * (Bm % 2)
        for i in range(8):
            mm = t.matmul(rotO8[Bm % 2][:, i, 0:33], exp_sb[:, Bm % 3, i, :],
                          v_int[:, (Bm // 16) % 2, tloc + i, :],
                          start=True, stop=True)
            if i == 7:
                mm.then_inc(s_om.h, 8)
                s_om.inc(8)

    es2.close()
    es3.close()
    es4 = contextlib.ExitStack()
    sb4 = lambda n, s, d: es4.enter_context(nc.sbuf_tensor(n, s, d))
    wT = sb4("wT", [128, 4, 16, 512], BF16)
    atile = sb4("atile", [128, 2, 512], F32)
    xtile = sb4("xtile", [128, 2, 512], F32)
    beffT2 = sb4("beffT2", [17, 2, 512], BF16)

    wfT_scr_v = wfT_scr.rearrange("(c p) o -> p c o", p=128)
    for h in range(1, 16):
        @block.scalar
        def _(sc, h=h):
            if h == 1:
                sc.wait_ge(s_om.h, 1024)
                sc.wait_ge(s_nm.h, 128)
                sc.wait_ge(s_vi.h, 64)
                sc.wait_ge(s_mg[0].h, s_mg[0].v)
                sc.wait_ge(s_mg[1].h, s_mg[1].v)
                sc.wait_ge(s_ccf.h, 1)
            if h >= 5:
                sc.wait_ge(s_fm.h, 8 * ((h - 4) // 2 + 1))
            oc, hh = h // 2, h % 2
            sc.dma_start(out=wT[:, (h - 1) % 4, :, :],
                         in_=wfT_scr_v[:, 16 * hh:16 * (hh + 1),
                                       512 * oc:512 * (oc + 1)]
                         ).then_inc(s_wt4[(h - 1) % 4].h, 16)
            s_wt4[(h - 1) % 4].inc(16)

    for oc in range(8):
        @block.sync
        def _(sync, oc=oc):
            if oc == 0:
                sync.wait_ge(s_ex.h, 128)
                sync.wait_ge(s_om.h, 1024)
                sync.wait_ge(s_nm.h, 128)
                sync.wait_ge(s_vi.h, 64)
                sync.wait_ge(s_mg[0].h, s_mg[0].v)
                sync.wait_ge(s_mg[1].h, s_mg[1].v)
                sync.wait_ge(s_ccf.h, 1)
            if oc >= 2:
                sync.wait_ge(s_fm.h, 8 * (oc - 1))
            sync.dma_start(out=beffT2[:, oc % 2, :], in_=beffT_scr[:, oc, :]
                           ).then_inc(s_bti.h, 16)
            s_bti.inc(16)

        for tt in range(NT):
            i = oc * NT + tt
            bnk = pmb[i % 2]

            @block.sync
            def _(sync, oc=oc, tt=tt, i=i):
                if i == 0:
                    sync.wait_ge(s_mg[0].h, s_mg[0].v)
                    sync.wait_ge(s_mg[1].h, s_mg[1].v)
                if i >= 2:
                    sync.wait_ge(s_out[i % 2].h, 16 * (i // 2))
                sync.dma_start(out=atile[:, i % 2, :],
                               in_=attn_scr[tt, :, 16 * oc:16 * (oc + 1), :]
                               .rearrange("p a b -> p (a b)")).then_inc(s_ar[i % 2].h, 16)
                s_ar[i % 2].inc(16)
                if i >= 2:
                    sync.wait_ge(s_a2.h, i - 1)
                sync.dma_start(out=xtile[:, i % 2, :],
                               in_=xsr[tt][:, 512 * oc:512 * (oc + 1)]).then_inc(s_xr[i % 2].h, 16)
                s_xr[i % 2].inc(16)

            @block.tensor
            def _(t, oc=oc, tt=tt, i=i, bnk=bnk):
                if tt == 0:
                    for h in (2 * oc, 2 * oc + 1):
                        if h == 0:
                            t.wait_ge(s_brw.h, 32)
                        else:
                            t.wait_ge(s_wt4[(h - 1) % 4].h, 16 * ((h - 1) // 4 + 1))
                if i == 0:
                    t.wait_ge(s_ex.h, 128)
                    t.wait_ge(s_nm.h, 128)
                if i >= 2:
                    t.wait_ge(s_a1.h, i - 1)
                for c in range(EC):
                    h_used = 2 * oc + c // 16
                    rhs = (wTboot[:, c, :] if h_used == 0
                           else wT[:, (h_used - 1) % 4, c % 16, :])
                    t.matmul(bnk[:], xnT[:, c, 128 * tt:128 * (tt + 1)],
                             rhs, start=(c == 0), stop=False)
                if tt == 0:
                    t.wait_ge(s_bti.h, 16 * (oc + 1))
                t.matmul(bnk[:], t1T[:, 128 * tt:128 * (tt + 1)], beffT2[:, oc % 2, :],
                         start=False, stop=True).then_inc(s_fm.h, 1)
                s_fm.inc()

            @block.vector
            def _(v, i=i, bnk=bnk):
                v.wait_ge(s_fm.h, i + 1)
                v.wait_ge(s_ar[i % 2].h, 16 * (i // 2 + 1))
                if i >= 2:
                    v.wait_ge(s_a2.h, i - 1)
                v.tensor_tensor(atile[:, i % 2, :], bnk[:], atile[:, i % 2, :],
                                ALU.add).then_inc(s_a1.h, 1)
                s_a1.inc()

            @block.gpsimd
            def _(gp, oc=oc, tt=tt, i=i):
                gp.wait_ge(s_a1.h, i + 1)
                gp.wait_ge(s_xr[i % 2].h, 16 * (i // 2 + 1))
                gp.tensor_tensor(atile[:, i % 2, :], atile[:, i % 2, :],
                                 xtile[:, i % 2, :], ALU.add).then_inc(s_a2.h, 1)
                s_a2.inc()
                gp.wait_ge(s_a2.h, i + 1)
                gp.dma_start(out=outr[tt][:, 512 * oc:512 * (oc + 1)],
                             in_=atile[:, i % 2, :]).then_inc(s_out[i % 2].h, 16)
                s_out[i % 2].inc(16)

    @block.scalar
    def _(sc):
        for s in (s_x + s_ktw + s_qtw + s_cbw + s_wl + s_wt4):
            sc.wait_ge(s.h, s.v)

    @block.sync
    def _(sync):
        for s in ([s_small, s_lab, s_bti, s_cck, s_ccv, s_ccq, s_ccf]
                  + s_lk + s_lv + s_lq + s_lf + s_sk + s_sv + s_sq + s_sf
                  + s_stk + s_mg + s_ar + s_xr + s_out):
            sync.wait_ge(s.h, s.v)

    @block.gpsimd
    def _(gp):
        gp.wait_ge(s_gpd.h, s_gpd.v)
        gp.wait_ge(s_fbc.h, s_fbc.v)
        gp.wait_ge(s_bias.h, s_bias.v)

    es4.close()
    es_boot.close()
    es2a.close()
    es.close()
    return nc


def kernel(**inputs):
    x = np.asarray(inputs["L0_attn_x"], dtype=np.float32).reshape(2 * 4096, E)
    shards = [np.ascontiguousarray(x[i * T:(i + 1) * T]) for i in range(NCORES)]
    wqf = np.asarray(inputs["L0_attn_Wq"], dtype=np.float32).reshape(E, 128, 32)
    wkf = np.asarray(inputs["L0_attn_Wk"], dtype=np.float32).reshape(E, 128, 8)
    wvf = np.asarray(inputs["L0_attn_Wv"], dtype=np.float32).reshape(E, 128, 8)
    wfT = np.asarray(inputs["L0_ffn_weight"], dtype=np.float32).reshape(E, E).T
    base = {
        "bq": np.ascontiguousarray(np.asarray(inputs["L0_attn_bq"], dtype=np.float32)),
        "bk": np.ascontiguousarray(np.asarray(inputs["L0_attn_bk"], dtype=np.float32)),
        "bv": np.ascontiguousarray(np.asarray(inputs["L0_attn_bv"], dtype=np.float32)),
        "fb": np.ascontiguousarray(np.asarray(inputs["L0_ffn_bias"], dtype=np.float32)).reshape(1, E),
        "la": np.ascontiguousarray(np.asarray(inputs["L0_ffn_lora_A"], dtype=np.float32)),
        "lbw": np.ascontiguousarray(np.asarray(inputs["L0_ffn_lora_B"], dtype=np.float32)),
        "lsi": np.ascontiguousarray(np.asarray(inputs["L0_ffn_lora_start_indices"], dtype=np.int32)).reshape(1, 4),
    }
    in_maps = []
    for core in range(NCORES):
        m = dict(base)
        m["xs"] = shards[core]
        m["wkm"] = np.ascontiguousarray(wkf[512 * core:512 * (core + 1)])
        m["wvm"] = np.ascontiguousarray(wvf[512 * core:512 * (core + 1)])
        m["wqm"] = np.ascontiguousarray(wqf[512 * core:512 * (core + 1)])
        m["wfm"] = np.ascontiguousarray(wfT[512 * core:512 * (core + 1)])
        m["bidx"] = np.array([[core // 4]], dtype=np.int32)
        in_maps.append(m)

    nc = build_program()
    res = run_bass_kernel_spmd(nc, in_maps, core_ids=list(range(NCORES)))
    outs = [np.asarray(r["out"]) for r in res.results]
    return np.concatenate(outs, axis=0).reshape(2, 4096, E).astype(np.float32)


# version 33
# speedup vs baseline: 1.0279x; 1.0183x over previous
import contextlib
import math

import numpy as np
import concourse.bass as bass
import concourse.mybir as mybir
from concourse.bass_utils import run_bass_kernel_spmd

F32 = mybir.dt.float32
BF16 = mybir.dt.bfloat16
I32 = mybir.dt.int32
AF = mybir.ActivationFunctionType
ALU = mybir.AluOpType
AX = mybir.AxisListType

NCORES = 8
T = 1024
NT = 8
E = 4096
EC = 32
ISD = float(1.0 / math.sqrt(128.0))


class Sem:
    def __init__(self, h):
        self.h = h
        self.v = 0

    def inc(self, n=1):
        self.v += n
        return self.v


def build_program():
    nc = bass.Bass("TRN2", target_bir_lowering=False)

    xs = nc.dram_tensor("xs", [T, E], F32, kind="ExternalInput").ap()
    wkm = nc.dram_tensor("wkm", [512, 128, 8], F32, kind="ExternalInput").ap()
    wvm = nc.dram_tensor("wvm", [512, 128, 8], F32, kind="ExternalInput").ap()
    wqm = nc.dram_tensor("wqm", [512, 128, 32], F32, kind="ExternalInput").ap()
    wfm = nc.dram_tensor("wfm", [512, E], F32, kind="ExternalInput").ap()
    bq = nc.dram_tensor("bq", [128, 32], F32, kind="ExternalInput").ap()
    bk = nc.dram_tensor("bk", [128, 8], F32, kind="ExternalInput").ap()
    bv = nc.dram_tensor("bv", [128, 8], F32, kind="ExternalInput").ap()
    fb = nc.dram_tensor("fb", [1, E], F32, kind="ExternalInput").ap()
    la = nc.dram_tensor("la", [3, 16, E], F32, kind="ExternalInput").ap()
    lbw = nc.dram_tensor("lbw", [3, E, 16], F32, kind="ExternalInput").ap()
    lsi = nc.dram_tensor("lsi", [1, 4], I32, kind="ExternalInput").ap()
    bidx = nc.dram_tensor("bidx", [1, 1], I32, kind="ExternalInput").ap()
    out = nc.dram_tensor("out", [T, E], F32, kind="ExternalOutput").ap()

    wkstg = nc.dram_tensor("wkstg", [1024, 512], BF16, kind="Internal").ap()
    wvstg = nc.dram_tensor("wvstg", [1024, 512], BF16, kind="Internal").ap()
    wqstg = nc.dram_tensor("wqstg", [1024, 512], BF16, kind="Internal").ap()
    wfstg = nc.dram_tensor("wfstg", [512, E], BF16, kind="Internal").ap()
    wk_scr = nc.dram_tensor("wk_scr", [8192, 512], BF16, kind="Internal",
                            addr_space="Shared").ap()
    wv_scr = nc.dram_tensor("wv_scr", [8192, 512], BF16, kind="Internal",
                            addr_space="Shared").ap()
    wq_scr = nc.dram_tensor("wq_scr", [8192, 512], BF16, kind="Internal",
                            addr_space="Shared").ap()
    wfT_scr = nc.dram_tensor("wfT_scr", [E, E], BF16, kind="Internal",
                             addr_space="Shared").ap()
    qtok_scr = nc.dram_tensor("qtok_scr", [NT, 128, 9, 128], BF16, kind="Internal").ap()
    ktok_scr = nc.dram_tensor("ktok_scr", [NT, 128, 8, 128], BF16, kind="Internal").ap()
    attn_scr = nc.dram_tensor("attn_scr", [NT, 128, 128, 32], F32, kind="Internal").ap()
    brow_scr = nc.dram_tensor("brow_scr", [3, 8, 128], BF16, kind="Internal").ap()
    beffT_scr = nc.dram_tensor("beffT_scr", [17, 8, 512], BF16, kind="Internal").ap()

    es = contextlib.ExitStack()
    sb = lambda n, s, d: es.enter_context(nc.sbuf_tensor(n, s, d))
    pst = lambda n, s, d: es.enter_context(nc.psum_tensor(n, s, d))
    mks = lambda n: Sem(es.enter_context(nc.semaphore(n)))

    ptpA = pst("ptpA", [128, 4, 128], BF16)
    ptpB = pst("ptpB", [128, 4, 128], BF16)
    pmAB = pst("pmAB", [128, 1024], F32)
    psAB = pst("psAB", [128, 1024], F32)
    poAB = pst("poAB", [128, 1024], F32)
    pmA = pmAB[:, 0:512]
    pmB = pmAB[:, 512:1024]
    psA = psAB[:, 0:512].rearrange("p (a b) -> p a b", b=128)
    psB = psAB[:, 512:1024].rearrange("p (a b) -> p a b", b=128)
    poA = poAB[:, 0:512].rearrange("p (a b) -> p a b", b=128)
    poB = poAB[:, 512:1024].rearrange("p (a b) -> p a b", b=128)
    psAb = psAB[:, 0:512].bitcast(BF16).rearrange("p (a b) -> p a b", b=128)
    psBb = psAB[:, 512:1024].bitcast(BF16).rearrange("p (a b) -> p a b", b=128)
    poAb = poAB[:, 0:512].bitcast(BF16).rearrange("p (a b) -> p a b", b=128)
    poBb = poAB[:, 512:1024].bitcast(BF16).rearrange("p (a b) -> p a b", b=128)
    ptp4 = (ptpA[:], ptpB[:], psAb[:, 0:4, :], psBb[:, 0:4, :])
    rotS8 = (psAB[:].rearrange("p (a b) -> p a b", b=128),
             pmAB[:].rearrange("p (a b) -> p a b", b=128))
    rotO8 = (poAB[:, 0:264].rearrange("p (a b) -> p a b", b=33),
             poAB[:, 512:776].rearrange("p (a b) -> p a b", b=33))
    pmb = (pmA, pmB)
    ptp = (ptpA, ptpB)

    xnT = sb("xnT", [128, EC, T], BF16)
    xnt2 = sb("xnt2", [128, 2, E], BF16)
    ident = sb("ident", [128, 128], BF16)
    lsi_s = sb("lsi_s", [1, 4], I32)
    lsi_f = sb("lsi_f", [1, 4], F32)
    bidx_s = sb("bidx_s", [1, 1], I32)
    bidx_f = sb("bidx_f", [1, 1], F32)
    lsi_rep = sb("lsi_rep", [128, 4], F32)
    bidx_rep = sb("bidx_rep", [128, 1], F32)
    cge = sb("cge", [128, 4], F32)
    seg = sb("seg", [128, 1], F32)
    usev = sb("usev", [128, 1], F32)
    m0 = sb("m0", [128, 1], F32)
    m1 = sb("m1", [128, 1], F32)
    m2 = sb("m2", [128, 1], F32)
    ta_ = sb("ta_", [128, 1], F32)
    tb_ = sb("tb_", [128, 1], F32)
    tc_ = sb("tc_", [128, 1], F32)
    td_ = sb("td_", [128, 1], F32)
    bq_s = sb("bq_s", [128, 32], F32)
    bqsum = sb("bqsum", [128, 8], F32)
    bk_s = sb("bk_s", [128, 8], F32)
    bv_s = sb("bv_s", [128, 8], F32)
    aeffT = sb("aeffT", [128, EC, 16], BF16)
    t1T = sb("t1T", [17, T], BF16)
    qmax = sb("qmax", [128, NT, 8], F32)
    kmax = sb("kmax", [128, NT, 8], F32)
    cb8 = sb("cb8", [128, NT], F32)
    cbprod = sb("cbprod", [128, 8], F32)
    ones512 = sb("ones512", [1, 512], BF16)
    bns = sb("bns", [128, 2, 8, 6], F32)
    mv = sb("mv", [128, 2, 2], F32)
    negmu = sb("negmu", [128, 2, 1], F32)
    lnv = sb("lnv", [128, 2, 1], F32)
    rstd = sb("rstd", [128, 2, 1], F32)
    lnt = sb("lnt", [128, 2, 1], F32)

    s_small = mks("s_small"); s_lab = mks("s_lab"); s_gpd = mks("s_gpd"); s_gp = mks("s_gp")
    s_v0 = mks("s_v0"); s_t0 = mks("s_t0"); s_sc0 = mks("s_sc0")
    s_x = [mks("s_x0"), mks("s_x1"), mks("s_x2")]
    s_ln = mks("s_ln"); s_rs = mks("s_rs"); s_ap = mks("s_ap")
    s_tr = mks("s_tr"); s_xe = mks("s_xe")
    s_lk = [mks("s_lk0"), mks("s_lk1")]
    s_lv = [mks("s_lv0"), mks("s_lv1")]
    s_lq = [mks("s_lq0"), mks("s_lq1")]
    s_lf = [mks("s_lf0"), mks("s_lf1")]
    s_wl = [mks("s_wl0"), mks("s_wl1"), mks("s_wl2")]
    s_gpm = mks("s_gpm")
    s_gpe = mks("s_gpe")
    s_lacc = mks("s_lacc")
    s_wt4 = [mks("s_wt40"), mks("s_wt41"), mks("s_wt42"), mks("s_wt43")]
    s_sk = [mks("s_sk0"), mks("s_sk1")]
    s_sv = [mks("s_sv0"), mks("s_sv1")]
    s_sq = [mks("s_sq0"), mks("s_sq1")]
    s_sf = [mks("s_sf0"), mks("s_sf1")]
    s_ck = mks("s_ck"); s_cv = mks("s_cv"); s_js = mks("s_js"); s_brw = mks("s_brw")
    s_bt = mks("s_bt"); s_bias = mks("s_bias"); s_fbc = mks("s_fbc")
    s_cf = mks("s_cf")
    s_cck = mks("s_cck"); s_ccv = mks("s_ccv")
    s_ccq = mks("s_ccq"); s_ccf = mks("s_ccf")
    s_ktr = mks("s_ktr"); s_qtr = mks("s_qtr")
    s_kte = mks("s_kte"); s_qte = mks("s_qte")
    s_ktw = [mks("s_ktw0"), mks("s_ktw1")]
    s_qtw = [mks("s_qtw0"), mks("s_qtw1")]
    s_kmx = mks("s_kmx"); s_qmx = mks("s_qmx")
    s_cb = mks("s_cb"); s_cbw = [mks("s_cbw0"), mks("s_cbw1")]
    s_t1 = mks("s_t1"); s_t1e = mks("s_t1e"); s_bto = mks("s_bto"); s_bti = mks("s_bti")
    s_stk = [mks("s_stk0"), mks("s_stk1")]; s_vi = mks("s_vi")
    s_sm = mks("s_sm"); s_ex = mks("s_ex"); s_om = mks("s_om")
    s_rc = mks("s_rc"); s_nm = mks("s_nm"); s_mg = [mks("s_mg0"), mks("s_mg1")]
    s_wc = mks("s_wc"); s_wt = mks("s_wt"); s_we = mks("s_we"); s_fm = mks("s_fm")
    s_ar = [mks("s_ar0"), mks("s_ar1")]
    s_xr = [mks("s_xr0"), mks("s_xr1")]
    s_a1 = mks("s_a1"); s_a2 = mks("s_a2")
    s_out = [mks("s_out0"), mks("s_out1")]

    wproj = xnt2[:].rearrange("p b (c d) -> p b c d", d=128)
    wprj = wproj
    xsr = xs.rearrange("(n p) e -> n p e", p=128)
    outr = out.rearrange("(n p) e -> n p e", p=128)

    block = es.enter_context(nc.Block())

    es2a = contextlib.ExitStack()
    sb2a = lambda n, s, d: es2a.enter_context(nc.sbuf_tensor(n, s, d))
    wqraw = sb2a("wqraw", [128, 2, 2048], F32)
    wkc = wqraw[:, 0, :].rearrange("p (b c d) -> p b c d", b=2, c=128, d=8)
    wvc = wqraw[:, 1, :].rearrange("p (b c d) -> p b c d", b=2, c=128, d=8)
    wk_st = sb2a("wk_st", [128, 2, 8, 128], BF16)
    wv_st = sb2a("wv_st", [128, 2, 8, 128], BF16)
    bias_rows = sb2a("bias_rows", [1, 3, 8, 128], BF16)

    es_xt = contextlib.ExitStack()
    xt = es_xt.enter_context(nc.sbuf_tensor("xt", [128, 3, E], F32))

    es0 = contextlib.ExitStack()
    sb0 = lambda n, s, d: es0.enter_context(nc.sbuf_tensor(n, s, d))
    la_a = sb0("la_a", [16, E], BF16)
    iv = sb0("iv", [128, 128], I32)
    onesf = sb0("onesf", [1, 128], F32)
    bqt1 = sb0("bqt1", [128, 8], F32)
    bqt2 = sb0("bqt2", [128, 8], F32)
    ones_long = sb0("ones_long", [1, 1024], BF16)
    aeffA = sb0("aeffA", [16, E], BF16)
    lb_a = sb0("lb_a", [128, EC, 16], BF16)
    beffA = sb0("beffA", [128, EC, 16], BF16)
    beffT = sb0("beffT", [17, 8, 512], BF16)
    bqb = sb0("bqb", [128, 3, 8], BF16)
    brow_sb = sb0("brow_sb", [8, 3, 128], BF16)

    @block.sync
    def _(sync):
        for dst, src in ((lsi_s, lsi), (bidx_s, bidx), (bq_s, bq), (bk_s, bk),
                         (bv_s, bv)):
            sync.dma_start(out=dst[:], in_=src[:]).then_inc(s_small.h, 16)
            s_small.inc(16)

    @block.gpsimd
    def _(gp):
        gp.iota(iv[:], pattern=[[1, 128]], base=0, channel_multiplier=-1).then_inc(s_gp.h, 1)
        s_gp.inc()
        gp.memset(onesf[:], 1.0).then_inc(s_gp.h, 1)
        s_gp.inc()
        gp.memset(ones_long[:], 1.0).then_inc(s_gp.h, 1)
        s_gp.inc()
        gp.memset(ones512[:], 1.0).then_inc(s_gp.h, 1)
        s_gp.inc()

    wkmr = wkm.rearrange("(cl p) d g -> cl p d g", p=128)
    wvmr = wvm.rearrange("(cl p) d g -> cl p d g", p=128)
    wqmr = wqm.rearrange("(cl p) d h -> cl p d h", p=128)
    wfmr = wfm.rearrange("(mc p) o -> mc p o", p=128)
    wkstg_v = wkstg.rearrange("(g p) (cl d) -> g p cl d", p=128, d=128)
    wvstg_v = wvstg.rearrange("(g p) (cl d) -> g p cl d", p=128, d=128)
    wqstg_v = wqstg.rearrange("(g p) (cl d) -> g p cl d", p=128, d=128)

    @block.sync
    def _(sync):
        for st in ("k", "v"):
            lvw, stg_v, st_b, s_l, s_s, s_c = {
                "k": (wkmr, wkstg_v, wk_st, s_lk, s_sk, s_ck),
                "v": (wvmr, wvstg_v, wv_st, s_lv, s_sv, s_cv),
            }[st]
            for cl in range(4):
                if cl >= 2:
                    sync.wait_ge(s_c.h, cl - 1)
                sync.dma_start(out=(wkc if st == "k" else wvc)[:, cl % 2, :, :],
                               in_=lvw[cl]).then_inc(s_l[cl % 2].h, 16)
                s_l[cl % 2].inc(16)
                if cl >= 1:
                    sync.wait_ge(s_c.h, cl)
                    sync.dma_start(out=stg_v[:, :, cl - 1, :].transpose([1, 0, 2]),
                                   in_=st_b[:, (cl - 1) % 2, :, :]
                                   ).then_inc(s_s[(cl - 1) % 2].h, 16)
                    s_s[(cl - 1) % 2].inc(16)
            sync.wait_ge(s_c.h, 4)
            sync.dma_start(out=stg_v[:, :, 3, :].transpose([1, 0, 2]),
                           in_=st_b[:, 1, :, :]).then_inc(s_s[1].h, 16)
            s_s[1].inc(16)

    @block.sync
    def _(sync):
        sync.wait_ge(s_brw.h, 16)
        sync.dma_start(out=bias_rows[:], in_=brow_scr.unsqueeze(0)
                       ).then_inc(s_bias.h, 16)
        s_bias.inc(16)

    def emit_la_load_gp(gp, a):
        if a > 0:
            gp.wait_ge(s_lacc.h, 2 * a)
        gp.dma_start(out=la_a[:], in_=la[a]).then_inc(s_lab.h, 16)
        s_lab.inc(16)
        gp.dma_start(out=lb_a[:],
                     in_=lbw[a].rearrange("(c p) r -> p c r", p=128)
                     ).then_inc(s_lab.h, 16)
        s_lab.inc(16)

    @block.gpsimd
    def _(gp):
        emit_la_load_gp(gp, 0)
        for cl in range(4):
            gp.wait_ge(s_lk[cl % 2].h, 16 * (cl // 2 + 1))
            if cl >= 2:
                gp.wait_ge(s_sk[cl % 2].h, 16 * (cl // 2))
            gp.tensor_copy(wk_st[:, cl % 2, :, :],
                           wkc[:, cl % 2, :, :].transpose([0, 2, 1])).then_inc(s_ck.h, 1)
            s_ck.inc()
        gp.wait_ge(s_sk[0].h, 32)
        gp.wait_ge(s_sk[1].h, 32)
        gp.collective_compute("AllGather", ALU.bypass,
                              replica_groups=[list(range(8))],
                              ins=[wkstg], outs=[wk_scr]).then_inc(s_cck.h, 1)
        s_cck.inc()
        emit_la_load_gp(gp, 1)
        for cl in range(4):
            gp.wait_ge(s_lv[cl % 2].h, 16 * (cl // 2 + 1))
            if cl >= 2:
                gp.wait_ge(s_sv[cl % 2].h, 16 * (cl // 2))
            gp.tensor_copy(wv_st[:, cl % 2, :, :],
                           wvc[:, cl % 2, :, :].transpose([0, 2, 1])).then_inc(s_cv.h, 1)
            s_cv.inc()
        gp.wait_ge(s_sv[0].h, 32)
        gp.wait_ge(s_sv[1].h, 32)
        gp.collective_compute("AllGather", ALU.bypass,
                              replica_groups=[list(range(8))],
                              ins=[wvstg], outs=[wv_scr]).then_inc(s_ccv.h, 1)
        s_ccv.inc()
        emit_la_load_gp(gp, 2)

    @block.vector
    def _(v):
        v.wait_ge(s_gp.h, 1)
        v.tensor_scalar(ident[:], iv[:], 0.0, None, ALU.is_equal).then_inc(s_v0.h, 1)
        s_v0.inc()
        v.wait_ge(s_small.h, 80)
        v.tensor_copy(lsi_f[:], lsi_s[:]).then_inc(s_v0.h, 1)
        s_v0.inc()
        v.tensor_copy(bidx_f[:], bidx_s[:]).then_inc(s_v0.h, 1)
        s_v0.inc()

    @block.tensor
    def _(t):
        t.wait_ge(s_v0.h, 3)
        t.wait_ge(s_gp.h, 2)
        t.matmul(poA[:, 0, 0:4], onesf[:], lsi_f[:], start=True, stop=True).then_inc(s_t0.h, 1)
        s_t0.inc()
        t.matmul(poA[:, 1, 0:1], onesf[:], bidx_f[:], start=True, stop=True).then_inc(s_t0.h, 1)
        s_t0.inc()

    @block.scalar
    def _(sc):
        sc.wait_ge(s_t0.h, 2)
        sc.copy(lsi_rep[:], poA[:, 0, 0:4]).then_inc(s_sc0.h, 1)
        s_sc0.inc()
        sc.copy(bidx_rep[:], poA[:, 1, 0:1]).then_inc(s_sc0.h, 1)
        s_sc0.inc()

    @block.vector
    def _(v):
        def step(fn):
            v.wait_ge(s_v0.h, s_v0.v)
            fn().then_inc(s_v0.h, 1)
            s_v0.inc()
        v.wait_ge(s_sc0.h, 2)
        step(lambda: v.tensor_scalar(cge[:], lsi_rep[:], bidx_rep[:, 0:1], None, ALU.is_gt))
        step(lambda: v.tensor_scalar(ta_[:], cge[:, 0:1], -1.0, 1.0, ALU.mult, ALU.add))
        step(lambda: v.tensor_scalar(tb_[:], ta_[:], -2.0, None, ALU.add))
        step(lambda: v.tensor_tensor(tc_[:], tb_[:], cge[:, 1:2], ALU.mult))
        step(lambda: v.tensor_scalar(tc_[:], tc_[:], 2.0, None, ALU.add))
        step(lambda: v.tensor_scalar(td_[:], cge[:, 3:4], -1.0, 4.0, ALU.mult, ALU.add))
        step(lambda: v.tensor_tensor(ta_[:], tc_[:], td_[:], ALU.subtract))
        step(lambda: v.tensor_tensor(tb_[:], ta_[:], cge[:, 2:3], ALU.mult))
        step(lambda: v.tensor_tensor(tc_[:], tb_[:], td_[:], ALU.add))
        step(lambda: v.tensor_scalar(seg[:], tc_[:], -1.0, None, ALU.add))
        step(lambda: v.tensor_scalar(usev[:], seg[:], 0.0, None, ALU.is_gt))
        step(lambda: v.tensor_scalar(ta_[:], seg[:], 1.0, None, ALU.is_le))
        step(lambda: v.tensor_tensor(m0[:], ta_[:], usev[:], ALU.mult))
        step(lambda: v.tensor_scalar(ta_[:], seg[:], 2.0, None, ALU.is_equal))
        step(lambda: v.tensor_tensor(m1[:], ta_[:], usev[:], ALU.mult))
        step(lambda: v.tensor_scalar(ta_[:], seg[:], 3.0, None, ALU.is_ge))
        step(lambda: v.tensor_tensor(m2[:], ta_[:], usev[:], ALU.mult))
        bq4 = bq_s[:].rearrange("p (g j) -> p g j", j=4)
        step(lambda: v.tensor_tensor(bqt1[:], bq4[:, :, 0], bq4[:, :, 1], ALU.add))
        step(lambda: v.tensor_tensor(bqt2[:], bq4[:, :, 2], bq4[:, :, 3], ALU.add))
        step(lambda: v.tensor_tensor(bqsum[:], bqt1[:], bqt2[:], ALU.add))
        step(lambda: v.tensor_copy(bqb[:, 0, :], bqsum[:]))
        step(lambda: v.tensor_copy(bqb[:, 1, :], bk_s[:]))
        step(lambda: v.tensor_copy(bqb[:, 2, :], bv_s[:]))

    masks = (m0, m1, m2)

    @block.tensor
    def _(t):
        t.wait_ge(s_v0.h, 26)
        for j in range(3):
            t.transpose(poBb[0:8, j, :], bqb[:, j, :], ident[:]).then_inc(s_bt.h, 1)
            s_bt.inc()

    def emit_brow_evac():
        @block.scalar
        def _(sc):
            sc.wait_ge(s_bt.h, 3)
            sc.copy(brow_sb[:], poBb[0:8, 0:3, :]).then_inc(s_bt.h, 1)
            s_bt.inc()
            sc.wait_ge(s_bt.h, 4)
            sc.dma_start(out=brow_scr.transpose([1, 0, 2]), in_=brow_sb[:]
                         ).then_inc(s_brw.h, 16)
            s_brw.inc(16)

    pob = (poAb, poBb)

    def emit_combine(a):
        @block.vector
        def _(v, a=a):
            v.wait_ge(s_lab.h, 32 * (a + 1))
            if a == 0:
                v.wait_ge(s_v0.h, 23)
                v.tensor_scalar(aeffA[:], la_a[:], masks[a][0:16, 0:1], None,
                                ALU.mult).then_inc(s_lacc.h, 1)
            else:
                v.wait_ge(s_lacc.h, 2 * a)
                v.scalar_tensor_tensor(aeffA[:], la_a[:], masks[a][0:16, 0:1],
                                       aeffA[:], ALU.mult, ALU.add).then_inc(s_lacc.h, 1)
            s_lacc.inc()
            v.wait_ge(s_lacc.h, s_lacc.v)
            if a == 0:
                v.tensor_scalar(beffA[:], lb_a[:], masks[a][:, 0:1], None,
                                ALU.mult).then_inc(s_lacc.h, 1)
            else:
                v.scalar_tensor_tensor(beffA[:], lb_a[:], masks[a][:, 0:1],
                                       beffA[:], ALU.mult, ALU.add).then_inc(s_lacc.h, 1)
            s_lacc.inc()

    p0c = {}

    def emit_p0_transposes():
        @block.tensor
        def _(t):
            t.wait_ge(s_lacc.h, 6)
            t.wait_ge(s_bt.h, 4)
            for c in range(EC):
                grp, slot = c // 4, c % 4
                if grp == 0:
                    t.wait_ge(s_sc0.h, 2)
                if grp >= 2:
                    t.wait_ge(s_sc0.h, 2 + (grp - 1))
                t.transpose(pob[grp % 2][:, slot, 0:16], aeffA[:, 128 * c:128 * (c + 1)],
                            ident[0:16, 0:16]).then_inc(s_t0.h, 1)
                s_t0.inc()

        @block.scalar
        def _(sc):
            for grp in range(8):
                sc.wait_ge(s_t0.h, 2 + 4 * (grp + 1))
                sc.copy(aeffT[:, 4 * grp:4 * (grp + 1), :], pob[grp % 2][:, 0:4, 0:16]
                        ).then_inc(s_sc0.h, 1)
                s_sc0.inc()
        p0c["SC0_AEFF"] = s_sc0.v
        p0c["T0_AEFF"] = s_t0.v

        @block.tensor
        def _(t):
            for c in range(EC):
                grp, slot = c // 4, c % 4
                if slot == 0:
                    t.wait_ge(s_sc0.h, p0c["SC0_AEFF"] - 1 + grp)
                t.transpose(pob[grp % 2][0:16, slot, :], beffA[:, c, :], ident[:]
                            ).then_inc(s_t0.h, 1)
                s_t0.inc()
        p0c["T0_ALL"] = s_t0.v

        @block.scalar
        def _(sc):
            for grp in range(8):
                sc.wait_ge(s_t0.h, p0c["T0_AEFF"] + 4 * (grp + 1))
                sc.copy(beffT[0:16, grp, :], pob[grp % 2][0:16, 0:4, :]).then_inc(s_sc0.h, 1)
                s_sc0.inc()
        p0c["SC0_ALL"] = s_sc0.v

    def emit_xload(i):
        @block.scalar
        def _(sc, i=i):
            if i >= 3:
                sc.wait_ge(s_ap.h, i - 2)
            hi = E if i < 2 else E // 2
            sc.dma_start(out=xt[:, i % 3, 0:hi], in_=xsr[i][:, 0:hi]
                         ).then_inc(s_x[i % 3].h, 16)
            s_x[i % 3].inc(16)

        if i >= 2:
            @block.sync
            def _(sync, i=i):
                if i >= 3:
                    sync.wait_ge(s_ap.h, i - 2)
                sync.dma_start(out=xt[:, i % 3, E // 2:E], in_=xsr[i][:, E // 2:E]
                               ).then_inc(s_x[i % 3].h, 16)
                s_x[i % 3].inc(16)

    emit_xload(0)
    emit_xload(1)
    emit_brow_evac()

    def emit_apply(i):
        @block.vector
        def _(v, i=i):
            v.wait_ge(s_rs.h, 2 * (i + 1))
            if i >= 2:
                v.wait_ge(s_tr.h, 32 * (i - 1))
            v.tensor_scalar(xnt2[:, i % 2, :], xt[:, i % 3, :], negmu[:, i % 2, :],
                            rstd[:, i % 2, :], ALU.add, ALU.mult).then_inc(s_ap.h, 1)
            s_ap.inc()

        @block.tensor
        def _(t, i=i):
            t.wait_ge(s_ap.h, i + 1)
            if i == 0:
                t.wait_ge(s_v0.h, 1)
            for c in range(EC):
                grp = c // 4
                gidx = 8 * i + grp
                if gidx >= 2:
                    t.wait_ge(s_xe.h, gidx - 1)
                t.transpose(ptp[gidx % 2][:, c % 4, :],
                            xnt2[:, i % 2, 128 * c:128 * (c + 1)], ident[:]
                            ).then_inc(s_tr.h, 1)
                s_tr.inc()

        @block.scalar
        def _(sc, i=i):
            for grp in range(8):
                gidx = 8 * i + grp
                sc.wait_ge(s_tr.h, 4 * (gidx + 1))
                sc.copy(xnT[:, 4 * grp:4 * (grp + 1), 128 * i:128 * (i + 1)],
                        ptp[gidx % 2][:, :, :]).then_inc(s_xe.h, 1)
                s_xe.inc()

    for i in range(NT):
        if i >= 1:
            emit_apply(i - 1)

        @block.vector
        def _(v, i=i):
            xw = [16, 16, 32, 48, 48, 64, 80, 80][i]
            v.wait_ge(s_x[i % 3].h, xw)
            for c in range(8):
                v.bn_stats(bns[:, i % 2, c, :], xt[:, i % 3, 512 * c:512 * (c + 1)]
                           ).then_inc(s_ln.h, 1)
                s_ln.inc()
            v.wait_ge(s_ln.h, s_ln.v)
            v.wait_ge(s_rs.h, 2 * i)
            v.bn_aggr(mv[:, i % 2, :], bns[:, i % 2, :, :]).then_inc(s_ln.h, 1)
            s_ln.inc()
            v.wait_ge(s_ln.h, s_ln.v)
            v.tensor_scalar(negmu[:, i % 2, :], mv[:, i % 2, 0:1], -1.0, None,
                            ALU.mult).then_inc(s_ln.h, 1)
            s_ln.inc()
            v.wait_ge(s_ln.h, s_ln.v)
            v.tensor_scalar(lnv[:, i % 2, :], mv[:, i % 2, 1:2], 1e-5, None,
                            ALU.add).then_inc(s_ln.h, 1)
            s_ln.inc()

        @block.scalar
        def _(sc, i=i):
            sc.wait_ge(s_ln.h, 11 * i + 11)
            sc.activation(lnt[:, i % 2, :], lnv[:, i % 2, :], AF.Ln).then_inc(s_rs.h, 1)
            s_rs.inc()
            sc.wait_ge(s_rs.h, s_rs.v)
            sc.activation(rstd[:, i % 2, :], lnt[:, i % 2, :], AF.Exp, scale=-0.5
                          ).then_inc(s_rs.h, 1)
            s_rs.inc()

        if i + 2 < NT:
            emit_xload(i + 2)
        if i in (1, 2, 3):
            emit_combine(i - 1)
        if i == 4:
            emit_p0_transposes()

    emit_apply(7)

    @block.gpsimd
    def _(gp):
        gp.wait_ge(s_sc0.h, p0c["SC0_ALL"])
        gp.wait_ge(s_t0.h, p0c["T0_ALL"])
        gp.dma_start(out=beffT[16:17, :, :], in_=fb.rearrange("a (b c) -> a b c", c=512)
                     ).then_inc(s_fbc.h, 16)
        s_fbc.inc(16)

    @block.scalar
    def _(sc):
        sc.wait_ge(s_fbc.h, 16)
        sc.dma_start(out=beffT_scr[:], in_=beffT[:]).then_inc(s_bto.h, 16)
        s_bto.inc(16)
        sc.wait_ge(s_gp.h, 4)
        sc.dma_start(out=t1T[16:17, :], in_=ones_long[0:1, 0:1024]).then_inc(s_gpd.h, 16)
        s_gpd.inc(16)

    es0.close()
    es_xt.close()
    es_boot = contextlib.ExitStack()
    wTboot = es_boot.enter_context(nc.sbuf_tensor("wTboot", [128, 16, 512], BF16))
    es3 = contextlib.ExitStack()
    sb3 = lambda n, s, d: es3.enter_context(nc.sbuf_tensor(n, s, d))
    vT = sb3("vT", [128, 8, T], BF16)
    q_stk = sb3("q_stk", [9, 2, 16, 128], BF16)
    k_stk = sb3("k_stk", [9, 2, 16, 128], BF16)
    v_int = sb3("v_int", [128, 2, 128, 33], BF16)
    exp_sb = sb3("exp_sb", [128, 3, 8, 128], BF16)
    attn_sb = sb3("attn_sb", [128, 2, 16, 32], F32)
    rcp8 = sb3("rcp8", [128, 3, 8], F32)

    es2 = contextlib.ExitStack()
    sb2 = lambda n, s, d: es2.enter_context(nc.sbuf_tensor(n, s, d))
    kpl = sb2("kpl", [128, 2, 512], BF16)
    cbc = kpl[:, 0, 0:128]
    qtk4 = sb2("qtk4", [128, 2, 4, 128], BF16)
    wq_st = sb2("wq_st", [128, 8, 128], BF16)
    tjA = sb2("tjA", [128, 8, 64], BF16)
    wfb = sb2("wfb", [128, 1024], BF16)
    wprjC = sb2("wprjC", [128, EC, 128], BF16)

    @block.sync
    def _(sync):
        for cl in range(4):
            for hh in range(2):
                h = 2 * cl + hh
                if h == 0:
                    sync.wait_ge(s_ck.h, 4)
                    sync.wait_ge(s_cv.h, 4)
                if h >= 2:
                    sync.wait_ge(s_js.h, 3 * (h - 1))
                sync.dma_start(out=wqraw[:, h % 2, :]
                               .rearrange("p (d h4) -> p d h4", h4=32),
                               in_=wqmr[cl][:, 64 * hh:64 * (hh + 1), :]
                               ).then_inc(s_lq[h % 2].h, 16)
                s_lq[h % 2].inc(16)
            sync.wait_ge(s_js.h, 6 * (cl + 1))
            sync.dma_start(out=wqstg_v[:, :, cl, :].transpose([1, 0, 2]),
                           in_=wq_st[:, :, :]).then_inc(s_sq[cl % 2].h, 16)
            s_sq[cl % 2].inc(16)
        for pc in range(8):
            mc, hh = pc // 2, pc % 2
            if pc < 2:
                sync.wait_ge(s_js.h, 24)
            if pc >= 2:
                sync.wait_ge(s_cf.h, 2 * (pc - 1))
            sync.dma_start(out=wqraw[:, pc % 2, :],
                           in_=wfmr[mc][:, 2048 * hh:2048 * (hh + 1)]
                           ).then_inc(s_lf[pc % 2].h, 16)
            s_lf[pc % 2].inc(16)
            if pc >= 1:
                for hf in range(2):
                    w = 2 * (pc - 1) + hf
                    sync.wait_ge(s_cf.h, w + 1)
                    sync.dma_start(
                        out=wfstg[128 * ((pc - 1) // 2):128 * ((pc - 1) // 2 + 1),
                                  2048 * ((pc - 1) % 2) + 1024 * hf:
                                  2048 * ((pc - 1) % 2) + 1024 * (hf + 1)],
                        in_=wfb[:]).then_inc(s_sf[w % 2].h, 16)
                    s_sf[w % 2].inc(16)
        for hf in range(2):
            w = 14 + hf
            sync.wait_ge(s_cf.h, w + 1)
            sync.dma_start(out=wfstg[128 * 3:128 * 4,
                                     2048 + 1024 * hf:2048 + 1024 * (hf + 1)],
                           in_=wfb[:]).then_inc(s_sf[w % 2].h, 16)
            s_sf[w % 2].inc(16)

    @block.gpsimd
    def _(gp):
        for h in range(8):
            cl, hh = h // 2, h % 2
            gp.wait_ge(s_lq[h % 2].h, 16 * (h // 2 + 1))
            if cl >= 1 and hh == 0:
                gp.wait_ge(s_sq[(cl - 1) % 2].h, 16 * ((cl - 1) // 2 + 1))
            wqh = wqraw[:, h % 2, :].rearrange("p (d g j) -> p d g j", g=8, j=4)
            a0 = wqh[:, :, :, 0].transpose([0, 2, 1])
            a1 = wqh[:, :, :, 1].transpose([0, 2, 1])
            a2 = wqh[:, :, :, 2].transpose([0, 2, 1])
            a3 = wqh[:, :, :, 3].transpose([0, 2, 1])
            outv = wq_st[:, :, 64 * hh:64 * (hh + 1)]
            if h == 0:
                gp.wait_ge(s_bto.h, 16)
                gp.wait_ge(s_gpd.h, 16)
                gp.wait_ge(s_lacc.h, 6)
                gp.wait_ge(s_brw.h, 16)
                gp.wait_ge(s_ap.h, 8)
            if h >= 1:
                gp.wait_ge(s_js.h, s_js.v)
            gp.tensor_tensor(tjA[:], a0, a1, ALU.add).then_inc(s_js.h, 1)
            s_js.inc()
            gp.tensor_tensor(outv, a2, a3, ALU.add).then_inc(s_js.h, 1)
            s_js.inc()
            gp.wait_ge(s_js.h, s_js.v)
            gp.tensor_tensor(outv, outv, tjA[:], ALU.add).then_inc(s_js.h, 1)
            s_js.inc()
        gp.wait_ge(s_sq[0].h, 32)
        gp.wait_ge(s_sq[1].h, 32)
        gp.collective_compute("AllGather", ALU.bypass,
                              replica_groups=[list(range(8))],
                              ins=[wqstg], outs=[wq_scr]).then_inc(s_ccq.h, 1)
        s_ccq.inc()
        for ph in range(16):
            pc, hq = ph // 2, ph % 2
            if hq == 0:
                gp.wait_ge(s_lf[pc % 2].h, 16 * (pc // 2 + 1))
            if ph == 0:
                gp.wait_ge(s_bto.h, 16)
                gp.wait_ge(s_gpd.h, 16)
                gp.wait_ge(s_lacc.h, 6)
                gp.wait_ge(s_brw.h, 16)
                gp.wait_ge(s_ap.h, 8)
            if ph >= 1:
                gp.wait_ge(s_sf[(ph - 1) % 2].h, 16 * ((ph - 1) // 2 + 1))
            gp.tensor_copy(wfb[:], wqraw[:, pc % 2, 1024 * hq:1024 * (hq + 1)]
                           ).then_inc(s_cf.h, 1)
            s_cf.inc()
        gp.wait_ge(s_sf[0].h, 128)
        gp.wait_ge(s_sf[1].h, 128)
        gp.collective_compute("AllGather", ALU.bypass,
                              replica_groups=[list(range(8))],
                              ins=[wfstg], outs=[wfT_scr]).then_inc(s_ccf.h, 1)
        s_ccf.inc()


    @block.scalar
    def _(sc):
        sc.wait_ge(s_gpd.h, 16)
        sc.wait_ge(s_bto.h, 16)
        sc.wait_ge(s_lacc.h, 6)
        sc.wait_ge(s_brw.h, 16)
        sc.wait_ge(s_ap.h, 8)
        for half in range(2):
            for hh in range(2):
                sc.dma_start(out=k_stk[8:9, half, 8 * hh:8 * (hh + 1), :]
                             .rearrange("a c d -> a (c d)"),
                             in_=t1T[16:17, 0:1024]).then_inc(s_gpd.h, 16)
                s_gpd.inc(16)

    @block.vector
    def _(v):
        v.wait_ge(s_bto.h, 16)
        v.wait_ge(s_gpd.h, 16)
        v.wait_ge(s_lacc.h, 6)
        v.wait_ge(s_brw.h, 16)
        v.memset(v_int[:, :, :, 32:33], 1.0).then_inc(s_gp.h, 1)
        s_gp.inc()

    wk_scr5 = wk_scr.rearrange("(r g p) (cl d) -> r g p cl d", g=8, p=128, d=128)
    wv_scr5 = wv_scr.rearrange("(r g p) (cl d) -> r g p cl d", g=8, p=128, d=128)
    wq_scr5 = wq_scr.rearrange("(r g p) (cl d) -> r g p cl d", g=8, p=128, d=128)

    def plane_load(j):
        pl = j // 8
        g = j % 8
        s_cc = (s_cck, s_ccv, s_ccq)[pl]
        scr5 = (wk_scr5, wv_scr5, wq_scr5)[pl]

        @block.scalar
        def _(sc, j=j, g=g, s_cc=s_cc, scr5=scr5):
            sc.wait_ge(s_cc.h, 1)
            if j < 2:
                sc.wait_ge(s_tr.h, 224 if j == 0 else 256)
            elif j == 2:
                sc.wait_ge(s_ap.h, 8)
                sc.wait_ge(s_bto.h, 16)
                sc.wait_ge(s_gpd.h, 16)
                sc.wait_ge(s_lacc.h, 6)
                sc.wait_ge(s_brw.h, 16)
            else:
                sc.wait_ge(s_gpm.h, 2 * (j - 2))
            slot = (wprj[:, j % 3] if j % 3 < 2 else wprjC)
            sc.dma_start(out=slot.rearrange("p (r cl) d -> p r cl d", cl=4),
                         in_=scr5[:, g].transpose([1, 0, 2, 3])
                         ).then_inc(s_wl[j % 3].h, 16)
            s_wl[j % 3].inc(16)

    def plane_phase(kind):
        plane = {"k": 0, "v": 1, "q": 2}[kind]
        bidx_row = {"k": 1, "v": 2, "q": 0}[kind]
        s_tr_, s_te_, s_tw_, s_mx_ = {
            "k": (s_ktr, s_kte, s_ktw, s_kmx),
            "q": (s_qtr, s_qte, s_qtw, s_qmx),
            "v": (None, None, None, None),
        }[kind]
        tok_scr = {"k": ktok_scr, "q": qtok_scr, "v": None}[kind]
        mx = {"k": kmax, "q": qmax, "v": None}[kind]

        def emit_mm(m):
            g, th = m // 2, m % 2
            mg = 16 * plane + m
            j = 8 * plane + g
            bank = pmb[m % 2]

            @block.tensor
            def _(t, g=g, th=th, m=m, mg=mg, j=j, bank=bank):
                t.wait_ge(s_wl[j % 3].h, 16 * (j // 3 + 1))

                if kind == "k":
                    t.wait_ge(s_xe.h, 32 if th == 0 else 64)
                if mg >= 2:
                    t.wait_ge(s_gpe.h, mg - 1)
                if mg == 0:
                    t.wait_ge(s_bias.h, 16)
                    t.wait_ge(s_gp.h, 4)
                wsl = (wprj[:, j % 3] if j % 3 < 2 else wprjC)
                for c in range(EC):
                    t.matmul(bank[:], wsl[:, c, :],
                             xnT[:, c, 512 * th:512 * (th + 1)],
                             start=(c == 0), stop=False)
                t.matmul(bank[:], bias_rows[0:1, bidx_row, g, :],
                         ones512[:], start=False, stop=True).then_inc(s_gpm.h, 1)
                s_gpm.inc()

        def emit_evac(m):
            g, th = m // 2, m % 2
            mg = 16 * plane + m
            bank = pmb[m % 2]

            @block.vector
            def _(v, g=g, th=th, m=m, mg=mg, bank=bank):
                v.wait_ge(s_gpm.h, mg + 1)
                if mg == 0:
                    v.wait_ge(s_bto.h, 16)
                    v.wait_ge(s_gpd.h, 16)
                    v.wait_ge(s_lacc.h, 6)
                    v.wait_ge(s_brw.h, 16)
                if kind == "v":
                    v.tensor_copy(vT[:, g, 512 * th:512 * (th + 1)], bank[:]
                                  ).then_inc(s_gpe.h, 1)
                    s_gpe.inc()
                else:
                    if m >= 2:
                        v.wait_ge(s_tr_.h, 4 * (m - 1))
                    v.tensor_copy(kpl[:, m % 2, :], bank[:]).then_inc(s_gpe.h, 1)
                    s_gpe.inc()

        def emit_tr(m):
            g, th = m // 2, m % 2
            mg = 16 * plane + m

            @block.tensor
            def _(t, m=m, mg=mg):
                t.wait_ge(s_gpe.h, mg + 1)
                for j in range(4):
                    if m >= 4 and j == 0:
                        t.wait_ge(s_te_.h, m - 3)
                    t.transpose(ptp4[m % 4][:, j, :],
                                kpl[:, m % 2, 128 * j:128 * (j + 1)], ident[:]
                                ).then_inc(s_tr_.h, 1)
                    s_tr_.inc()

            @block.vector
            def _(v, g=g, th=th, m=m):
                v.wait_ge(s_tr_.h, 4 * (m + 1))
                if m == 0 and kind == "q":
                    v.wait_ge(s_ktw[0].h, 128)
                    v.wait_ge(s_ktw[1].h, 128)
                    v.wait_ge(s_kmx.h, 64)
                if m >= 2:
                    v.wait_ge(s_tw_[m % 2].h, 16 * (m // 2))
                    v.wait_ge(s_mx_.h, 4 * (m - 1))
                v.tensor_copy(qtk4[:, m % 2, :, :], ptp4[m % 4][:, :, :]
                              ).then_inc(s_te_.h, 1)
                s_te_.inc()

            @block.scalar
            def _(sc, g=g, th=th, m=m):
                sc.wait_ge(s_te_.h, m + 1)
                sc.dma_start(
                    out=tok_scr[4 * th:4 * th + 4, :, g, :].transpose([1, 0, 2]),
                    in_=qtk4[:, m % 2, :, :]).then_inc(s_tw_[m % 2].h, 16)
                s_tw_[m % 2].inc(16)

            @block.vector
            def _(v, g=g, th=th, m=m):
                v.wait_ge(s_te_.h, m + 1)
                for j in range(4):
                    v.reduce_max(mx[:, 4 * th + j, g:g + 1],
                                 qtk4[:, m % 2, j, :], axis=AX.X,
                                 apply_absolute_value=True).then_inc(s_mx_.h, 1)
                    s_mx_.inc()

        for m in range(16):
            emit_mm(m)
            emit_evac(m)
            if kind != "v" and m >= 1:
                emit_tr(m - 1)
            if m % 2 == 1:
                nj = 8 * plane + m // 2 + 2
                if nj < 24:
                    plane_load(nj)
        if kind != "v":
            emit_tr(15)

    plane_load(0)
    plane_load(1)
    plane_phase("k")
    plane_phase("v")

    plane_phase("q")

    for i in range(NT):
        @block.vector
        def _(v, i=i):
            if i == 0:
                v.wait_ge(s_qmx.h, 64)
                v.wait_ge(s_kmx.h, 64)
            v.wait_ge(s_cb.h, s_cb.v)
            v.tensor_tensor(cbprod[:], qmax[:, i, :], kmax[:, i, :], ALU.mult
                            ).then_inc(s_cb.h, 1)
            s_cb.inc()
            v.wait_ge(s_cb.h, s_cb.v)
            v.reduce_sum(cb8[:, i:i + 1], cbprod[:], axis=AX.X, negate=True
                         ).then_inc(s_cb.h, 1)
            s_cb.inc()
            v.wait_ge(s_cb.h, s_cb.v)
            if i >= 1:
                v.wait_ge(s_cbw[(i - 1) % 2].h, 16 * ((i - 1) // 2 + 1))
            v.tensor_copy(cbc[:], cb8[:, i:i + 1].broadcast_to([128, 128])
                          ).then_inc(s_cb.h, 1)
            s_cb.inc()

        @block.scalar
        def _(sc, i=i):
            sc.wait_ge(s_cb.h, 3 * (i + 1))
            sc.dma_start(out=qtok_scr[i, :, 8, :], in_=cbc[:]).then_inc(s_cbw[i % 2].h, 16)
            s_cbw[i % 2].inc(16)

    for th in range(2):
        @block.tensor
        def _(t, th=th):
            bank = pmb[th % 2]
            t.wait_ge(s_gpe.h, 48)
            t.wait_ge(s_sc0.h, p0c["SC0_ALL"])
            for c in range(EC):
                mm = t.matmul(bank[0:16, :], aeffT[:, c, :],
                              xnT[:, c, 512 * th:512 * (th + 1)],
                              start=(c == 0), stop=(c == EC - 1))
                if c == EC - 1:
                    mm.then_inc(s_t1.h, 1)
            s_t1.inc()

        @block.scalar
        def _(sc, th=th):
            sc.wait_ge(s_t1.h, th + 1)
            sc.copy(t1T[0:16, 512 * th:512 * (th + 1)], pmb[th % 2][0:16, :]
                    ).then_inc(s_t1e.h, 1)
            s_t1e.inc()

    def emit_stack(sg):
        tt = sg // 8
        r0 = 16 * (sg % 8)

        @block.sync
        def _(sync, sg=sg, tt=tt, r0=r0):
            if sg == 0:
                sync.wait_ge(s_qtw[0].h, 128)
                sync.wait_ge(s_qtw[1].h, 128)
                sync.wait_ge(s_ktw[0].h, 128)
                sync.wait_ge(s_ktw[1].h, 128)
                sync.wait_ge(s_cbw[0].h, s_cbw[0].v)
                sync.wait_ge(s_cbw[1].h, s_cbw[1].v)
                sync.wait_ge(s_gpd.h, 80)
            if sg >= 2:
                sync.wait_ge(s_sm.h, 16 * (sg - 1))
            sync.dma_start(out=q_stk[:, sg % 2, :, :],
                           in_=qtok_scr[tt, r0:r0 + 16, :, :].transpose([1, 0, 2])
                           ).then_inc(s_stk[sg % 2].h, 16)
            s_stk[sg % 2].inc(16)
            sync.dma_start(out=k_stk[0:8, sg % 2, :, :],
                           in_=ktok_scr[tt, r0:r0 + 16, :, :].transpose([1, 0, 2])
                           ).then_inc(s_stk[sg % 2].h, 16)
            s_stk[sg % 2].inc(16)

    def emit_merge(sg):
        tt = sg // 8
        r0 = 16 * (sg % 8)

        @block.gpsimd
        def _(gp, sg=sg, tt=tt, r0=r0):
            gp.wait_ge(s_nm.h, 2 * (sg + 1))
            for hh in range(2):
                gp.dma_start(
                    out=attn_scr[tt, r0 + 8 * hh:r0 + 8 * (hh + 1), :, :]
                    .transpose([1, 0, 2]),
                    in_=attn_sb[:, sg % 2, 8 * hh:8 * (hh + 1), :]
                    ).then_inc(s_mg[sg % 2].h, 16)
                s_mg[sg % 2].inc(16)

    @block.scalar
    def _(sc):
        sc.wait_ge(s_ccf.h, 1)
        sc.wait_ge(s_cf.h, 16)
        for s in (s_sk[0], s_sk[1], s_sv[0], s_sv[1]):
            sc.wait_ge(s.h, 32)
        sc.wait_ge(s_ap.h, 8)
        sc.wait_ge(s_bto.h, 16)
        sc.wait_ge(s_lacc.h, 6)
        sc.wait_ge(s_gpd.h, 16)
        sc.wait_ge(s_brw.h, 16)
        sc.dma_start(out=wTboot[:],
                     in_=wfT_scr.rearrange("(c p) o -> p c o", p=128)[:, 0:16, 0:512]
                     ).then_inc(s_brw.h, 16)
        s_brw.inc(16)

    for tt in range(NT):
        @block.scalar
        def _(sc, tt=tt):
            for lg in range(16):
                B = 16 * tt + lg
                sc.wait_ge(s_sm.h, 8 * (B + 1))
                if B >= 3:
                    sc.wait_ge(s_om.h, 8 * (B - 2))
                sc.activation(exp_sb[:, B % 3, :, :], rotS8[B % 2], AF.Exp, scale=ISD
                              ).then_inc(s_ex.h, 1)
                s_ex.inc()

    emit_stack(0)
    emit_stack(1)
    def emit_vint(tt):
        @block.vector
        def _(v, tt=tt):
            if tt == 0:
                v.wait_ge(s_gpe.h, 32)
                v.wait_ge(s_gp.h, 5)
            if tt >= 2:
                v.wait_ge(s_om.h, 128 * (tt - 1))
            for g in range(8):
                v.tensor_copy(
                    v_int[:, tt % 2, :, 4 * g:4 * (g + 1)],
                    vT[:, g, 128 * tt:128 * (tt + 1)].unsqueeze(2)
                    .broadcast_to([128, 128, 4])).then_inc(s_vi.h, 1)
                s_vi.inc()

    emit_vint(0)
    for sg in range(64):
        tt = sg // 8
        if sg % 8 == 4 and tt + 1 < NT:
            emit_vint(tt + 1)

        for bh in range(2):
            B = 2 * sg + bh

            @block.tensor
            def _(t, sg=sg, bh=bh, B=B):
                if bh == 0:
                    t.wait_ge(s_stk[sg % 2].h, 32 * (sg // 2 + 1))
                if B == 0:
                    t.wait_ge(s_qte.h, 16)
                if B == 1:
                    t.wait_ge(s_t1e.h, 2)
                if B >= 2:
                    t.wait_ge(s_ex.h, B - 1)
                for i in range(8):
                    mm = t.matmul(rotS8[B % 2][:, i, :], k_stk[:, sg % 2, 8 * bh + i, :],
                                  q_stk[:, sg % 2, 8 * bh + i, :], start=True, stop=True)
                    if i == 7:
                        mm.then_inc(s_sm.h, 8)
                        s_sm.inc(8)

            Bm = B - 1
            if Bm >= 0:
                @block.tensor
                def _(t, Bm=Bm):
                    ttm = Bm // 16
                    t.wait_ge(s_ex.h, Bm + 1)
                    if Bm % 16 == 0:
                        t.wait_ge(s_vi.h, 8 * (ttm + 1))
                    if Bm == 0:
                        t.wait_ge(s_sc0.h, p0c["SC0_ALL"])
                    if Bm >= 2:
                        t.wait_ge(s_nm.h, Bm - 1)
                    tloc = 16 * ((Bm // 2) % 8) + 8 * (Bm % 2)
                    for i in range(8):
                        mm = t.matmul(rotO8[Bm % 2][:, i, 0:33], exp_sb[:, Bm % 3, i, :],
                                      v_int[:, ttm % 2, tloc + i, :],
                                      start=True, stop=True)
                        if i == 7:
                            mm.then_inc(s_om.h, 8)
                            s_om.inc(8)

        @block.vector
        def _(v, sg=sg):
            for bh in range(2):
                B = 2 * sg + bh
                v.wait_ge(s_om.h, 8 * (B + 1))
                v.reciprocal(rcp8[:, B % 3, :], rotO8[B % 2][:, :, 32]).then_inc(s_rc.h, 1)
                s_rc.inc()
                v.wait_ge(s_rc.h, s_rc.v)
                if sg >= 2 and bh == 0:
                    v.wait_ge(s_mg[sg % 2].h, 32 * (sg // 2))
                v.tensor_tensor(attn_sb[:, sg % 2, 8 * bh:8 * (bh + 1), :],
                                rotO8[B % 2][:, :, 0:32],
                                rcp8[:, B % 3, :].unsqueeze(2).broadcast_to([128, 8, 32]),
                                ALU.mult).then_inc(s_nm.h, 1)
                s_nm.inc()

        if sg + 2 < 64:
            emit_stack(sg + 2)
        emit_merge(sg)

    @block.tensor
    def _(t):
        Bm = 127
        t.wait_ge(s_ex.h, 128)
        t.wait_ge(s_vi.h, 64)
        t.wait_ge(s_nm.h, Bm - 1)
        tloc = 16 * ((Bm // 2) % 8) + 8 * (Bm % 2)
        for i in range(8):
            mm = t.matmul(rotO8[Bm % 2][:, i, 0:33], exp_sb[:, Bm % 3, i, :],
                          v_int[:, (Bm // 16) % 2, tloc + i, :],
                          start=True, stop=True)
            if i == 7:
                mm.then_inc(s_om.h, 8)
                s_om.inc(8)

    es2.close()
    es3.close()
    es4 = contextlib.ExitStack()
    sb4 = lambda n, s, d: es4.enter_context(nc.sbuf_tensor(n, s, d))
    wT = sb4("wT", [128, 4, 16, 512], BF16)
    atile = sb4("atile", [128, 2, 512], F32)
    xtile = sb4("xtile", [128, 2, 512], F32)
    beffT2 = sb4("beffT2", [17, 2, 512], BF16)

    wfT_scr_v = wfT_scr.rearrange("(c p) o -> p c o", p=128)
    for h in range(1, 16):
        @block.scalar
        def _(sc, h=h):
            if h == 1:
                sc.wait_ge(s_om.h, 1024)
                sc.wait_ge(s_nm.h, 128)
                sc.wait_ge(s_vi.h, 64)
                sc.wait_ge(s_mg[0].h, s_mg[0].v)
                sc.wait_ge(s_mg[1].h, s_mg[1].v)
                sc.wait_ge(s_ccf.h, 1)
            if h >= 5:
                sc.wait_ge(s_fm.h, 8 * ((h - 4) // 2 + 1))
            oc, hh = h // 2, h % 2
            sc.dma_start(out=wT[:, (h - 1) % 4, :, :],
                         in_=wfT_scr_v[:, 16 * hh:16 * (hh + 1),
                                       512 * oc:512 * (oc + 1)]
                         ).then_inc(s_wt4[(h - 1) % 4].h, 16)
            s_wt4[(h - 1) % 4].inc(16)

    for oc in range(8):
        @block.sync
        def _(sync, oc=oc):
            if oc == 0:
                sync.wait_ge(s_ex.h, 128)
                sync.wait_ge(s_om.h, 1024)
                sync.wait_ge(s_nm.h, 128)
                sync.wait_ge(s_vi.h, 64)
                sync.wait_ge(s_mg[0].h, s_mg[0].v)
                sync.wait_ge(s_mg[1].h, s_mg[1].v)
                sync.wait_ge(s_ccf.h, 1)
            if oc >= 2:
                sync.wait_ge(s_fm.h, 8 * (oc - 1))
            sync.dma_start(out=beffT2[:, oc % 2, :], in_=beffT_scr[:, oc, :]
                           ).then_inc(s_bti.h, 16)
            s_bti.inc(16)

        for tt in range(NT):
            i = oc * NT + tt
            bnk = pmb[i % 2]

            @block.sync
            def _(sync, oc=oc, tt=tt, i=i):
                if i == 0:
                    sync.wait_ge(s_mg[0].h, s_mg[0].v)
                    sync.wait_ge(s_mg[1].h, s_mg[1].v)
                if i >= 2:
                    sync.wait_ge(s_out[i % 2].h, 16 * (i // 2))
                sync.dma_start(out=atile[:, i % 2, :],
                               in_=attn_scr[tt, :, 16 * oc:16 * (oc + 1), :]
                               .rearrange("p a b -> p (a b)")).then_inc(s_ar[i % 2].h, 16)
                s_ar[i % 2].inc(16)
                if i >= 2:
                    sync.wait_ge(s_a2.h, i - 1)
                sync.dma_start(out=xtile[:, i % 2, :],
                               in_=xsr[tt][:, 512 * oc:512 * (oc + 1)]).then_inc(s_xr[i % 2].h, 16)
                s_xr[i % 2].inc(16)

            @block.tensor
            def _(t, oc=oc, tt=tt, i=i, bnk=bnk):
                if tt == 0:
                    for h in (2 * oc, 2 * oc + 1):
                        if h == 0:
                            t.wait_ge(s_brw.h, 32)
                        else:
                            t.wait_ge(s_wt4[(h - 1) % 4].h, 16 * ((h - 1) // 4 + 1))
                if i == 0:
                    t.wait_ge(s_ex.h, 128)
                    t.wait_ge(s_nm.h, 128)
                if i >= 2:
                    t.wait_ge(s_a1.h, i - 1)
                for c in range(EC):
                    h_used = 2 * oc + c // 16
                    rhs = (wTboot[:, c, :] if h_used == 0
                           else wT[:, (h_used - 1) % 4, c % 16, :])
                    t.matmul(bnk[:], xnT[:, c, 128 * tt:128 * (tt + 1)],
                             rhs, start=(c == 0), stop=False)
                if tt == 0:
                    t.wait_ge(s_bti.h, 16 * (oc + 1))
                t.matmul(bnk[:], t1T[:, 128 * tt:128 * (tt + 1)], beffT2[:, oc % 2, :],
                         start=False, stop=True).then_inc(s_fm.h, 1)
                s_fm.inc()

            @block.vector
            def _(v, i=i, bnk=bnk):
                v.wait_ge(s_fm.h, i + 1)
                v.wait_ge(s_ar[i % 2].h, 16 * (i // 2 + 1))
                if i >= 2:
                    v.wait_ge(s_a2.h, i - 1)
                v.tensor_tensor(atile[:, i % 2, :], bnk[:], atile[:, i % 2, :],
                                ALU.add).then_inc(s_a1.h, 1)
                s_a1.inc()

            @block.gpsimd
            def _(gp, oc=oc, tt=tt, i=i):
                gp.wait_ge(s_a1.h, i + 1)
                gp.wait_ge(s_xr[i % 2].h, 16 * (i // 2 + 1))
                gp.tensor_tensor(atile[:, i % 2, :], atile[:, i % 2, :],
                                 xtile[:, i % 2, :], ALU.add).then_inc(s_a2.h, 1)
                s_a2.inc()
                gp.wait_ge(s_a2.h, i + 1)
                gp.dma_start(out=outr[tt][:, 512 * oc:512 * (oc + 1)],
                             in_=atile[:, i % 2, :]).then_inc(s_out[i % 2].h, 16)
                s_out[i % 2].inc(16)

    @block.scalar
    def _(sc):
        for s in (s_x + s_ktw + s_qtw + s_cbw + s_wl + s_wt4):
            sc.wait_ge(s.h, s.v)

    @block.sync
    def _(sync):
        for s in ([s_small, s_lab, s_bti, s_cck, s_ccv, s_ccq, s_ccf]
                  + s_lk + s_lv + s_lq + s_lf + s_sk + s_sv + s_sq + s_sf
                  + s_stk + s_mg + s_ar + s_xr + s_out):
            sync.wait_ge(s.h, s.v)

    @block.gpsimd
    def _(gp):
        gp.wait_ge(s_gpd.h, s_gpd.v)
        gp.wait_ge(s_fbc.h, s_fbc.v)
        gp.wait_ge(s_bias.h, s_bias.v)

    es4.close()
    es_boot.close()
    es2a.close()
    es.close()
    return nc


def kernel(**inputs):
    x = np.asarray(inputs["L0_attn_x"], dtype=np.float32).reshape(2 * 4096, E)
    shards = [np.ascontiguousarray(x[i * T:(i + 1) * T]) for i in range(NCORES)]
    wqf = np.asarray(inputs["L0_attn_Wq"], dtype=np.float32).reshape(E, 128, 32)
    wkf = np.asarray(inputs["L0_attn_Wk"], dtype=np.float32).reshape(E, 128, 8)
    wvf = np.asarray(inputs["L0_attn_Wv"], dtype=np.float32).reshape(E, 128, 8)
    wfT = np.asarray(inputs["L0_ffn_weight"], dtype=np.float32).reshape(E, E).T
    base = {
        "bq": np.ascontiguousarray(np.asarray(inputs["L0_attn_bq"], dtype=np.float32)),
        "bk": np.ascontiguousarray(np.asarray(inputs["L0_attn_bk"], dtype=np.float32)),
        "bv": np.ascontiguousarray(np.asarray(inputs["L0_attn_bv"], dtype=np.float32)),
        "fb": np.ascontiguousarray(np.asarray(inputs["L0_ffn_bias"], dtype=np.float32)).reshape(1, E),
        "la": np.ascontiguousarray(np.asarray(inputs["L0_ffn_lora_A"], dtype=np.float32)),
        "lbw": np.ascontiguousarray(np.asarray(inputs["L0_ffn_lora_B"], dtype=np.float32)),
        "lsi": np.ascontiguousarray(np.asarray(inputs["L0_ffn_lora_start_indices"], dtype=np.int32)).reshape(1, 4),
    }
    in_maps = []
    for core in range(NCORES):
        m = dict(base)
        m["xs"] = shards[core]
        m["wkm"] = np.ascontiguousarray(wkf[512 * core:512 * (core + 1)])
        m["wvm"] = np.ascontiguousarray(wvf[512 * core:512 * (core + 1)])
        m["wqm"] = np.ascontiguousarray(wqf[512 * core:512 * (core + 1)])
        m["wfm"] = np.ascontiguousarray(wfT[512 * core:512 * (core + 1)])
        m["bidx"] = np.array([[core // 4]], dtype=np.int32)
        in_maps.append(m)

    nc = build_program()
    res = run_bass_kernel_spmd(nc, in_maps, core_ids=list(range(NCORES)))
    outs = [np.asarray(r["out"]) for r in res.results]
    return np.concatenate(outs, axis=0).reshape(2, 4096, E).astype(np.float32)
